# version 04 (final)
# baseline (speedup 1.0000x reference)
import numpy as np
import ml_dtypes
import concourse.bass as bass
import concourse.mybir as mybir
from concourse.bass_utils import run_bass_kernel_spmd

F32 = mybir.dt.float32
BF16 = mybir.dt.bfloat16
I32 = mybir.dt.int32
AF = mybir.ActivationFunctionType
ALU = mybir.AluOpType
AX = mybir.AxisListType

B, S, D = 8, 2048, 1024
NCORE = 8
NCHUNK = D // 128
NTTILE = S // 128
EPS_LN = 1e-5

N_STAGES = 12

ENGS = ("sync", "gpsimd", "vector", "scalar", "tensor")


class Op:
    __slots__ = ("eng", "kind", "fn", "deps", "inc", "val", "known", "waits", "tag")

    def __init__(self, eng, kind, fn, deps, tag):
        self.eng = eng
        self.kind = kind
        self.fn = fn
        self.deps = deps
        self.inc = False
        self.val = 0
        self.known = None
        self.waits = None
        self.tag = tag


class Slot:

    def __init__(self, name=""):
        self.name = name
        self.writers = []
        self.readers = []

    def wdeps(self):
        return self.writers + self.readers

    def rdeps(self):
        return list(self.writers)

    def wrote(self, op):
        self.writers = [op]
        self.readers = []

    def wrote_many(self, ops):
        self.writers = list(ops)
        self.readers = []

    def read(self, op):
        self.readers.append(op)


class Prog:
    def __init__(self, nc):
        self.nc = nc
        self.order = []
        self.streams = {e: [] for e in ENGS}
        self.sems = {}
        for e in ENGS:
            for k in ("c", "d"):
                self.sems[(e, k)] = nc.alloc_semaphore(f"sem_{e}_{k}")
        self.shield = nc.alloc_sbuf_tensor("shield", [1, 2], F32)

    def op(self, eng, fn, deps=(), tag="", kind="c"):
        dd = []
        for d in deps:
            if d is None:
                continue
            if isinstance(d, (list, tuple)):
                dd.extend(x for x in d if x is not None)
            else:
                dd.append(d)
        o = Op(eng, kind, fn, dd, tag)
        self.order.append(o)
        self.streams[eng].append(o)
        return o

    def dma(self, eng, fn, deps=(), tag=""):
        return self.op(eng, fn, deps, tag, kind="d")

    def tap(self, nc, name, ap, deps, shape=None, dtype=None):
        shape = shape or ap.shape
        h = nc.dram_tensor("tap_" + name, list(shape), dtype or ap.dtype,
                           kind="ExternalOutput").ap()
        o = self.dma("sync", lambda e: e.dma_start(out=h[:], in_=ap),
                     deps=deps, tag="tap_" + name)
        self.taps = getattr(self, "taps", [])
        self.taps.append(o)
        return o

    def finalize(self):
        for o in self.order:
            if o.kind == "d":
                o.inc = True
            for d in o.deps:
                if d.eng == o.eng and d.kind == "c" and o.kind == "c":
                    continue
                d.inc = True
        for e in ENGS:
            cc = 0
            cd = 0
            for o in self.streams[e]:
                if o.inc:
                    if o.kind == "d":
                        cd += 16
                        o.val = cd
                    else:
                        cc += 1
                        o.val = cc
        known = {e: {} for e in ENGS}
        for o in self.order:
            k = known[o.eng]
            waits = {}
            for d in o.deps:
                same_cc = d.eng == o.eng and d.kind == "c" and o.kind == "c"
                if not same_cc:
                    s = (d.eng, d.kind)
                    if d.val > k.get(s, 0):
                        if d.val > waits.get(s, 0):
                            waits[s] = d.val
                        k[s] = d.val
                if d.known:
                    for s2, v2 in d.known.items():
                        if v2 > k.get(s2, 0):
                            k[s2] = v2
            o.waits = list(waits.items())
            o.known = dict(k)
            if o.inc:
                o.known[(o.eng, o.kind)] = o.val

    def emit(self):
        self.finalize()
        blk = self.nc.Block()
        with blk:
            for e in ENGS:
                stream = self.streams[e]
                if not stream:
                    continue

                def body(eng, e=e, stream=stream):
                    prev1 = None
                    prev2 = None
                    for o in stream:
                        for (se, sk), v in o.waits:
                            eng.wait_ge(self.sems[(se, sk)], v)
                        if e != "sync" and any(
                                (d is prev1 or d is prev2) for d in o.deps):
                            eng.drain()
                        inst = o.fn(eng)
                        prev2 = prev1
                        prev1 = o
                        if o.inc:
                            if inst is None:
                                eng.drain()
                                eng.nop().then_inc(self.sems[(o.eng, "c")], 1)
                            else:
                                inst.then_inc(
                                    self.sems[(o.eng, o.kind)],
                                    16 if o.kind == "d" else 1)

                getattr(blk, e)(body)


class SPack:
    def __init__(self, dtype):
        self.dtype = dtype
        self.cols = 0
        self.reg = {}
        self.arrs = []

    def add(self, name, arr):
        a = np.asarray(arr, dtype=np.float32)
        assert a.ndim == 2 and a.shape[0] <= 128, (name, a.shape)
        self.reg[name] = (self.cols, a.shape[0], a.shape[1])
        self.cols += a.shape[1]
        self.arrs.append(a)

    def host(self):
        out = np.zeros((128, self.cols), dtype=np.float32)
        c = 0
        for a in self.arrs:
            out[: a.shape[0], c : c + a.shape[1]] = a
            c += a.shape[1]
        return out

    def ap(self, sbuf, name, p=None, w=None, coff=0):
        off, pp, ww = self.reg[name]
        return sbuf[0 : (p or pp), off + coff : off + coff + (w or ww)]


def _f32(x):
    return np.ascontiguousarray(x, dtype=np.float32)


def _build_packs(inputs):
    spf = SPack(np.float32)
    spb = SPack(np.float32)

    spf.add("ident", np.eye(128, dtype=np.float32))
    spb.add("identb", np.eye(128, dtype=np.float32))
    spf.add("ones_col", np.ones((128, 1), np.float32))
    spb.add("ones_colb", np.ones((128, 1), np.float32))
    spf.add("ones_row", np.ones((1, 128), np.float32))
    spf.add("id8", np.eye(8, dtype=np.float32))
    spf.add("c_epsln", np.full((128, 1), 1e-5, np.float32))
    spf.add("c_eps6", np.full((128, 1), 1e-6, np.float32))

    wrt = _f32(np.asarray(inputs["L0_ffn_Wr"]).T).reshape(8, 128, 8)
    spb.add("l0_wrt", wrt.transpose(1, 0, 2).reshape(128, 64))
    b1 = _f32(inputs["L0_ffn_b1"]).reshape(8, 32, 128)
    spf.add("l0_b1", b1.transpose(2, 0, 1).reshape(128, 256))
    b2 = _f32(inputs["L0_ffn_b2"]).reshape(8, 8, 128)
    spf.add("l0_b2", b2.transpose(2, 0, 1).reshape(128, 64))

    for nm in ("Wq", "Wk", "Wv"):
        w = _f32(np.asarray(inputs[f"L0_attn_{nm}"]).T).reshape(8, 128, 64)
        spb.add(f"l0a_{nm.lower()}t", w.transpose(1, 0, 2).reshape(128, 512))
    spb.add("l0a_wot", _f32(np.asarray(inputs["L0_attn_Wo"]).T))
    spf.add("l0a_bq", _f32(np.asarray(inputs["L0_attn_bq"]).reshape(64, 1)))
    spf.add("l0a_bk", _f32(np.asarray(inputs["L0_attn_bk"]).reshape(64, 1)))
    spf.add("l0a_bv", _f32(np.asarray(inputs["L0_attn_bv"]).reshape(64, 1)))
    spf.add("l0a_bo", _f32(np.asarray(inputs["L0_attn_bo"]).reshape(8, 128).T))
    spf.add("l0a_dec", _f32(np.asarray(inputs["L0_attn_decay"]).reshape(64, 1)))
    spf.add("l0a_decr", _f32(np.asarray(inputs["L0_attn_decay"]).reshape(1, 64)))
    spf.add("triu", np.triu(np.ones((128, 128), np.float32)))
    b1 = _f32(inputs["L1_ffn_b1"]).reshape(32, 128)
    spf.add("l1_b1", b1.T)
    spf.add("l1_b2", _f32(inputs["L1_ffn_b2"]).reshape(8, 128).T)
    for nm in ("bq", "bk", "bv", "bo"):
        spf.add(f"l1a_{nm}", _f32(inputs[f"L1_attn_{nm}"]).reshape(8, 128).T)
    spf.add("c_inv8", np.full((128, 1), 0.125, np.float32))
    spf.add("id16", np.eye(16, dtype=np.float32))
    gw = _f32(np.asarray(inputs["L2_ffn_gate_w"]).T).reshape(8, 128, 16)
    spb.add("l2f_gwt", gw.transpose(1, 0, 2).reshape(128, 128))

    lnw = _f32(inputs["L2_attn_ln_w"]).reshape(4, 8, 128)
    spf.add("l2a_lnw", lnw.transpose(2, 0, 1).reshape(128, 32))
    cw = _f32(inputs["L2_attn_conv_w"]).reshape(4, 16, 128, 4)
    spf.add("l2a_cw", cw.transpose(2, 0, 1, 3).reshape(128, 256))
    cb = _f32(inputs["L2_attn_conv_b"]).reshape(4, 16, 128)
    spf.add("l2a_cb", cb.transpose(2, 0, 1).reshape(128, 64))
    sk = _f32(inputs["L2_attn_skip"]).reshape(4, 16, 128)
    spf.add("l2a_skip", sk.transpose(2, 0, 1).reshape(128, 64))
    onw = _f32(inputs["L2_attn_onorm_w"]).reshape(4, 16, 128)
    spf.add("l2a_onw", onw.transpose(2, 0, 1).reshape(128, 64))
    gb = np.concatenate([_f32(inputs["L2_attn_ig_b"]),
                         _f32(inputs["L2_attn_fg_b"])], axis=1)
    spf.add("l2a_gb", gb.T)
    spf.add("l2a_postw", _f32(inputs["L2_attn_post_w"]).reshape(8, 128).T)
    spf.add("l2a_latb", _f32(inputs["L2_attn_lat_b"]).reshape(8, 128).T)
    sel4 = np.zeros((4, 4 * 128), np.float32)
    for h in range(4):
        sel4[h, h * 128 : (h + 1) * 128] = 1.0
    spf.add("sel4", sel4)
    spf.add("id4", np.eye(4, dtype=np.float32))
    spf.add("c_neg1", np.full((128, 1), -1.0, np.float32))
    spf.add("c_one", np.ones((128, 1), np.float32))
    spf.add("c_isq512", np.full((128, 1), 1.0 / np.sqrt(512.0), np.float32))
    spf.add("l3_ws", _f32(inputs["L3_ffn_weight_scale"]).reshape(1, 1))
    spf.add("l3_bias", _f32(inputs["L3_ffn_bias"]).reshape(8, 128).T)
    gg = np.concatenate([_f32(inputs["L2_attn_ig_w"]),
                         _f32(inputs["L2_attn_fg_w"])], axis=1)
    ggt = gg.transpose(0, 2, 1).reshape(4, 48, 128, 8)
    spb.add("l2a_gw", ggt.transpose(2, 0, 1, 3).reshape(128, 4 * 384))
    spf.add("iota_col", np.arange(128, dtype=np.float32).reshape(128, 1))
    return spf, spb


class Env:
    pass


def _build(n_stages):
    _L0N[0] = 0
    _L1N[0] = 0
    _L2N[0] = 0
    _L3N[0] = 0
    nc = bass.Bass("TRN2", target_bir_lowering=False)
    P = Prog(nc)
    env = Env()
    env.nc = nc
    env.P = P

    env.x_h = nc.dram_tensor("x", [D, S], F32, kind="ExternalInput").ap()
    env.out_h = nc.dram_tensor("out", [D, S], F32, kind="ExternalOutput").ap()
    env.spf_h = nc.dram_tensor("spf", list(_SPF_SHAPE), F32, kind="ExternalInput").ap()
    env.spb_h = nc.dram_tensor("spb", list(_SPB_SHAPE), F32, kind="ExternalInput").ap()
    env.l0w1_h = nc.dram_tensor("l0w1t", [8, 32, 128, 8 * 128], F32,
                                kind="ExternalInput").ap()
    env.l0w2_h = nc.dram_tensor("l0w2t", [8, 8, 128, 32 * 128], F32,
                                kind="ExternalInput").ap()
    env.l3pk_h = nc.dram_tensor("l3pk", [256, 1024], I32,
                                kind="ExternalInput").ap()
    env.l3w_h = nc.dram_tensor("l3w", [48, 128, 512], F32,
                               kind="ExternalInput").ap()
    env.l3wb_h = nc.dram_tensor("l3wb", [1024, 1024], BF16, kind="Internal").ap()
    env.l2up_h = nc.dram_tensor("l2up", [4, 32, 128, 1024], F32,
                                kind="ExternalInput").ap()
    env.l2dn_h = nc.dram_tensor("l2dn", [4, 8, 128, 2048], F32,
                                kind="ExternalInput").ap()
    env.l2lat_h = nc.dram_tensor("l2lat", [8, 128, 1024], F32,
                                 kind="ExternalInput").ap()
    env.l2bq_h = nc.dram_tensor("l2bq", [4, 16, 128, 128], F32,
                                kind="ExternalInput").ap()
    env.l2bk_h = nc.dram_tensor("l2bk", [4, 16, 128, 128], F32,
                                kind="ExternalInput").ap()
    env.l2bv_h = nc.dram_tensor("l2bv", [4, 16, 128, 128], F32,
                                kind="ExternalInput").ap()
    env.xcd_h = nc.dram_tensor("xcd", [16, 128, 2048], BF16, kind="Internal").ap()
    env.szd_h = nc.dram_tensor("szd", [16, 128, 2048], BF16, kind="Internal").ap()
    env.htd_h = nc.dram_tensor("htd", [16, 128, 2048], BF16, kind="Internal").ap()
    env.qd_h = nc.dram_tensor("qd", [16, 128, 2048], BF16, kind="Internal").ap()
    env.kd_h = nc.dram_tensor("kd", [16, 128, 2048], BF16, kind="Internal").ap()
    env.vtd_h = nc.dram_tensor("vtd", [4, 128, 8192], BF16, kind="Internal").ap()
    env.l2fw1_h = nc.dram_tensor("l2fw1", [16, 4, 128, 1024], F32,
                                 kind="ExternalInput").ap()
    env.l2fw2_h = nc.dram_tensor("l2fw2", [16, 8, 128, 512], F32,
                                 kind="ExternalInput").ap()
    env.l2sf1_h = nc.dram_tensor("l2sf1", [16, 128, 1024], F32,
                                 kind="ExternalInput").ap()
    env.l2sf2_h = nc.dram_tensor("l2sf2", [16, 128, 1024], F32,
                                 kind="ExternalInput").ap()
    env.l2sf3_h = nc.dram_tensor("l2sf3", [8, 128, 2048], F32,
                                 kind="ExternalInput").ap()
    for nm in ("l1wq", "l1wk", "l1wv", "l1wo"):
        setattr(env, nm + "_h", nc.dram_tensor(nm, [8, 128, 1024], F32,
                                               kind="ExternalInput").ap())
    env.l1w1_h = nc.dram_tensor("l1w1t", [32, 128, 1024], F32,
                                kind="ExternalInput").ap()
    env.l1w2_h = nc.dram_tensor("l1w2t", [8, 128, 4096], F32,
                                kind="ExternalInput").ap()
    env.l1w1b_h = nc.dram_tensor("l1w1b", [32, 128, 1024], BF16,
                                 kind="Internal").ap()
    env.l1w2b_h = nc.dram_tensor("l1w2b", [8, 128, 4096], BF16,
                                 kind="Internal").ap()
    env.xdump_h = nc.dram_tensor("xdump", [128, NCHUNK * S], F32, kind="Internal").ap()
    env.hdram_h = nc.dram_tensor("hdram", [4, 128, 32 * 512], BF16,
                                 kind="Internal").ap()

    env.X = nc.alloc_sbuf_tensor("X", [128, NCHUNK * S], F32)
    env.sX = Slot("X")
    env.XLN = nc.alloc_sbuf_tensor("XLN", [128, NCHUNK * S], BF16)
    env.sXLN = Slot("XLN")
    env.C = nc.alloc_sbuf_tensor("C", [128, 8192], BF16)
    env.sC = [Slot("C0"), Slot("C1")]
    env.Dg = nc.alloc_sbuf_tensor("Dg", [128, 16384], BF16)
    env.sD = [Slot("D0"), Slot("D1")]
    env.E = nc.alloc_sbuf_tensor("E", [128, 6144], BF16)
    env.sE = [Slot("E0"), Slot("E1"), Slot("E2"), Slot("E3")]
    env.MUR = nc.alloc_sbuf_tensor("MUR", [1, 2048], F32)
    env.sMUR = Slot("MUR")
    env.SC1 = nc.alloc_sbuf_tensor("SC1", [128, 2048], F32)
    env.sSC1 = Slot("SC1")
    env.SC2 = nc.alloc_sbuf_tensor("SC2", [128, 2048], F32)
    env.sSC2 = Slot("SC2")
    env.WTM = nc.alloc_sbuf_tensor("WTM", [16, S], F32)
    env.sWTM = Slot("WTM")
    env.SPF = nc.alloc_sbuf_tensor("SPF", list(_SPF_SHAPE), F32)
    env.sSPF = Slot("SPF")
    env.SPB = nc.alloc_sbuf_tensor("SPB", list(_SPB_SHAPE), BF16)
    env.sSPB = Slot("SPB")

    env.PS = nc.alloc_psum_tensor("PS", [128, 4096], F32)
    env.bank = lambda i: env.PS[:, 512 * i : 512 * (i + 1)]
    env.sB = [Slot(f"bank{i}") for i in range(8)]

    ld_x = P.dma("sync", lambda e: e.dma_start(
        out=env.X[:].rearrange("p (c s) -> p c s", c=NCHUNK),
        in_=env.x_h.rearrange("(c p) s -> p c s", p=128)), tag="ld_x")
    env.sX.wrote(ld_x)
    ld_spf = P.dma("sync", lambda e: e.dma_start(out=env.SPF[:], in_=env.spf_h[:]),
                   tag="ld_spf")
    env.sSPF.wrote(ld_spf)
    ld_spb = P.dma("gpsimd", lambda e: e.dma_start(out=env.SPB[:], in_=env.spb_h[:]),
                   tag="ld_spb")
    env.sSPB.wrote(ld_spb)

    env.spf = lambda name, p=None, w=None, coff=0: _SPF.ap(env.SPF, name, p, w, coff)
    env.spb = lambda name, p=None, w=None, coff=0: _SPB.ap(env.SPB, name, p, w, coff)

    env.BROWT = nc.alloc_sbuf_tensor("BROWT", [128, 128], F32)
    env.BROW = {}
    env.sBROW = Slot("BROW")

    def mk_brow(name, slot_col):
        bk = env.bank(6)
        t1 = P.op("tensor", lambda e: e.transpose(
            bk[0:1, 0:64], env.spf(name), env.spf("ident", p=64, w=64)),
            deps=[env.sSPF.writers] + env.sB[6].wdeps(), tag=f"brow_t_{name}")
        env.sB[6].wrote(t1)
        c1 = P.op("vector", lambda e: e.tensor_copy(
            env.SC1[0:1, 1024 + slot_col : 1088 + slot_col], bk[0:1, 0:64]),
            deps=[t1] + env.sSC1.wdeps(), tag=f"brow_c_{name}")
        env.sB[6].read(c1)
        env.sSC1.wrote(c1)
        t2 = P.op("tensor", lambda e: e.matmul(
            bk[:, 0:64], env.spf("ones_row"),
            env.SC1[0:1, 1024 + slot_col : 1088 + slot_col],
            start=True, stop=True), deps=[c1] + env.sB[6].wdeps(),
            tag=f"brow_b_{name}")
        env.sB[6].wrote(t2)
        env.sSC1.read(t2)
        c2 = P.op("vector", lambda e: e.tensor_copy(
            env.BROWT[:, slot_col : slot_col + 64], bk[:, 0:64]),
            deps=[t2] + env.sBROW.wdeps(), tag=f"brow_e_{name}")
        env.sB[6].read(c2)
        env.BROW[name] = env.BROWT[:, slot_col : slot_col + 64]
        env.sBROW.writers.append(c2)
    env.mk_brow = mk_brow

    stages = [
        lambda: emit_l0_ffn(env), lambda: emit_l0_attn(env), lambda: emit_l0_ffn(env),
        lambda: emit_l1_ffn(env), lambda: emit_l1_attn(env), lambda: emit_l1_ffn(env),
        lambda: emit_l2_ffn(env), lambda: emit_l2_attn(env), lambda: emit_l2_ffn(env),
        lambda: emit_l3_ffn(env), lambda: emit_l3_attn(env), lambda: emit_l3_ffn(env),
    ]
    env.stage_guard = []
    for i in range(min(n_stages, 12)):
        stages[i]()
        env.stage_guard = [P.streams[e][-1] for e in ENGS if P.streams[e]]

    st = P.dma("sync", lambda e: e.dma_start(
        out=env.out_h.rearrange("(c p) s -> p c s", p=128),
        in_=env.X[:].rearrange("p (c s) -> p c s", c=NCHUNK)),
        deps=env.sX.rdeps(), tag="st_out")
    env.sX.read(st)
    P.op("sync", lambda e: e.nop(), deps=[st], tag="fin")

    P.emit()
    return nc


def emit_ln(env):
    P = env.P
    src, out = env.X, env.XLN
    MU = env.MUR[0:1, 0:2048]
    RS = env.SC1[0:1, 0:2048]
    TMP = env.SC2[0:1, 0:2048]
    rv_guard = env.sMUR.wdeps() + env.sSC1.wdeps() + \
        getattr(env, "stage_guard", [])
    out_guard = env.sXLN.wdeps()
    copies = []

    for j in range(4):
        cs = slice(512 * j, 512 * (j + 1))
        bk6, bk7 = env.bank(6), env.bank(7)

        def sum_grp(e, j=j, bk6=bk6):
            r = None
            for c in range(NCHUNK):
                r = e.matmul(bk6[0:1, :], env.spf("ones_col"),
                             src[:, c * S + 512 * j : c * S + 512 * j + 512],
                             start=(c == 0), stop=(c == NCHUNK - 1))
            return r
        o1 = P.op("tensor", sum_grp, deps=env.sX.rdeps() + [env.sSPF.writers] +
                  env.sB[6].wdeps(), tag=f"ln_sum{j}")
        env.sB[6].wrote(o1)
        env.sX.read(o1)
        c1 = P.op("vector", lambda e, cs=cs, bk6=bk6: e.tensor_copy(
            MU[0:1, cs], bk6[0:1, :]), deps=[o1] + rv_guard, tag=f"ln_scp{j}")
        env.sB[6].read(c1)
        env.sMUR.read(c1)
        copies.append(c1)

        sq_mm = []
        for c in range(NCHUNK):
            scol = 512 * (c % 4)
            sq = P.op("scalar", lambda e, c=c, j=j, scol=scol: e.activation(
                env.SC2[:, scol : scol + 512],
                src[:, c * S + 512 * j : c * S + 512 * j + 512], AF.Square),
                deps=env.sX.rdeps() + env.sSC2.wdeps(), tag=f"ln_sq{j}_{c}")
            env.sSC2.wrote(sq)
            env.sX.read(sq)
            mm = P.op("tensor", lambda e, c=c, scol=scol, bk7=bk7: e.matmul(
                bk7[0:1, :], env.spf("ones_col"), env.SC2[:, scol : scol + 512],
                start=(c == 0), stop=(c == NCHUNK - 1)),
                deps=[sq] + (env.sB[7].wdeps() if c == 0 else []),
                tag=f"ln_sqmm{j}_{c}")
            env.sSC2.read(mm)
            sq_mm.append(mm)
        env.sB[7].wrote(sq_mm[-1])
        c2 = P.op("vector", lambda e, cs=cs, bk7=bk7: e.tensor_copy(
            RS[0:1, cs], bk7[0:1, :]), deps=[sq_mm[-1]] + rv_guard +
            env.sSC1.wdeps(), tag=f"ln_qcp{j}")
        env.sB[7].read(c2)
        env.sSC1.read(c2)
        copies.append(c2)

    rm1 = P.op("vector", lambda e: e.tensor_scalar(MU, MU, 1.0 / D, None, ALU.mult),
               deps=copies, tag="ln_mu")
    rm2 = P.op("vector", lambda e: e.tensor_scalar(RS, RS, 1.0 / D, None, ALU.mult),
               deps=[rm1], tag="ln_msq")
    rm3 = P.op("vector", lambda e: e.tensor_tensor(TMP, MU, MU, ALU.mult),
               deps=[rm1] + env.sSC2.wdeps(), tag="ln_mu2")
    env.sSC2.wrote(rm3)
    rm4 = P.op("vector", lambda e: e.tensor_tensor(RS, RS, TMP, ALU.subtract),
               deps=[rm2, rm3], tag="ln_var")
    rm5 = P.op("scalar", lambda e: e.activation(TMP, RS, AF.Sqrt,
               bias=env.spf("c_epsln", p=1)), deps=[rm4], tag="ln_sd")
    rm6 = P.op("vector", lambda e: e.reciprocal(RS, TMP), deps=[rm5], tag="ln_rs")
    env.sMUR.wrote_many([rm1])
    env.sSC1.wrote_many([rm6])
    env.sSC2.read(rm5)

    outs = []
    for j in range(4):
        cs = slice(512 * j, 512 * (j + 1))
        bk6, bk7 = env.bank(6), env.bank(7)
        m1 = P.op("tensor", lambda e, cs=cs, bk6=bk6: e.matmul(
            bk6, env.spf("ones_row"), MU[0:1, cs], start=True, stop=True),
            deps=[rm6, rm1] + env.sB[6].wdeps(), tag=f"ln_bmu{j}")
        env.sB[6].wrote(m1)
        env.sMUR.read(m1)
        m2 = P.op("tensor", lambda e, cs=cs, bk7=bk7: e.matmul(
            bk7, env.spf("ones_row"), RS[0:1, cs], start=True, stop=True),
            deps=[rm6] + env.sB[7].wdeps(), tag=f"ln_brs{j}")
        env.sB[7].wrote(m2)
        env.sSC1.read(m2)
        for c in range(NCHUNK):
            sl = slice(c * S + 512 * j, c * S + 512 * j + 512)
            v1 = P.op("vector", lambda e, sl=sl, bk6=bk6: e.tensor_tensor(
                env.SC2[:, 0:512], src[:, sl], bk6, ALU.subtract),
                deps=[m1] + env.sX.rdeps() + env.sSC2.wdeps(), tag=f"ln_sub{j}_{c}")
            env.sSC2.wrote(v1)
            env.sX.read(v1)
            env.sB[6].read(v1)
            v2 = P.op("vector", lambda e, sl=sl, bk7=bk7: e.tensor_tensor(
                out[:, sl], env.SC2[:, 0:512], bk7, ALU.mult),
                deps=[m2, v1] + out_guard, tag=f"ln_mul{j}_{c}")
            env.sSC2.read(v2)
            env.sB[7].read(v2)
            outs.append(v2)
    env.sXLN.wrote_many(outs)
    return outs


_L0N = [0]


def emit_l0_ffn(env):
    P = env.P
    emit_ln(env)

    wtm_ops = []
    wtm_guard = env.sWTM.wdeps()
    for j in range(NTTILE):
        bk = env.bank(5)

        def rgrp(e, j=j, bk=bk):
            r = None
            for c in range(NCHUNK):
                r = e.matmul(bk[0:128, 0:8],
                             env.XLN[:, c * S + 128 * j : c * S + 128 * j + 128],
                             env.spb("l0_wrt")[:, c * 8 : c * 8 + 8],
                             start=(c == 0), stop=(c == NCHUNK - 1))
            return r
        mm = P.op("tensor", rgrp, deps=env.sXLN.rdeps() + [env.sSPB.writers] +
                  env.sB[5].wdeps(), tag=f"l0r_mm{j}")
        env.sB[5].wrote(mm)
        env.sXLN.read(mm)
        T8 = env.SC1[:, 0:8]
        LG = env.SC1[:, 32:40]
        M1 = env.SC1[:, 8:9]
        SUM = env.SC1[:, 9:10]
        PRB = env.SC1[:, 16:24]
        MSK = env.SC1[:, 24:32]
        g = env.sSC1.wdeps()
        v0 = P.op("vector", lambda e, bk=bk: e.tensor_copy(LG, bk[0:128, 0:8]),
                  deps=[mm] + g, tag=f"l0r_lgc{j}")
        env.sB[5].read(v0)
        v1 = P.op("vector", lambda e: e.reduce_max(
            M1, LG, axis=AX.X, negate=True), deps=[v0], tag=f"l0r_max{j}")
        s1 = P.op("scalar", lambda e: e.activation(
            T8, LG, AF.Exp, bias=M1),
            deps=[v0, v1] + g, tag=f"l0r_exp{j}")
        v2a = P.op("vector", lambda e: e.reduce_sum(SUM, T8, axis=AX.X),
                   deps=[s1], tag=f"l0r_sum{j}")
        if DEBUG_TAPS and j == 0 and _L0N[0] == 0:
            sa = P.op("vector", lambda e: e.tensor_copy(env.SC2[:, 120:128], T8),
                      deps=[v2a] + env.sSC2.wdeps(), tag="snap_t8")
            env.sSC2.wrote(sa)
            sb_ = P.op("vector", lambda e: e.tensor_copy(env.SC2[:, 128:129], SUM),
                       deps=[v2a], tag="snap_sum")
            env.sSC2.read(sb_)
            ta = P.tap(env.nc, "t8v", env.SC2[:, 120:129], [sa, sb_])
            env.sSC2.read(ta)
        v2 = P.op("vector", lambda e: e.reciprocal(M1, SUM), deps=[v2a], tag=f"l0r_rcp{j}")
        v3 = P.op("vector", lambda e: e.tensor_scalar(PRB, T8, M1, None, ALU.mult),
                  deps=[v2, s1], tag=f"l0r_prb{j}")
        v4 = P.op("vector", lambda e: e.reduce_max(M1, PRB, axis=AX.X), deps=[v3],
                  tag=f"l0r_m1{j}")
        v5 = P.op("vector", lambda e: e.tensor_scalar(MSK, PRB, M1, None, ALU.is_lt),
                  deps=[v4], tag=f"l0r_lt{j}")
        v6 = P.op("vector", lambda e: e.tensor_tensor(MSK, PRB, MSK, ALU.mult),
                  deps=[v5], tag=f"l0r_pm{j}")
        v7 = P.op("vector", lambda e: e.reduce_max(M1, MSK, axis=AX.X), deps=[v6],
                  tag=f"l0r_m2{j}")
        v8 = P.op("vector", lambda e: e.tensor_scalar(MSK, PRB, M1, None, ALU.is_ge),
                  deps=[v7], tag=f"l0r_ge{j}")
        v9 = P.op("vector", lambda e: e.tensor_tensor(T8, PRB, MSK, ALU.mult),
                  deps=[v8], tag=f"l0r_w{j}")
        env.sSC1.wrote(v9)
        if DEBUG_TAPS and j == 0 and _L0N[0] == 0:
            sn1 = P.op("vector", lambda e: e.tensor_copy(env.SC2[:, 100:101],
                       env.SC1[:, 8:9]), deps=[v1] + env.sSC2.wdeps(), tag="snap_m1")
            env.sSC2.wrote(sn1)
            sn2 = P.op("scalar", lambda e: e.tensor_copy(env.SC2[:, 104:112],
                       env.SC1[:, 0:8]) if False else e.copy(env.SC2[:, 104:112],
                       env.SC1[:, 0:8]), deps=[s1], tag="snap_e8")
            env.sSC2.read(sn2)
            sn3 = P.op("vector", lambda e: e.tensor_copy(env.SC2[:, 112:120],
                       env.SC1[:, 32:40]), deps=[v0], tag="snap_lg")
            env.sSC2.read(sn3)
            tm1 = P.tap(env.nc, "m1a", env.SC2[:, 100:101], [sn1])
            env.sSC2.read(tm1)
            te8 = P.tap(env.nc, "e8", env.SC2[:, 104:112], [sn2])
            env.sSC2.read(te8)
            tlg2 = P.tap(env.nc, "lg2", env.SC2[:, 112:120], [sn3])
            env.sSC2.read(tlg2)
            cpl = P.op("vector", lambda e, bk=bk: e.tensor_copy(
                env.SC2[:, 0:8], bk[0:128, 0:8]), deps=[mm] + env.sSC2.wdeps(),
                tag="dbg_lgcp")
            env.sB[5].read(cpl)
            env.sSC2.wrote(cpl)
            tpl = P.tap(env.nc, "lg0", env.SC2[:, 0:8], [cpl])
            env.sSC2.read(tpl)
            tpp = P.tap(env.nc, "prb0", env.SC1[:, 16:24], [v3, v9])
            env.sSC1.read(tpp)
            tpw = P.tap(env.nc, "w0", env.SC1[:, 0:8], [v9])
            env.sSC1.read(tpw)
        bkt = env.bank(4)
        t1 = P.op("tensor", lambda e, bkt=bkt: e.transpose(
            bkt[0:8, 0:128], T8, env.spf("ident")),
            deps=[v9] + env.sB[4].wdeps() + [env.sSPF.writers], tag=f"l0r_tr{j}")
        env.sB[4].wrote(t1)
        env.sSC1.read(t1)
        c1 = P.op("vector", lambda e, j=j, bkt=bkt: e.tensor_copy(
            env.WTM[0:8, 128 * j : 128 * (j + 1)], bkt[0:8, 0:128]),
            deps=[t1] + wtm_guard, tag=f"l0r_cp{j}")
        env.sB[4].read(c1)
        wtm_ops.append(c1)
    env.sWTM.wrote_many(wtm_ops)

    if DEBUG_TAPS:
        t_ = P.tap(env.nc, f"xln{_L0N[0]}", env.XLN[:], env.sXLN.rdeps())
        env.sXLN.read(t_)
        t_ = P.tap(env.nc, f"wtm{_L0N[0]}", env.WTM[:], env.sWTM.rdeps())
        env.sWTM.read(t_)
    dx = P.dma("sync", lambda e: e.dma_start(out=env.xdump_h[:], in_=env.X[:]),
               deps=env.sX.rdeps(), tag="l0_dumpx")
    env.sX.read(dx)
    YACC = env.X

    for e_i in range(8):
        h_store = []
        for hc in range(32):
            slot = hc % 2
            W1S = env.E[:, slot * 1024 : slot * 1024 + 1024]
            sW1 = env.sE[slot]
            ldw = P.dma("gpsimd", lambda e, e_i=e_i, hc=hc, W1S=W1S: e.dma_start(
                out=W1S, in_=env.l0w1_h[e_i, hc]),
                deps=sW1.wdeps() + [env.sSPB.writers], tag=f"l0w1ld_{e_i}_{hc}")
            sW1.wrote(ldw)
            hsl = 2 + (hc % 2)
            HC = env.E[:, 2048 + (hc % 2) * 2048 : 2048 + (hc % 2) * 2048 + 2048]
            sHC = env.sE[hsl]
            hc_guard = sHC.wdeps()
            silus = []
            for t in range(4):
                bk = env.bank(t % 2)

                def w1grp(e, W1S=W1S, t=t, bk=bk):
                    r = None
                    for c in range(NCHUNK):
                        r = e.matmul(bk, W1S[:, 128 * c : 128 * (c + 1)],
                                     env.XLN[:, c * S + 512 * t : c * S + 512 * t + 512],
                                     start=(c == 0), stop=(c == NCHUNK - 1))
                    return r
                mm = P.op("tensor", w1grp, deps=[ldw] + env.sXLN.rdeps() +
                          env.sB[t % 2].wdeps(), tag=f"l0w1mm_{e_i}_{hc}_{t}")
                env.sB[t % 2].wrote(mm)
                env.sXLN.read(mm)
                sW1.read(mm)
                b1ap = env.spf("l0_b1", w=1, coff=e_i * 32 + hc)
                sv = P.op("scalar", lambda e, t=t, bk=bk, HC=HC, b1ap=b1ap:
                          e.activation(HC[:, 512 * t : 512 * (t + 1)], bk, AF.Silu,
                                       bias=b1ap),
                          deps=[mm] + hc_guard, tag=f"l0w1si_{e_i}_{hc}_{t}")
                env.sB[t % 2].read(sv)
                silus.append(sv)
            sHC.wrote_many(silus)
            std = P.dma("sync", lambda e, hc=hc, HC=HC: e.dma_start(
                out=env.hdram_h[:, :, 512 * hc : 512 * (hc + 1)].transpose([1, 0, 2]),
                in_=HC[:].rearrange("p (t w) -> p t w", t=4)),
                deps=silus, tag=f"l0hst_{e_i}_{hc}")
            sHC.read(std)
            h_store.append(std)

        for t in range(4):
            hdeps = []
            for hh in range(2):
                hs = env.sD[hh]
                HT = env.Dg[:, hh * 8192 : hh * 8192 + 8192]
                ldh = P.dma("sync", lambda e, t=t, hh=hh, HT=HT: e.dma_start(
                    out=HT, in_=env.hdram_h[t, :, 8192 * hh : 8192 * (hh + 1)]),
                    deps=hs.wdeps() + h_store, tag=f"l0htld_{e_i}_{t}_{hh}")
                hs.wrote(ldh)
                hdeps.append(ldh)
            bkw = env.bank(4)
            WROW = env.SC1[0:1, 1536:2048]
            b1_ = P.op("tensor", lambda e, e_i=e_i, t=t, bkw=bkw: e.matmul(
                bkw[0:1, :], env.spf("id8", p=8, w=1, coff=e_i),
                env.WTM[0:8, 512 * t : 512 * (t + 1)], start=True, stop=True),
                deps=env.sWTM.rdeps() + env.sB[4].wdeps(), tag=f"l0wr_{e_i}_{t}")
            env.sB[4].wrote(b1_)
            env.sWTM.read(b1_)
            c1_ = P.op("vector", lambda e, bkw=bkw: e.tensor_copy(
                WROW, bkw[0:1, :]), deps=[b1_] + env.sSC1.wdeps(),
                tag=f"l0wrc_{e_i}_{t}")
            env.sB[4].read(c1_)
            env.sSC1.wrote(c1_)
            bw = P.op("tensor", lambda e, bkw=bkw: e.matmul(
                bkw, env.spf("ones_row"), WROW, start=True, stop=True),
                deps=[c1_] + env.sB[4].wdeps(), tag=f"l0wb_{e_i}_{t}")
            env.sB[4].wrote(bw)
            env.sSC1.read(bw)
            cw = P.op("vector", lambda e, bkw=bkw: e.tensor_copy(
                env.SC2[:, 0:512], bkw), deps=[bw] + env.sSC2.wdeps(),
                tag=f"l0wbc_{e_i}_{t}")
            env.sB[4].read(cw)
            env.sSC2.wrote(cw)
            for och in range(8):
                qs = och % 2
                W2S = env.C[:, qs * 4096 : qs * 4096 + 4096]
                sW2 = env.sC[qs]
                ldw2 = P.dma("gpsimd", lambda e, e_i=e_i, och=och, W2S=W2S:
                             e.dma_start(out=W2S, in_=env.l0w2_h[e_i, och]),
                             deps=sW2.wdeps(), tag=f"l0w2ld_{e_i}_{t}_{och}")
                sW2.wrote(ldw2)
                gc = och
                bk = env.bank(2 + och % 2)

                def w2grp(e, W2S=W2S, bk=bk):
                    r = None
                    for hcc in range(32):
                        r = e.matmul(
                            bk, W2S[:, 128 * hcc : 128 * (hcc + 1)],
                            env.Dg[:, (hcc // 16) * 8192 + (hcc % 16) * 512 :
                                   (hcc // 16) * 8192 + (hcc % 16) * 512 + 512],
                            start=(hcc == 0), stop=(hcc == 31))
                    return r
                mm = P.op("tensor", w2grp, deps=[ldw2] + hdeps +
                          env.sB[2 + och % 2].wdeps(), tag=f"l0w2mm_{e_i}_{t}_{och}")
                env.sB[2 + och % 2].wrote(mm)
                sW2.read(mm)
                env.sD[0].read(mm)
                env.sD[1].read(mm)
                b2ap = env.spf("l0_b2", w=1, coff=e_i * 8 + gc)
                sv = P.op("scalar", lambda e, bk=bk, b2ap=b2ap: e.activation(
                    env.SC1[:, 512:1024], bk, AF.Identity, bias=b2ap),
                    deps=[mm] + env.sSC1.wdeps(), tag=f"l0yb2_{e_i}_{t}_{och}")
                env.sB[2 + och % 2].read(sv)
                env.sSC1.wrote(sv)
                ysl = slice(gc * S + 512 * t, gc * S + 512 * t + 512)
                if e_i == 0:
                    yop = P.op("vector", lambda e, ysl=ysl: e.tensor_tensor(
                        YACC[:, ysl], env.SC1[:, 512:1024], env.SC2[:, 0:512],
                        ALU.mult), deps=[sv, cw, dx] + env.sX.wdeps(),
                        tag=f"l0y_{e_i}_{t}_{och}")
                else:
                    def yfn(e, ysl=ysl):
                        e.tensor_tensor(env.SC1[:, 1024:1536], env.SC1[:, 512:1024],
                                        env.SC2[:, 0:512], ALU.mult)
                        return e.tensor_tensor(YACC[:, ysl], YACC[:, ysl],
                                               env.SC1[:, 1024:1536], ALU.add)
                    yop = P.op("vector", yfn, deps=[sv, cw] + env.sX.wdeps(),
                               tag=f"l0y_{e_i}_{t}_{och}")
                env.sSC1.read(yop)
                env.sSC2.read(yop)
                env.sX.wrote(yop)

    _L0N[0] += 1

    for c in range(NCHUNK):
        ld = P.dma("sync", lambda e, c=c: e.dma_start(
            out=env.SC2[:, 0:2048], in_=env.xdump_h[:, c * S : (c + 1) * S]),
            deps=env.sSC2.wdeps() + [dx], tag=f"l0xrl_{c}")
        env.sSC2.wrote(ld)
        ad = P.op("vector", lambda e, c=c: e.scalar_tensor_tensor(
            env.X[:, c * S : (c + 1) * S], env.X[:, c * S : (c + 1) * S], 0.5,
            env.SC2[:, 0:2048], ALU.mult, ALU.add),
            deps=[ld] + env.sX.wdeps(), tag=f"l0xadd_{c}")
        env.sSC2.read(ad)
        env.sX.wrote(ad)


def emit_l0_attn(env):
    P = env.P
    nc = env.nc
    emit_ln(env)

    QF = env.C[:, 0:2048]
    KF = env.C[:, 2048:4096]
    SCN = env.C[:, 4096:6144]
    VT = env.Dg[:, 0:1024]
    KT = env.Dg[:, 1024:2048]
    AP_ = env.Dg[:, 2048 + 0:2048 + 256]
    VTL = env.Dg[:, 2560:2560 + 256]
    sQF, sKF, sSCN = env.sC[0], env.sC[1], Slot("SCN")
    sVT, sKT, sAP, sVTL = env.sD[0], env.sD[1], Slot("AP"), Slot("VTL")
    if not hasattr(env, "TST"):
        env.TST = nc.alloc_sbuf_tensor("TST", [64, 64], F32)
        env.DTB = nc.alloc_sbuf_tensor("DTB", [128, 256], F32)
        env.D64 = nc.alloc_sbuf_tensor("D64", [64, 192], F32)
        env.sTST = Slot("TST")
        env.sDTB = Slot("DTB")
        env.sD64 = Slot("D64")
    TST = env.TST

    if "l0a_bk" not in env.BROW:
        env.mk_brow("l0a_bk", 0)
        env.mk_brow("l0a_bv", 64)

    DposT = env.DTB[:, 0:64]
    DnegT = env.DTB[:, 64:128]
    DrevT = env.DTB[:, 128:192]
    LNDB = env.DTB[:, 192:256]
    DCOL = env.D64[:, 0:64]
    DROW = env.D64[:, 64:128]
    R64 = env.SC1[0:1, 0:64]
    g_dtb = env.sDTB.wdeps() + env.sD64.wdeps() + env.sSC1.wdeps()
    d1 = P.op("scalar", lambda e: e.activation(R64, env.spf("l0a_decr"),
              AF.Sigmoid), deps=[env.sSPF.writers] + g_dtb, tag="l0a_sig")
    d2 = P.op("scalar", lambda e: e.activation(R64, R64, AF.Ln), deps=[d1],
              tag="l0a_ln")
    bkd = env.bank(6)
    d3 = P.op("tensor", lambda e: e.matmul(bkd[:, 0:64], env.spf("ones_row"),
              R64, start=True, stop=True), deps=[d2] + env.sB[6].wdeps(),
              tag="l0a_blnd")
    env.sB[6].wrote(d3)
    env.sSC1.read(d3)
    d4 = P.op("vector", lambda e: e.tensor_copy(LNDB, bkd[:, 0:64]),
              deps=[d3], tag="l0a_lndcp")
    env.sB[6].read(d4)
    d5 = P.op("vector", lambda e: e.tensor_scalar(DposT, LNDB,
              env.spf("iota_col"), None, ALU.mult), deps=[d4], tag="l0a_t1")
    d5b = P.op("scalar", lambda e: e.activation(DposT, DposT, AF.Exp),
               deps=[d5], tag="l0a_t1e")
    d6 = P.op("vector", lambda e: e.tensor_scalar(DnegT, LNDB,
              env.spf("iota_col"), -1.0, ALU.mult, ALU.mult), deps=[d4],
              tag="l0a_t2")
    d6b = P.op("scalar", lambda e: e.activation(DnegT, DnegT, AF.Exp),
               deps=[d6], tag="l0a_t2e")
    def t3fn(e):
        e.tensor_scalar(env.SC2[:, 192:193], env.spf("iota_col"), -1.0, 127.0,
                        ALU.mult, ALU.add)
        e.drain()
        return e.tensor_scalar(DrevT, LNDB, env.SC2[:, 192:193], None, ALU.mult)
    d7 = P.op("vector", t3fn, deps=[d4] + env.sSC2.wdeps(), tag="l0a_t3")
    env.sSC2.wrote(d7)
    d7b = P.op("scalar", lambda e: e.activation(DrevT, DrevT, AF.Exp),
               deps=[d7], tag="l0a_t3e")
    d8 = P.op("vector", lambda e: e.tensor_scalar(env.SC1[0:1, 64:128], R64,
              128.0, None, ALU.mult), deps=[d2] + env.sSC1.wdeps(), tag="l0a_t4")
    d8b = P.op("scalar", lambda e: e.activation(env.SC1[0:1, 64:128],
               env.SC1[0:1, 64:128], AF.Exp), deps=[d8], tag="l0a_t4e")
    d9 = P.op("tensor", lambda e: e.matmul(bkd[0:64, 64:128],
              env.spf("ones_row", w=64), env.SC1[0:1, 64:128],
              start=True, stop=True),
              deps=[d8b] + env.sB[6].wdeps(), tag="l0a_bd128")
    env.sB[6].wrote(d9)
    env.sSC1.read(d9)
    d10 = P.op("vector", lambda e: e.tensor_copy(DCOL, bkd[0:64, 64:128]),
               deps=[d9], tag="l0a_dcolcp")
    env.sB[6].read(d10)
    dsg = P.op("scalar", lambda e: e.activation(env.SC1[0:1, 128:192],
               env.spf("l0a_decr"), AF.Sigmoid), deps=g_dtb, tag="l0a_sig2")
    d11 = P.op("tensor", lambda e: e.matmul(bkd[0:64, 128:192],
               env.spf("ones_row", w=64), env.SC1[0:1, 128:192],
               start=True, stop=True),
               deps=[dsg] + env.sB[6].wdeps(), tag="l0a_bdrow")
    env.sB[6].wrote(d11)
    env.sSC1.read(d11)
    d12 = P.op("vector", lambda e: e.tensor_copy(DROW, bkd[0:64, 128:192]),
               deps=[d11], tag="l0a_drowcp")
    env.sB[6].read(d12)
    env.sDTB.wrote_many([d5b, d6b, d7b])
    env.sD64.wrote_many([d10, d12])
    env.sSC1.wrote_many([d9, d11])

    pq = []
    for t in range(4):
        for which, W, bias, out in (("q", "l0a_wqt", "l0a_bq", QF),
                                    ("k", "l0a_wkt", "l0a_bk", KF)):
            bk = env.bank(t % 2)

            def pgrp(e, W=W, t=t, bk=bk):
                r = None
                for c in range(NCHUNK):
                    r = e.matmul(bk[0:64, :], env.spb(W)[:, 64 * c : 64 * c + 64],
                                 env.XLN[:, c * S + 512 * t : c * S + 512 * t + 512],
                                 start=(c == 0), stop=(c == NCHUNK - 1))
                return r
            mm = P.op("tensor", pgrp, deps=env.sXLN.rdeps() + [env.sSPB.writers] +
                      env.sB[t % 2].wdeps(), tag=f"l0a_p{which}{t}")
            env.sB[t % 2].wrote(mm)
            env.sXLN.read(mm)
            sl = sQF if which == "q" else sKF
            sv = P.op("scalar", lambda e, bk=bk, t=t, out=out, bias=bias:
                      e.activation(out[0:64, 512 * t : 512 * (t + 1)], bk[0:64, :],
                                   AF.Identity, bias=env.spf(bias)),
                      deps=[mm] + (sl.wdeps() if t == 0 else []),
                      tag=f"l0a_e{which}{t}")
            env.sB[t % 2].read(sv)
            sl.read(sv)
            pq.append(sv)
    sQF.wrote_many([o for o in pq[::2]])
    sKF.wrote_many([o for o in pq[1::2]])
    ptm = {"k": [], "v": []}
    for gch in range(16):
        for which, W, bias, out, sl in (("k", "l0a_wkt", "l0a_bk", KT, sKT),
                                        ("v", "l0a_wvt", "l0a_bv", VT, sVT)):
            bk = env.bank(2 + (gch % 2))

            def tgrp(e, W=W, gch=gch, bk=bk):
                r = None
                for c in range(NCHUNK):
                    r = e.matmul(bk[0:128, 0:64],
                                 env.XLN[:, c * S + 128 * gch : c * S + 128 * gch + 128],
                                 env.spb(W)[:, 64 * c : 64 * c + 64],
                                 start=(c == 0), stop=(c == NCHUNK - 1))
                return r
            mm = P.op("tensor", tgrp, deps=env.sXLN.rdeps() + [env.sSPB.writers] +
                      env.sB[2 + gch % 2].wdeps(), tag=f"l0a_t{which}{gch}")
            env.sB[2 + gch % 2].wrote(mm)
            env.sXLN.read(mm)
            sv = P.op("vector", lambda e, bk=bk, gch=gch, out=out, bias=bias:
                      e.tensor_tensor(out[:, 64 * gch : 64 * (gch + 1)],
                                      bk[0:128, 0:64], env.BROW[bias],
                                      ALU.add),
                      deps=[mm, env.sBROW.writers] + (sl.wdeps() if gch == 0 else []),
                      tag=f"l0a_et{which}{gch}")
            env.sB[2 + gch % 2].read(sv)
            sl.read(sv)
            ptm[which].append(sv)
    sKT.wrote_many(ptm["k"])
    sVT.wrote_many(ptm["v"])

    z0 = P.op("vector", lambda e: e.memset(TST[:], 0.0),
              deps=env.sTST.wdeps(), tag="l0a_z0")
    env.sTST.wrote(z0)
    scn_ops = []
    for gch in range(16):
        sl = gch % 2
        APb = env.Dg[:, 2048 + sl * 128 : 2048 + sl * 128 + 128]
        VTb = env.Dg[:, 2560 + sl * 64 : 2560 + sl * 64 + 64]
        VRb = env.Dg[:, 2688 + sl * 64 : 2688 + sl * 64 + 64]
        bka = env.bank(4 + sl)
        mma = P.op("tensor", lambda e, gch=gch, bka=bka: e.matmul(
            bka[:, 0:128], KF[0:64, 128 * gch : 128 * (gch + 1)],
            QF[0:64, 128 * gch : 128 * (gch + 1)], start=True, stop=True),
            deps=sQF.rdeps() + sKF.rdeps() + env.sB[4 + sl].wdeps(),
            tag=f"l0a_A{gch}")
        env.sB[4 + sl].wrote(mma)
        sQF.read(mma)
        sKF.read(mma)
        va = P.op("vector", lambda e, APb=APb, bka=bka: e.tensor_tensor(
            APb, bka[:, 0:128], env.spf("triu"), ALU.mult),
            deps=[mma] + sAP.wdeps(), tag=f"l0a_Am{gch}")
        env.sB[4 + sl].read(va)
        sAP.wrote(va)
        vv = P.op("vector", lambda e, VTb=VTb, gch=gch: e.tensor_tensor(
            VTb, VT[:, 64 * gch : 64 * (gch + 1)], DnegT, ALU.mult),
            deps=sVT.rdeps() + [env.sDTB.writers] + sVTL.wdeps(),
            tag=f"l0a_vt{gch}")
        sVT.read(vv)
        vr = P.op("vector", lambda e, VRb=VRb, gch=gch: e.tensor_tensor(
            VRb, VT[:, 64 * gch : 64 * (gch + 1)], DrevT, ALU.mult),
            deps=sVT.rdeps() + [env.sDTB.writers] + sVTL.wdeps(),
            tag=f"l0a_vr{gch}")
        sVT.read(vr)
        sVTL.wrote_many([vv, vr])
        TT_ = env.Dg[0:64, 2816:2880]
        vt2 = P.op("vector", lambda e, TT_=TT_: e.tensor_tensor(
            TT_, TST[:], DROW, ALU.mult),
            deps=env.sTST.rdeps() + [env.sD64.writers] + env.sSC2.wdeps(),
            tag=f"l0a_ttilde{gch}")
        env.sTST.read(vt2)
        env.sSC2.wrote(vt2)
        bko = env.bank(6)
        mo1 = P.op("tensor", lambda e, gch=gch, bko=bko, TT_=TT_: e.matmul(
            bko[:, 0:64], QF[0:64, 128 * gch : 128 * (gch + 1)], TT_,
            start=True, stop=False), deps=[vt2] + sQF.rdeps() +
            env.sB[6].wdeps(), tag=f"l0a_o1{gch}")
        mo2 = P.op("tensor", lambda e, APb=APb, VTb=VTb, bko=bko: e.matmul(
            bko[:, 0:64], APb, VTb, start=False, stop=True),
            deps=[mo1, va, vv], tag=f"l0a_o2{gch}")
        env.sB[6].wrote(mo2)
        env.sSC2.read(mo1)
        sAP.read(mo2)
        sVTL.read(mo2)
        stm = env.SC2[:, 128:192]
        ve = P.op("vector", lambda e, stm=stm, bko=bko: e.tensor_tensor(
            stm, bko[:, 0:64], DposT, ALU.mult),
            deps=[mo2] + env.sSC2.wdeps(), tag=f"l0a_oscl{gch}")
        env.sB[6].read(ve)
        env.sSC2.wrote(ve)
        bkt = env.bank(7)
        tr = P.op("tensor", lambda e, stm=stm, bkt=bkt: e.transpose(
            bkt[0:64, 0:128], stm, env.spf("ident")),
            deps=[ve] + env.sB[7].wdeps(), tag=f"l0a_tr{gch}")
        env.sB[7].wrote(tr)
        env.sSC2.read(tr)
        ce = P.op("vector", lambda e, gch=gch, bkt=bkt: e.tensor_copy(
            SCN[0:64, 128 * gch : 128 * (gch + 1)], bkt[0:64, 0:128]),
            deps=[tr] + (sSCN.wdeps() if gch == 0 else []), tag=f"l0a_scn{gch}")
        env.sB[7].read(ce)
        scn_ops.append(ce)
        bks = env.bank(4 + sl)
        ms = P.op("tensor", lambda e, gch=gch, bks=bks, VRb=VRb: e.matmul(
            bks[0:64, 0:64], KT[:, 64 * gch : 64 * (gch + 1)], VRb,
            start=True, stop=True), deps=[vr, va] + sKT.rdeps() +
            env.sB[4 + sl].wdeps(), tag=f"l0a_su{gch}")
        env.sB[4 + sl].wrote(ms)
        sKT.read(ms)
        sVTL.read(ms)
        def upd(e, bks=bks):
            e.tensor_tensor(TST[:], TST[:], DCOL, ALU.mult)
            e.drain()
            return e.tensor_tensor(TST[:], TST[:], bks[0:64, 0:64], ALU.add)
        vu = P.op("vector", upd, deps=[ms, vt2, mo1] + env.sTST.wdeps(),
                  tag=f"l0a_tupd{gch}")
        env.sB[4 + sl].read(vu)
        env.sTST.wrote(vu)
    sSCN.wrote_many(scn_ops)

    for t in range(4):
        for oc in range(NCHUNK):
            bk = env.bank(t % 2)
            mm = P.op("tensor", lambda e, t=t, oc=oc, bk=bk: e.matmul(
                bk, env.spb("l0a_wot")[0:64, 128 * oc : 128 * (oc + 1)],
                SCN[0:64, 512 * t : 512 * (t + 1)], start=True, stop=True),
                deps=sSCN.rdeps() + env.sB[t % 2].wdeps(), tag=f"l0a_om{t}_{oc}")
            env.sB[t % 2].wrote(mm)
            sSCN.read(mm)
            sv = P.op("scalar", lambda e, bk=bk, oc=oc: e.activation(
                env.SC1[:, 0:512], bk, AF.Identity,
                bias=env.spf("l0a_bo", w=1, coff=oc)),
                deps=[mm] + env.sSC1.wdeps(), tag=f"l0a_ob{t}_{oc}")
            env.sB[t % 2].read(sv)
            env.sSC1.wrote(sv)
            xsl = slice(oc * S + 512 * t, oc * S + 512 * t + 512)
            def xadd(e, xsl=xsl):
                e.tensor_tensor(env.SC1[:, 512:1024], env.SC1[:, 0:512],
                                env.XLN[:, xsl], ALU.add)
                e.drain()
                return e.tensor_tensor(env.X[:, xsl], env.X[:, xsl],
                                       env.SC1[:, 512:1024], ALU.add)
            vx = P.op("vector", xadd, deps=[sv] + env.sXLN.rdeps() +
                      env.sX.wdeps() + env.sX.readers, tag=f"l0a_x{t}_{oc}")
            env.sSC1.read(vx)
            env.sXLN.read(vx)
            env.sX.wrote(vx)


_L1N = [0]


def emit_l1_ffn(env):
    P = env.P
    nc = env.nc
    call = _L1N[0]
    _L1N[0] += 1
    emit_ln(env)
    DgF = env.Dg.ap().bitcast(F32)

    if call == 0:
        if not hasattr(env, "TH"):
            env.TH = nc.alloc_sbuf_tensor("TH", [128, 4], F32)
            env.sTH = Slot("TH")
        for wi, (wh, nsl, nelem) in enumerate(
                ((env.l1w1_h, 32, 4096 * 1024), (env.l1w2_h, 8, 4096 * 1024))):
            ACC = env.SC1[:, 0:1]
            accops = []
            z0 = P.op("vector", lambda e, ACC=ACC: e.memset(ACC, 0.0),
                      deps=env.sSC1.wdeps(), tag=f"l1s_z{wi}")
            env.sSC1.wrote(z0)
            for s_i in range(nsl):
                slb = s_i % 2
                wcols = 1024 if wi == 0 else 4096
                FB = DgF[:, slb * 4096 : slb * 4096 + 1024] if wi == 0 else                     DgF[:, 0:4096]
                sFB = env.sD[slb if wi == 0 else 0]
                ld = P.dma("sync", lambda e, wh=wh, s_i=s_i, FB=FB: e.dma_start(
                    out=FB, in_=wh[s_i]), deps=sFB.wdeps(), tag=f"l1s_ld{wi}_{s_i}")
                sFB.wrote(ld)
                tcol = 1 if s_i % 2 == 0 else 3
                def redfn(e, FB=FB, ACC=ACC, tcol=tcol):
                    e.drain()
                    e.reduce_sum(env.SC1[:, tcol : tcol + 1], FB, axis=AX.X,
                                 apply_absolute_value=True)
                    e.drain()
                    return e.tensor_tensor(ACC, ACC, env.SC1[:, tcol : tcol + 1],
                                           ALU.add)
                rd = P.op("vector", redfn, deps=[ld, z0] + env.sSC1.wdeps(),
                          tag=f"l1s_r{wi}_{s_i}")
                sFB.read(rd)
                env.sSC1.wrote(rd)
                accops.append(rd)
            bk = env.bank(6)
            m1 = P.op("tensor", lambda e, bk=bk, ACC=ACC: e.matmul(
                bk[0:1, 0:1], env.spf("ones_col"), ACC, start=True, stop=True),
                deps=accops + env.sB[6].wdeps(), tag=f"l1s_t{wi}")
            env.sB[6].wrote(m1)
            env.sSC1.read(m1)
            def thfn(e, bk=bk, nelem=nelem):
                e.tensor_scalar(env.SC1[0:1, 2:3], bk[0:1, 0:1], 1.0 / nelem,
                                1e-8, ALU.mult, ALU.max)
                e.drain()
                return e.tensor_scalar(env.SC1[0:1, 2:3], env.SC1[0:1, 2:3],
                                       0.5, None, ALU.mult)
            t1 = P.op("vector", thfn, deps=[m1] + env.sSC1.wdeps(), tag=f"l1s_h{wi}")
            env.sB[6].read(t1)
            env.sSC1.wrote(t1)
            b1_ = P.op("tensor", lambda e, bk=bk: e.matmul(
                bk[:, 0:1], env.spf("ones_row"), env.SC1[0:1, 2:3],
                start=True, stop=True), deps=[t1] + env.sB[6].wdeps(),
                tag=f"l1s_b{wi}")
            env.sB[6].wrote(b1_)
            env.sSC1.read(b1_)
            c1 = P.op("vector", lambda e, bk=bk, wi=wi: e.tensor_copy(
                env.TH[:, wi : wi + 1], bk[:, 0:1]),
                deps=[b1_] + env.sTH.wdeps(), tag=f"l1s_c{wi}")
            env.sB[6].read(c1)
            env.sTH.writers.append(c1)
        if DEBUG_TAPS:
            P.tap(nc, "l1th", env.TH[0:1, 0:4], deps=list(env.sTH.writers))

    def ternarize(s_i, wh, wbh, thcol, cols, tag):
        slb = s_i % 2
        if cols == 1024:
            FB = DgF[:, slb * 4096 : slb * 4096 + cols]
            sFB = env.sD[slb]
            TB = env.E[:, slb * 1024 : slb * 1024 + cols]
            sTB = env.sE[slb]
        else:
            FB = DgF[:, 0:4096]
            sFB = env.sD[0]
            TB = env.C[:, slb * 4096 : slb * 4096 + 4096]
            sTB = env.sC[slb]
        ld = P.dma("gpsimd", lambda e, wh=wh, s_i=s_i, FB=FB: e.dma_start(
            out=FB, in_=wh[s_i]), deps=sFB.wdeps(), tag=f"{tag}ld{s_i}")
        sFB.wrote(ld)
        v1s = []
        tb_guard = sTB.wdeps()
        for h0 in range(0, cols, 2048):
            w_ = min(2048, cols - h0)
            FBh = FB[:, h0 : h0 + w_]
            TBh = TB[:, h0 : h0 + w_]
            SG = env.SC2[:, 0:w_]
            s1 = P.op("scalar", lambda e, FBh=FBh, SG=SG: e.activation(
                SG, FBh, AF.Sign), deps=[ld] + env.sSC2.wdeps(),
                tag=f"{tag}sg{s_i}_{h0}")
            env.sSC2.wrote(s1)
            sFB.read(s1)
            s2 = P.op("scalar", lambda e, FBh=FBh: e.activation(FBh, FBh, AF.Abs),
                      deps=[ld, s1], tag=f"{tag}ab{s_i}_{h0}")
            sFB.read(s2)
            def mkt(e, FBh=FBh, SG=SG, TBh=TBh, thcol=thcol):
                e.tensor_scalar(FBh, FBh, env.TH[:, thcol : thcol + 1], None,
                                ALU.is_gt)
                e.drain()
                return e.tensor_tensor(TBh, SG, FBh, ALU.mult)
            v1 = P.op("vector", mkt, deps=[s2, s1, env.sTH.writers] + tb_guard,
                      tag=f"{tag}tn{s_i}_{h0}")
            sFB.read(v1)
            env.sSC2.read(v1)
            v1s.append(v1)
        sFB.wrote_many(v1s)
        sTB.wrote_many(v1s)
        v1 = v1s
        st = P.dma("sync", lambda e, wbh=wbh, s_i=s_i, TB=TB: e.dma_start(
            out=wbh[s_i], in_=TB), deps=[v1], tag=f"{tag}st{s_i}")
        sTB.read(st)
        env.tern_st = getattr(env, "tern_st", [])
        env.tern_st.append(st)
        return TB, sTB, [v1, st]

    h_store = []
    for hc in range(32):
        if call == 0:
            W1S, sW1, rdy = ternarize(hc, env.l1w1_h, env.l1w1b_h, 0, 1024, "l1a")
        else:
            slb = hc % 2
            W1S = env.E[:, slb * 1024 : slb * 1024 + 1024]
            sW1 = env.sE[slb]
            ld = P.dma("gpsimd", lambda e, hc=hc, W1S=W1S: e.dma_start(
                out=W1S, in_=env.l1w1b_h[hc]),
                deps=sW1.wdeps() + getattr(env, "tern_st", []),
                tag=f"l1w1ld_{hc}")
            sW1.wrote(ld)
            rdy = [ld]
        hsl = 2 + (hc % 2)
        HC = env.E[:, 2048 + (hc % 2) * 2048 : 2048 + (hc % 2) * 2048 + 2048]
        sHC = env.sE[hsl]
        hc_guard = sHC.wdeps()
        silus = []
        for t in range(4):
            bk = env.bank(t % 2)

            def w1grp(e, W1S=W1S, t=t, bk=bk):
                r = None
                for c in range(NCHUNK):
                    r = e.matmul(bk, W1S[:, 128 * c : 128 * (c + 1)],
                                 env.XLN[:, c * S + 512 * t : c * S + 512 * t + 512],
                                 start=(c == 0), stop=(c == NCHUNK - 1))
                return r
            mm = P.op("tensor", w1grp, deps=rdy + env.sXLN.rdeps() +
                      env.sB[t % 2].wdeps(), tag=f"l1w1mm_{call}_{hc}_{t}")
            env.sB[t % 2].wrote(mm)
            env.sXLN.read(mm)
            sW1.read(mm)
            b1ap = env.spf("l1_b1", w=1, coff=hc)
            sv = P.op("scalar", lambda e, t=t, bk=bk, HC=HC, b1ap=b1ap:
                      e.activation(HC[:, 512 * t : 512 * (t + 1)], bk, AF.Gelu,
                                   bias=b1ap),
                      deps=[mm] + hc_guard, tag=f"l1w1g_{call}_{hc}_{t}")
            env.sB[t % 2].read(sv)
            silus.append(sv)
        sHC.wrote_many(silus)
        std = P.dma("sync", lambda e, hc=hc, HC=HC: e.dma_start(
            out=env.hdram_h[:, :, 512 * hc : 512 * (hc + 1)].transpose([1, 0, 2]),
            in_=HC[:].rearrange("p (t w) -> p t w", t=4)),
            deps=silus, tag=f"l1hst_{call}_{hc}")
        sHC.read(std)
        h_store.append(std)

    if call == 0:
        for och in range(8):
            ternarize(och, env.l1w2_h, env.l1w2b_h, 1, 4096, "l1b")

    for t in range(4):
        hdeps = []
        for hh in range(2):
            hs = env.sD[hh]
            HT = env.Dg[:, hh * 8192 : hh * 8192 + 8192]
            ldh = P.dma("sync", lambda e, t=t, hh=hh, HT=HT: e.dma_start(
                out=HT, in_=env.hdram_h[t, :, 8192 * hh : 8192 * (hh + 1)]),
                deps=hs.wdeps() + h_store, tag=f"l1htld_{call}_{t}_{hh}")
            hs.wrote(ldh)
            hdeps.append(ldh)
        for och in range(8):
            qs = och % 2
            W2S = env.C[:, qs * 4096 : qs * 4096 + 4096]
            sW2 = env.sC[qs]
            ldw2 = P.dma("gpsimd", lambda e, och=och, W2S=W2S: e.dma_start(
                out=W2S, in_=env.l1w2b_h[och]),
                deps=sW2.wdeps() + getattr(env, "tern_st", []),
                tag=f"l1w2ld_{call}_{t}_{och}")
            sW2.wrote(ldw2)
            rdy2 = [ldw2]
            bk = env.bank(2 + och % 2)

            def w2grp(e, W2S=W2S, bk=bk):
                r = None
                for hcc in range(32):
                    r = e.matmul(
                        bk, W2S[:, 128 * hcc : 128 * (hcc + 1)],
                        env.Dg[:, (hcc // 16) * 8192 + (hcc % 16) * 512 :
                               (hcc // 16) * 8192 + (hcc % 16) * 512 + 512],
                        start=(hcc == 0), stop=(hcc == 31))
                return r
            mm = P.op("tensor", w2grp, deps=rdy2 + hdeps +
                      env.sB[2 + och % 2].wdeps(), tag=f"l1w2mm_{call}_{t}_{och}")
            env.sB[2 + och % 2].wrote(mm)
            sW2.read(mm)
            env.sD[0].read(mm)
            env.sD[1].read(mm)
            sv = P.op("scalar", lambda e, bk=bk, och=och: e.activation(
                env.SC1[:, 512:1024], bk, AF.Identity,
                bias=env.spf("l1_b2", w=1, coff=och)),
                deps=[mm] + env.sSC1.wdeps(), tag=f"l1y_{call}_{t}_{och}")
            env.sB[2 + och % 2].read(sv)
            env.sSC1.wrote(sv)
            xsl = slice(och * S + 512 * t, och * S + 512 * t + 512)
            vx = P.op("vector", lambda e, xsl=xsl: e.scalar_tensor_tensor(
                env.X[:, xsl], env.SC1[:, 512:1024], 0.5, env.X[:, xsl],
                ALU.mult, ALU.add),
                deps=[sv] + env.sX.wdeps() + env.sX.readers, tag=f"l1x_{call}_{t}_{och}")
            env.sSC1.read(vx)
            env.sX.wrote(vx)


def emit_l1_attn(env):
    P = env.P
    nc = env.nc
    emit_ln(env)

    SMF = env.E[:, 0:256]
    KS = env.E[:, 256:512]
    VS = env.E[:, 512:768]
    VTS = env.E[0:32, 768:1792]
    WB0 = env.E[:, 2048:3072]
    WB1 = env.E[:, 3072:4096]
    QS = env.Dg
    HB = env.C[:, 0:4096]
    sSM, sKS, sVS, sVTS = Slot("SM"), Slot("KS"), Slot("VS"), Slot("VTS")
    sQS = env.sD[0]
    sHB = env.sC[0]
    sWB = [env.sE[0], env.sE[1]]

    smops = []
    smg = sSM.wdeps() + [x for s_ in env.sE for x in s_.wdeps()]
    for c in range(NCHUNK):
        def smfn(e, c=c):
            e.tensor_copy(SMF[:, 32 * c : 32 * c + 1],
                          env.XLN[:, c * S : c * S + 1])
            e.reduce_sum(env.SC2[:, 0:31],
                         env.XLN[:, c * S + 1 : c * S + 1985].rearrange(
                             "p (w i) -> p w i", i=64), axis=AX.X)
            e.drain()
            return e.tensor_scalar(SMF[:, 32 * c + 1 : 32 * c + 32],
                                   env.SC2[:, 0:31],
                                   1.0 / 64.0, None, ALU.mult)
        o = P.op("vector", smfn, deps=env.sXLN.rdeps() + smg +
                 env.sSC2.wdeps(), tag=f"l1s_sm{c}")
        env.sXLN.read(o)
        env.sSC2.wrote(o)
        smops.append(o)
    sSM.wrote_many(smops)

    for which, wh, bias, out, sl, scale in (
            ("k", env.l1wk_h, "l1a_bk", KS, sKS, "c_inv8"),
            ("v", env.l1wv_h, "l1a_bv", VS, sVS, None)):
        for oc in range(NCHUNK):
            slb = oc % 2
            WB = (WB0, WB1)[slb]
            ld = P.dma("gpsimd", lambda e, wh=wh, oc=oc, WB=WB: e.dma_start(
                out=WB, in_=wh[oc]), deps=sWB[slb].wdeps(), tag=f"l1a_{which}ld{oc}")
            sWB[slb].wrote(ld)
            bk = env.bank(6)

            def kgrp(e, WB=WB, bk=bk):
                r = None
                for c in range(NCHUNK):
                    r = e.matmul(bk[:, 0:32], WB[:, 128 * c : 128 * (c + 1)],
                                 SMF[:, 32 * c : 32 * (c + 1)],
                                 start=(c == 0), stop=(c == NCHUNK - 1))
                return r
            mm = P.op("tensor", kgrp, deps=[ld] + sSM.rdeps() + env.sB[6].wdeps(),
                      tag=f"l1a_{which}mm{oc}")
            env.sB[6].wrote(mm)
            sSM.read(mm)
            sWB[slb].read(mm)
            kw = dict(bias=env.spf(bias, w=1, coff=oc))
            if scale:
                kw["scale"] = env.spf(scale)
            sv = P.op("scalar", lambda e, bk=bk, oc=oc, out=out, kw=kw:
                      e.activation(out[:, 32 * oc : 32 * (oc + 1)], bk[:, 0:32],
                                   AF.Identity, **kw),
                      deps=[mm] + (sl.wdeps() if oc == 0 else []),
                      tag=f"l1a_{which}e{oc}")
            env.sB[6].read(sv)
            sl.read(sv)
        sl.wrote_many(sl.readers[-NCHUNK:])
    vtops = []
    for h in range(16):
        bp = 64 * (h % 2)
        col = 32 * (h // 2)
        def trfn(e, h=h, bp=bp, col=col):
            e.transpose(VTS[0:32, 64 * h : 64 * h + 32],
                        VS[bp : bp + 32, col : col + 32])
            e.drain()
            return e.transpose(VTS[0:32, 64 * h + 32 : 64 * h + 64],
                               VS[bp + 32 : bp + 64, col : col + 32])
        o = P.op("vector", trfn, deps=sVS.rdeps() + sVTS.wdeps(), tag=f"l1a_vt{h}")
        sVS.read(o)
        vtops.append(o)
    sVTS.wrote_many(vtops)

    qops = []
    for oc in range(NCHUNK):
        slb = oc % 2
        WB = (WB0, WB1)[slb]
        ld = P.dma("gpsimd", lambda e, oc=oc, WB=WB: e.dma_start(
            out=WB, in_=env.l1wq_h[oc]), deps=sWB[slb].wdeps(), tag=f"l1a_qld{oc}")
        sWB[slb].wrote(ld)
        for t in range(4):
            bk = env.bank(t % 2)

            def qgrp(e, WB=WB, t=t, bk=bk):
                r = None
                for c in range(NCHUNK):
                    r = e.matmul(bk, WB[:, 128 * c : 128 * (c + 1)],
                                 env.XLN[:, c * S + 512 * t : c * S + 512 * t + 512],
                                 start=(c == 0), stop=(c == NCHUNK - 1))
                return r
            mm = P.op("tensor", qgrp, deps=[ld] + env.sXLN.rdeps() +
                      env.sB[t % 2].wdeps(), tag=f"l1a_qmm{oc}_{t}")
            env.sB[t % 2].wrote(mm)
            env.sXLN.read(mm)
            sWB[slb].read(mm)
            sv = P.op("scalar", lambda e, bk=bk, oc=oc, t=t: e.activation(
                QS[:, oc * S + 512 * t : oc * S + 512 * t + 512], bk, AF.Identity,
                bias=env.spf("l1a_bq", w=1, coff=oc)),
                deps=[mm] + (sQS.wdeps() if oc == 0 and t == 0 else []),
                tag=f"l1a_qe{oc}_{t}")
            env.sB[t % 2].read(sv)
            sQS.read(sv)
            qops.append(sv)
    sQS.wrote_many(qops)

    for t in range(4):
        hb_ops = []
        hb_guard = sHB.wdeps()
        for h in range(16):
            bp = 64 * (h % 2)
            qcol = (h // 2) * S + 512 * t
            bkS = env.bank(4)
            mmS = P.op("tensor", lambda e, h=h, bp=bp, qcol=qcol, bkS=bkS:
                       e.matmul(bkS[0:32, :], KS[bp : bp + 64, 32 * (h // 2) :
                                                 32 * (h // 2) + 32],
                                QS[bp : bp + 64, qcol : qcol + 512],
                                start=True, stop=True),
                       deps=sKS.rdeps() + sQS.rdeps() + env.sB[4].wdeps(),
                       tag=f"l1a_s{t}_{h}")
            env.sB[4].wrote(mmS)
            sKS.read(mmS)
            sQS.read(mmS)
            SCR = env.SC2[0:32, 0:512]
            c1 = P.op("vector", lambda e, bkS=bkS, SCR=SCR: e.tensor_copy(
                SCR, bkS[0:32, :]), deps=[mmS] + env.sSC2.wdeps(),
                tag=f"l1a_sc{t}_{h}")
            env.sB[4].read(c1)
            env.sSC2.wrote(c1)
            EB = env.SC2[0:32, 512:1024]
            s1 = P.op("scalar", lambda e, SCR=SCR, EB=EB: e.activation(
                EB, SCR, AF.Exp), deps=[c1], tag=f"l1a_exp{t}_{h}")
            env.sSC2.wrote(s1)
            g1 = P.op("gpsimd", lambda e, t=t, EB=EB: e.affine_select(
                EB, EB, pattern=[[1, 512]], compare_op=ALU.is_ge,
                fill=0.0, base=512 * t, channel_multiplier=-64),
                deps=[s1], tag=f"l1a_msk{t}_{h}")
            env.sSC2.wrote(g1)
            EBb = env.SC2.ap().bitcast(BF16)[0:32, 2048:2560]
            c2 = P.op("vector", lambda e, EB=EB, EBb=EBb: e.tensor_copy(EBb, EB),
                      deps=[g1], tag=f"l1a_ebb{t}_{h}")
            env.sSC2.wrote(c2)
            bkR = env.bank(5)
            mmR = P.op("tensor", lambda e, EBb=EBb, bkR=bkR: e.matmul(
                bkR[0:1, :], env.spb("ones_colb", p=32), EBb, start=True,
                stop=True), deps=[c2] + env.sB[5].wdeps(), tag=f"l1a_cs{t}_{h}")
            env.sB[5].wrote(mmR)
            env.sSC2.read(mmR)
            RR = env.SC1[0:1, 0:512]
            def rcp(e, bkR=bkR, RR=RR):
                e.tensor_copy(RR, bkR[0:1, :])
                e.drain()
                return e.reciprocal(RR, RR)
            v1 = P.op("vector", rcp, deps=[mmR] + env.sSC1.wdeps(),
                      tag=f"l1a_rcp{t}_{h}")
            env.sB[5].read(v1)
            env.sSC1.wrote(v1)
            bkA = env.bank(6)
            mmA = P.op("tensor", lambda e, h=h, EBb=EBb, bkA=bkA: e.matmul(
                bkA[0:64, :], VTS[0:32, 64 * h : 64 * (h + 1)], EBb,
                start=True, stop=True),
                deps=[c2] + sVTS.rdeps() + env.sB[6].wdeps(), tag=f"l1a_av{t}_{h}")
            env.sB[6].wrote(mmA)
            sVTS.read(mmA)
            env.sSC2.read(mmA)
            bkB = env.bank(7)
            mmB = P.op("tensor", lambda e, RR=RR, bkB=bkB: e.matmul(
                bkB[0:64, :], env.spf("ones_row", w=64), RR, start=True,
                stop=True), deps=[v1] + env.sB[7].wdeps(), tag=f"l1a_bc{t}_{h}")
            env.sB[7].wrote(mmB)
            env.sSC1.read(mmB)
            hcol = 512 * (h % 8) if False else 0
            dst = HB[bp : bp + 64, 512 * (h // 2) : 512 * (h // 2) + 512]
            RB = env.SC2[0:64, 1280:1792]
            cpA = P.op("scalar", lambda e, bkA=bkA, RB=RB: e.activation(
                RB, bkA[0:64, :], AF.Identity),
                deps=[mmA] + env.sSC2.wdeps(), tag=f"l1a_av_ev{t}_{h}")
            env.sB[6].read(cpA)
            env.sSC2.wrote(cpA)
            vh = P.op("vector", lambda e, dst=dst, RB=RB, bkB=bkB:
                      e.tensor_tensor(dst, RB, bkB[0:64, :], ALU.mult),
                      deps=[cpA, mmB] + hb_guard, tag=f"l1a_hb{t}_{h}")
            env.sB[7].read(vh)
            env.sSC2.read(vh)
            sHB.read(vh)
            hb_ops.append(vh)
        sHB.wrote_many(hb_ops)
        for oc in range(NCHUNK):
            slb = oc % 2
            WB = (WB0, WB1)[slb]
            ld = P.dma("gpsimd", lambda e, oc=oc, WB=WB: e.dma_start(
                out=WB, in_=env.l1wo_h[oc]), deps=sWB[slb].wdeps(),
                tag=f"l1a_old{t}_{oc}")
            sWB[slb].wrote(ld)
            bk = env.bank(t % 2)

            def ogrp(e, WB=WB, bk=bk):
                r = None
                for c in range(NCHUNK):
                    r = e.matmul(bk, WB[:, 128 * c : 128 * (c + 1)],
                                 HB[:, 512 * c : 512 * (c + 1)],
                                 start=(c == 0), stop=(c == NCHUNK - 1))
                return r
            mm = P.op("tensor", ogrp, deps=[ld] + sHB.rdeps() +
                      env.sB[t % 2].wdeps(), tag=f"l1a_om{t}_{oc}")
            env.sB[t % 2].wrote(mm)
            sHB.read(mm)
            sWB[slb].read(mm)
            sv = P.op("scalar", lambda e, bk=bk, oc=oc: e.activation(
                env.SC1[:, 512:1024], bk, AF.Identity,
                bias=env.spf("l1a_bo", w=1, coff=oc)),
                deps=[mm] + env.sSC1.wdeps(), tag=f"l1a_ob{t}_{oc}")
            env.sB[t % 2].read(sv)
            env.sSC1.wrote(sv)
            xsl = slice(oc * S + 512 * t, oc * S + 512 * t + 512)
            vx = P.op("vector", lambda e, xsl=xsl: e.tensor_tensor(
                env.X[:, xsl], env.X[:, xsl], env.SC1[:, 512:1024], ALU.add),
                deps=[sv] + env.sX.wdeps() + env.sX.readers, tag=f"l1a_x{t}_{oc}")
            env.sSC1.read(vx)
            env.sX.wrote(vx)


_L2N = [0]
_L3N = [0]


def emit_l2_ffn(env):
    P = env.P
    nc = env.nc
    call = _L2N[0]
    _L2N[0] += 1
    emit_ln(env)

    wtm_ops = []
    wtm_guard = env.sWTM.wdeps()
    for j in range(NTTILE):
        bk = env.bank(5)

        def rgrp(e, j=j, bk=bk):
            r = None
            for c in range(NCHUNK):
                r = e.matmul(bk[0:128, 0:16],
                             env.XLN[:, c * S + 128 * j : c * S + 128 * j + 128],
                             env.spb("l2f_gwt")[:, c * 16 : c * 16 + 16],
                             start=(c == 0), stop=(c == NCHUNK - 1))
            return r
        mm = P.op("tensor", rgrp, deps=env.sXLN.rdeps() + [env.sSPB.writers] +
                  env.sB[5].wdeps(), tag=f"l2r_mm{j}")
        env.sB[5].wrote(mm)
        env.sXLN.read(mm)
        LG = env.SC1[:, 0:16]
        M1 = env.SC1[:, 16:17]
        M2 = env.SC1[:, 17:18]
        P1 = env.SC1[:, 18:19]
        P2 = env.SC1[:, 19:20]
        AA = env.SC1[:, 20:21]
        GE1 = env.SC1[:, 32:48]
        GE2 = env.SC1[:, 48:64]
        WW = env.SC1[:, 64:80]
        g = env.sSC1.wdeps()
        v0 = P.op("vector", lambda e, bk=bk: e.tensor_copy(LG, bk[0:128, 0:16]),
                  deps=[mm] + g, tag=f"l2r_lg{j}")
        env.sB[5].read(v0)
        v1 = P.op("vector", lambda e: e.reduce_max(M1, LG, axis=AX.X),
                  deps=[v0], tag=f"l2r_m1{j}")
        v2 = P.op("vector", lambda e: e.tensor_scalar(GE1, LG, M1, None, ALU.is_ge),
                  deps=[v1], tag=f"l2r_ge1{j}")
        v3 = P.op("vector", lambda e: e.scalar_tensor_tensor(
            GE2, GE1, -1e30, LG, ALU.mult, ALU.add), deps=[v2], tag=f"l2r_msk{j}")
        v4 = P.op("vector", lambda e: e.reduce_max(M2, GE2, axis=AX.X),
                  deps=[v3], tag=f"l2r_m2{j}")
        def pfn(e):
            e.tensor_tensor(P1, M1, M2, ALU.subtract)
            return None
        v5 = P.op("vector", pfn, deps=[v4], tag=f"l2r_d{j}")
        s1 = P.op("scalar", lambda e: e.activation(P1, P1, AF.Sigmoid),
                  deps=[v5], tag=f"l2r_sg{j}")
        def wfn(e):
            e.tensor_scalar(P2, P1, -1.0, 1.0, ALU.mult, ALU.add)
            e.tensor_tensor(AA, P1, P2, ALU.subtract)
            e.drain()
            e.tensor_scalar(GE2, LG, M2, None, ALU.is_ge)
            e.drain()
            e.tensor_scalar(GE2, GE2, P2, None, ALU.mult)
            e.tensor_scalar(GE1, GE1, AA, None, ALU.mult)
            e.drain()
            return e.tensor_tensor(WW, GE1, GE2, ALU.add)
        v6 = P.op("vector", wfn, deps=[s1, v2], tag=f"l2r_w{j}")
        env.sSC1.wrote(v6)
        bkt = env.bank(4)
        t1 = P.op("tensor", lambda e, bkt=bkt: e.transpose(
            bkt[0:16, 0:128], WW, env.spf("ident")),
            deps=[v6] + env.sB[4].wdeps() + [env.sSPF.writers], tag=f"l2r_tr{j}")
        env.sB[4].wrote(t1)
        env.sSC1.read(t1)
        c1 = P.op("vector", lambda e, j=j, bkt=bkt: e.tensor_copy(
            env.WTM[0:16, 128 * j : 128 * (j + 1)], bkt[0:16, 0:128]),
            deps=[t1] + wtm_guard, tag=f"l2r_cp{j}")
        env.sB[4].read(c1)
        wtm_ops.append(c1)
    env.sWTM.wrote_many(wtm_ops)

    dx = P.dma("sync", lambda e: e.dma_start(out=env.xdump_h[:], in_=env.X[:]),
               deps=env.sX.rdeps(), tag=f"l2f_dumpx{call}")
    env.sX.read(dx)
    YACC = env.X

    h_store = []
    for hc in range(16):
        slb = hc % 2
        W1S = env.E[:, slb * 1024 : slb * 1024 + 1024]
        W2S = env.E[:, 2048 + slb * 1024 : 2048 + slb * 1024 + 1024]
        sW1 = env.sE[slb]
        sW2 = env.sE[2 + slb]
        ld1 = P.dma("gpsimd", lambda e, hc=hc, W1S=W1S: e.dma_start(
            out=W1S, in_=env.l2sf1_h[hc]), deps=sW1.wdeps(), tag=f"l2s1ld{call}_{hc}")
        sW1.wrote(ld1)
        ld2 = P.dma("gpsimd", lambda e, hc=hc, W2S=W2S: e.dma_start(
            out=W2S, in_=env.l2sf2_h[hc]), deps=sW2.wdeps(), tag=f"l2s2ld{call}_{hc}")
        sW2.wrote(ld2)
        GC = env.C[:, (hc % 2) * 2048 : (hc % 2) * 2048 + 2048]
        sGC = env.sC[hc % 2]
        gc_guard = sGC.wdeps()
        gops = []
        for t in range(4):
            bka = env.bank(t % 2)
            bkb = env.bank(2 + t % 2)

            def agrp(e, W1S=W1S, t=t, bka=bka):
                r = None
                for c in range(NCHUNK):
                    r = e.matmul(bka, W1S[:, 128 * c : 128 * (c + 1)],
                                 env.XLN[:, c * S + 512 * t : c * S + 512 * t + 512],
                                 start=(c == 0), stop=(c == NCHUNK - 1))
                return r
            mma = P.op("tensor", agrp, deps=[ld1] + env.sXLN.rdeps() +
                       env.sB[t % 2].wdeps(), tag=f"l2sa{call}_{hc}_{t}")
            env.sB[t % 2].wrote(mma)
            env.sXLN.read(mma)
            sW1.read(mma)

            def bgrp(e, W2S=W2S, t=t, bkb=bkb):
                r = None
                for c in range(NCHUNK):
                    r = e.matmul(bkb, W2S[:, 128 * c : 128 * (c + 1)],
                                 env.XLN[:, c * S + 512 * t : c * S + 512 * t + 512],
                                 start=(c == 0), stop=(c == NCHUNK - 1))
                return r
            mmb = P.op("tensor", bgrp, deps=[ld2] + env.sXLN.rdeps() +
                       env.sB[2 + t % 2].wdeps(), tag=f"l2sb{call}_{hc}_{t}")
            env.sB[2 + t % 2].wrote(mmb)
            env.sXLN.read(mmb)
            sW2.read(mmb)
            sv = P.op("scalar", lambda e, t=t, bka=bka: e.activation(
                env.SC1[:, 0:512], bka, AF.Silu),
                deps=[mma] + env.sSC1.wdeps(), tag=f"l2ss{call}_{hc}_{t}")
            env.sB[t % 2].read(sv)
            env.sSC1.wrote(sv)
            vg = P.op("vector", lambda e, t=t, GC=GC, bkb=bkb: e.tensor_tensor(
                GC[:, 512 * t : 512 * (t + 1)], env.SC1[:, 0:512], bkb, ALU.mult),
                deps=[sv, mmb] + gc_guard, tag=f"l2sg{call}_{hc}_{t}")
            env.sB[2 + t % 2].read(vg)
            env.sSC1.read(vg)
            sGC.read(vg)
            gops.append(vg)
        sGC.wrote_many(gops)
        std = P.dma("sync", lambda e, hc=hc, GC=GC: e.dma_start(
            out=env.hdram_h[:, :, 512 * hc : 512 * (hc + 1)].transpose([1, 0, 2]),
            in_=GC[:].rearrange("p (t w) -> p t w", t=4)),
            deps=gops, tag=f"l2gst{call}_{hc}")
        sGC.read(std)
        h_store.append(std)
    for t in range(4):
        hdeps = []
        hs = env.sD[0]
        HT = env.Dg[:, 0:8192]
        ldh = P.dma("sync", lambda e, t=t, HT=HT: e.dma_start(
            out=HT, in_=env.hdram_h[t, :, 0:8192]),
            deps=hs.wdeps() + h_store, tag=f"l2gld{call}_{t}")
        hs.wrote(ldh)
        hdeps.append(ldh)
        for och in range(NCHUNK):
            slb = och % 2
            W3S = env.E[:, slb * 2048 : slb * 2048 + 2048]
            if slb == 0:
                w3slots = [env.sE[0], env.sE[1]]
            else:
                w3slots = [env.sE[2]]
            ld3 = P.dma("gpsimd", lambda e, och=och, W3S=W3S: e.dma_start(
                out=W3S, in_=env.l2sf3_h[och]),
                deps=[d for s_ in w3slots for d in s_.wdeps()],
                tag=f"l2s3ld{call}_{t}_{och}")
            for s_ in w3slots:
                s_.wrote(ld3)
            sW3 = w3slots[0]
            bk = env.bank(4 + och % 2)

            def w3grp(e, W3S=W3S, bk=bk):
                r = None
                for hcc in range(16):
                    r = e.matmul(bk, W3S[:, 128 * hcc : 128 * (hcc + 1)],
                                 env.Dg[:, 512 * hcc : 512 * hcc + 512],
                                 start=(hcc == 0), stop=(hcc == 15))
                return r
            mm = P.op("tensor", w3grp, deps=[ld3] + hdeps +
                      env.sB[4 + och % 2].wdeps(), tag=f"l2s3m{call}_{t}_{och}")
            env.sB[4 + och % 2].wrote(mm)
            for s_ in w3slots:
                s_.read(mm)
            hs.read(mm)
            ysl = slice(och * S + 512 * t, och * S + 512 * t + 512)
            yop = P.op("vector", lambda e, ysl=ysl, bk=bk: e.tensor_copy(
                YACC[:, ysl], bk), deps=[mm, dx] + env.sX.wdeps(),
                tag=f"l2sy{call}_{t}_{och}")
            env.sB[4 + och % 2].read(yop)
            env.sX.wrote(yop)

    for e_i in range(16):
        HE = env.Dg[:, 8192:16384]
        sHE = env.sD[1]
        heops = []
        he_guard = sHE.wdeps()
        for hc in range(4):
            slb = hc % 2
            W1S = env.E[:, slb * 1024 : slb * 1024 + 1024]
            sW1 = env.sE[slb]
            ld = P.dma("gpsimd", lambda e, e_i=e_i, hc=hc, W1S=W1S: e.dma_start(
                out=W1S, in_=env.l2fw1_h[e_i, hc]), deps=sW1.wdeps(),
                tag=f"l2w1ld{call}_{e_i}_{hc}")
            sW1.wrote(ld)
            for t in range(4):
                bk = env.bank(t % 2)

                def w1grp(e, W1S=W1S, t=t, bk=bk):
                    r = None
                    for c in range(NCHUNK):
                        r = e.matmul(bk, W1S[:, 128 * c : 128 * (c + 1)],
                                     env.XLN[:, c * S + 512 * t : c * S + 512 * t + 512],
                                     start=(c == 0), stop=(c == NCHUNK - 1))
                    return r
                mm = P.op("tensor", w1grp, deps=[ld] + env.sXLN.rdeps() +
                          env.sB[t % 2].wdeps(), tag=f"l2w1m{call}_{e_i}_{hc}_{t}")
                env.sB[t % 2].wrote(mm)
                env.sXLN.read(mm)
                sW1.read(mm)
                sv = P.op("scalar", lambda e, hc=hc, t=t, bk=bk, HE=HE:
                          e.activation(HE[:, 2048 * hc + 512 * t :
                                          2048 * hc + 512 * t + 512], bk, AF.Silu),
                          deps=[mm] + he_guard, tag=f"l2hs{call}_{e_i}_{hc}_{t}")
                env.sB[t % 2].read(sv)
                sHE.read(sv)
                heops.append(sv)
        sHE.wrote_many(heops)
        for t in range(4):
            bkw = env.bank(4)
            WROW = env.SC1[0:1, 1536:2048]
            b1_ = P.op("tensor", lambda e, e_i=e_i, t=t, bkw=bkw: e.matmul(
                bkw[0:1, :], env.spf("id16", p=16, w=1, coff=e_i),
                env.WTM[0:16, 512 * t : 512 * (t + 1)], start=True, stop=True),
                deps=env.sWTM.rdeps() + env.sB[4].wdeps(),
                tag=f"l2wr{call}_{e_i}_{t}")
            env.sB[4].wrote(b1_)
            env.sWTM.read(b1_)
            c1_ = P.op("vector", lambda e, bkw=bkw: e.tensor_copy(
                WROW, bkw[0:1, :]), deps=[b1_] + env.sSC1.wdeps(),
                tag=f"l2wrc{call}_{e_i}_{t}")
            env.sB[4].read(c1_)
            env.sSC1.wrote(c1_)
            bw = P.op("tensor", lambda e, bkw=bkw: e.matmul(
                bkw, env.spf("ones_row"), WROW, start=True, stop=True),
                deps=[c1_] + env.sB[4].wdeps(), tag=f"l2wb{call}_{e_i}_{t}")
            env.sB[4].wrote(bw)
            env.sSC1.read(bw)
            cw = P.op("vector", lambda e, bkw=bkw: e.tensor_copy(
                env.SC2[:, 0:512], bkw), deps=[bw] + env.sSC2.wdeps(),
                tag=f"l2wbc{call}_{e_i}_{t}")
            env.sB[4].read(cw)
            env.sSC2.wrote(cw)
            for och in range(NCHUNK):
                slb = och % 2
                W2S = env.E[:, 2048 + slb * 512 : 2048 + slb * 512 + 512]
                sW2 = env.sE[2]
                ldw2 = P.dma("gpsimd", lambda e, e_i=e_i, och=och, W2S=W2S:
                             e.dma_start(out=W2S, in_=env.l2fw2_h[e_i, och]),
                             deps=sW2.wdeps(), tag=f"l2w2ld{call}_{e_i}_{t}_{och}")
                sW2.wrote(ldw2)
                bk = env.bank(6 + och % 2)

                def w2grp(e, W2S=W2S, t=t, bk=bk, HE=HE):
                    r = None
                    for hcc in range(4):
                        r = e.matmul(bk, W2S[:, 128 * hcc : 128 * (hcc + 1)],
                                     HE[:, 2048 * hcc + 512 * t :
                                        2048 * hcc + 512 * t + 512],
                                     start=(hcc == 0), stop=(hcc == 3))
                    return r
                mm = P.op("tensor", w2grp, deps=[ldw2] + sHE.rdeps() +
                          env.sB[6 + och % 2].wdeps(),
                          tag=f"l2w2m{call}_{e_i}_{t}_{och}")
                env.sB[6 + och % 2].wrote(mm)
                sHE.read(mm)
                sW2.read(mm)
                ysl = slice(och * S + 512 * t, och * S + 512 * t + 512)
                def yfn(e, ysl=ysl, bk=bk):
                    e.tensor_tensor(env.SC1[:, 1024:1536], bk, env.SC2[:, 0:512],
                                    ALU.mult)
                    e.drain()
                    return e.tensor_tensor(YACC[:, ysl], YACC[:, ysl],
                                           env.SC1[:, 1024:1536], ALU.add)
                yop = P.op("vector", yfn, deps=[mm, cw] + env.sX.wdeps(),
                           tag=f"l2y{call}_{e_i}_{t}_{och}")
                env.sB[6 + och % 2].read(yop)
                env.sSC1.wrote(yop)
                env.sSC2.read(yop)
                env.sX.wrote(yop)

    for c in range(NCHUNK):
        ld = P.dma("sync", lambda e, c=c: e.dma_start(
            out=env.SC2[:, 0:2048], in_=env.xdump_h[:, c * S : (c + 1) * S]),
            deps=env.sSC2.wdeps() + [dx], tag=f"l2xrl{call}_{c}")
        env.sSC2.wrote(ld)
        ad = P.op("vector", lambda e, c=c: e.scalar_tensor_tensor(
            env.X[:, c * S : (c + 1) * S], env.X[:, c * S : (c + 1) * S], 0.5,
            env.SC2[:, 0:2048], ALU.mult, ALU.add),
            deps=[ld] + env.sX.wdeps(), tag=f"l2xadd{call}_{c}")
        env.sSC2.read(ad)
        env.sX.wrote(ad)


def emit_l2_attn(env):
    P = env.P
    nc = env.nc
    emit_ln(env)
    dx = P.dma("sync", lambda e: e.dma_start(out=env.xdump_h[:], in_=env.X[:]),
               deps=env.sX.rdeps(), tag="l2a_dumpx")
    env.sX.read(dx)
    xi_ops = []
    for c in range(NCHUNK):
        o = P.op("vector", lambda e, c=c: e.tensor_copy(
            env.X[:, c * S : (c + 1) * S], env.XLN[:, c * S : (c + 1) * S]),
            deps=[dx] + env.sX.wdeps() + env.sXLN.rdeps(), tag=f"l2a_xi{c}")
        env.sXLN.read(o)
        env.sX.writers.append(o)
        xi_ops.append(o)
    env.sX.wrote_many(env.sX.writers[-NCHUNK:])

    CF = env.C.ap().bitcast(F32)
    XLF = env.XLN.ap().bitcast(F32)
    GT = env.WTM
    sGT = env.sWTM

    for l in range(4):
        if l > 0:
            emit_ln(env)
        lw_ops = []
        for c in range(NCHUNK):
            o = P.op("vector", lambda e, c=c, l=l: e.tensor_scalar(
                env.XLN[:, c * S : (c + 1) * S], env.XLN[:, c * S : (c + 1) * S],
                env.spf("l2a_lnw", w=1, coff=l * 8 + c), None, ALU.mult),
                deps=env.sXLN.rdeps() + env.sXLN.readers, tag=f"l2a_g{l}_{c}")
            env.sXLN.writers.append(o)
            lw_ops.append(o)

        XMB = env.Dg[:, 2048:4096]
        XCB = env.Dg[:, 0:2048]
        XCF = CF[:, 0:2048]
        VTH = env.Dg[:, 4096:12288]
        QFB = env.C[:, 0:2048]
        KFB = env.C[:, 2048:4096]
        VFB = env.C[:, 4096:6144]
        sXM, sXC, sXCF = Slot("XM"), Slot("XC"), Slot("XCF")
        sVTH, sQF, sKF, sVF = Slot("VTH"), Slot("QF2"), Slot("KF2"), Slot("VF2")
        gz = P.op("vector", lambda e: e.memset(GT[0:8, :], 0.0),
                  deps=sGT.wdeps(), tag=f"l2a_gz{l}")
        sGT.wrote(gz)
        gt_adds = []
        vth_st = []
        xc_st = []
        qk_st = []
        for j in range(16):
            h = j // 4
            slb = j % 2
            WUP = env.E[:, slb * 1024 : slb * 1024 + 1024]
            sWU = env.sE[slb]
            ldu = P.dma("gpsimd", lambda e, l=l, j=j, WUP=WUP: e.dma_start(
                out=WUP, in_=env.l2up_h[l, j]), deps=sWU.wdeps(),
                tag=f"l2a_upld{l}_{j}")
            sWU.wrote(ldu)
            BQ = env.E[:, 2048 + slb * 384 : 2048 + slb * 384 + 128]
            BK = env.E[:, 2048 + slb * 384 + 128 : 2048 + slb * 384 + 256]
            BV = env.E[:, 2048 + slb * 384 + 256 : 2048 + slb * 384 + 384]
            sBD = env.sE[2]
            bd_guard = sBD.wdeps()
            ldb = []
            for W_, hsrc in ((BQ, env.l2bq_h), (BK, env.l2bk_h), (BV, env.l2bv_h)):
                o = P.dma("gpsimd", lambda e, W_=W_, hsrc=hsrc, l=l, j=j:
                          e.dma_start(out=W_, in_=hsrc[l, j]),
                          deps=bd_guard, tag=f"l2a_bd{l}_{j}")
                ldb.append(o)
            sBD.wrote_many(ldb)
            xm_ops = []
            for t in range(4):
                bk = env.bank(t % 2)

                def ugrp(e, WUP=WUP, t=t, bk=bk):
                    r = None
                    for c in range(NCHUNK):
                        r = e.matmul(bk, WUP[:, 128 * c : 128 * (c + 1)],
                                     env.XLN[:, c * S + 512 * t : c * S + 512 * t + 512],
                                     start=(c == 0), stop=(c == NCHUNK - 1))
                    return r
                mm = P.op("tensor", ugrp, deps=[ldu] + lw_ops +
                          env.sB[t % 2].wdeps(), tag=f"l2a_up{l}_{j}_{t}")
                env.sB[t % 2].wrote(mm)
                env.sXLN.read(mm)
                sWU.read(mm)
                sv = P.op("scalar", lambda e, t=t, bk=bk: e.copy(
                    XMB[:, 512 * t : 512 * (t + 1)], bk),
                    deps=[mm] + (sXM.wdeps() if t == 0 else []),
                    tag=f"l2a_xm{l}_{j}_{t}")
                env.sB[t % 2].read(sv)
                sXM.read(sv)
                xm_ops.append(sv)
            sXM.wrote_many(xm_ops)

            def convfn(e, l=l, j=j):
                cwc = l * 64 + j * 4
                e.tensor_scalar(XCF, XMB,
                                env.spf("l2a_cw", w=1, coff=cwc + 3),
                                env.spf("l2a_cb", w=1, coff=l * 16 + j),
                                ALU.mult, ALU.add)
                e.drain()
                e.scalar_tensor_tensor(XCF[:, 1:2048], XMB[:, 0:2047],
                                       env.spf("l2a_cw", w=1, coff=cwc + 2),
                                       XCF[:, 1:2048], ALU.mult, ALU.add)
                e.drain()
                e.scalar_tensor_tensor(XCF[:, 2:2048], XMB[:, 0:2046],
                                       env.spf("l2a_cw", w=1, coff=cwc + 1),
                                       XCF[:, 2:2048], ALU.mult, ALU.add)
                e.drain()
                return e.scalar_tensor_tensor(XCF[:, 3:2048], XMB[:, 0:2045],
                                              env.spf("l2a_cw", w=1, coff=cwc + 0),
                                              XCF[:, 3:2048], ALU.mult, ALU.add)
            cv = P.op("vector", convfn, deps=xm_ops + sXCF.wdeps(),
                      tag=f"l2a_cv{l}_{j}")
            sXM.read(cv)
            sXCF.wrote(cv)
            si = P.op("scalar", lambda e: e.activation(XCB, XCF, AF.Silu),
                      deps=[cv] + sXC.wdeps(), tag=f"l2a_si{l}_{j}")
            sXCF.read(si)
            sXC.wrote(si)
            stc = P.dma("sync", lambda e, j=j: e.dma_start(
                out=env.xcd_h[j], in_=XCB), deps=[si], tag=f"l2a_xcst{l}_{j}")
            sXC.read(stc)
            xc_st.append(stc)
            for which, BW, OUTB, sOUT, dh, sc in (
                    ("q", BQ, QFB, sQF, env.qd_h, "c_isq512"),
                    ("k", BK, KFB, sKF, env.kd_h, None),
                    ("v", BV, VFB, sVF, None, None)):
                if which in ("q", "k"):
                    SRC, sSRC, src_dep = XCB, sXC, [si]
                else:
                    SRC, sSRC, src_dep = XMB, sXM, xm_ops
                evs = []
                for t in range(4):
                    bk = env.bank(4 + t % 2)
                    mm = P.op("tensor", lambda e, BW=BW, SRC=SRC, t=t, bk=bk:
                              e.matmul(bk, BW, SRC[:, 512 * t : 512 * (t + 1)],
                                       start=True, stop=True),
                              deps=ldb + src_dep + env.sB[4 + t % 2].wdeps(),
                              tag=f"l2a_{which}m{l}_{j}_{t}")
                    env.sB[4 + t % 2].wrote(mm)
                    sBD.read(mm)
                    sSRC.read(mm)
                    kw = {}
                    if sc:
                        kw["scale"] = env.spf(sc)
                    sv = P.op("scalar", lambda e, OUTB=OUTB, t=t, bk=bk, kw=kw:
                              e.activation(OUTB[:, 512 * t : 512 * (t + 1)], bk,
                                           AF.Identity, **kw),
                              deps=[mm] + (sOUT.wdeps() if t == 0 else []),
                              tag=f"l2a_{which}e{l}_{j}_{t}")
                    env.sB[4 + t % 2].read(sv)
                    sOUT.read(sv)
                    evs.append(sv)
                sOUT.wrote_many(evs)
                if dh is not None:
                    std = P.dma("sync", lambda e, dh=dh, j=j, OUTB=OUTB:
                                e.dma_start(out=dh[j], in_=OUTB),
                                deps=evs, tag=f"l2a_{which}st{l}_{j}")
                    sOUT.read(std)
                    qk_st.append(std)
                blk = {"q": j, "k": 16 + j, "v": 32 + j}[which]
                for t in range(4):
                    bkg = env.bank(6)
                    mg = P.op("tensor", lambda e, blk=blk, l=l, t=t, bkg=bkg,
                              OUTB=OUTB: e.matmul(
                        bkg[0:8, 0:512],
                        env.spb("l2a_gw")[:, (l * 48 + blk) * 8 :
                                          (l * 48 + blk) * 8 + 8],
                        OUTB[:, 512 * t : 512 * (t + 1)],
                        start=True, stop=True),
                        deps=evs + [env.sSPB.writers] + env.sB[6].wdeps(),
                        tag=f"l2a_gm{which}{l}_{j}_{t}")
                    env.sB[6].wrote(mg)
                    sOUT.read(mg)
                    ga = P.op("vector", lambda e, t=t, bkg=bkg: e.tensor_tensor(
                        GT[0:8, 512 * t : 512 * (t + 1)],
                        GT[0:8, 512 * t : 512 * (t + 1)], bkg[0:8, 0:512],
                        ALU.add),
                        deps=[mg, gz] + sGT.wdeps(), tag=f"l2a_ga{which}{l}_{j}_{t}")
                    env.sB[6].read(ga)
                    sGT.wrote(ga)
                    gt_adds.append(ga)
            vte = []
            for tt in range(16):
                bk = env.bank(7)
                mm = P.op("tensor", lambda e, tt=tt, BV=BV, bk=bk: e.matmul(
                    bk[0:128, 0:128], XMB[:, 128 * tt : 128 * (tt + 1)], BV,
                    start=True, stop=True),
                    deps=ldb + xm_ops + env.sB[7].wdeps(),
                    tag=f"l2a_vt{l}_{j}_{tt}")
                env.sB[7].wrote(mm)
                sXM.read(mm)
                sBD.read(mm)
                dcol = 512 * tt + 128 * (j % 4)
                cvt = P.op("vector", lambda e, dcol=dcol, bk=bk: e.tensor_copy(
                    VTH[:, dcol : dcol + 128], bk[0:128, 0:128]),
                    deps=[mm] + (sVTH.wdeps() if j % 4 == 0 and tt == 0 else []),
                    tag=f"l2a_vte{l}_{j}_{tt}")
                env.sB[7].read(cvt)
                sVTH.read(cvt)
                vte.append(cvt)
            if j % 4 == 3:
                sVTH.wrote_many(sVTH.readers[-64:])
                stv = P.dma("sync", lambda e, h=h: e.dma_start(
                    out=env.vtd_h[h], in_=VTH), deps=sVTH.writers + vte,
                    tag=f"l2a_vtst{l}_{h}")
                sVTH.read(stv)
                vth_st.append(stv)
        z_st = []
        for zc in range(16):
            slb = zc % 2
            WUP = env.E[:, slb * 1024 : slb * 1024 + 1024]
            sWU = env.sE[slb]
            ldu = P.dma("gpsimd", lambda e, l=l, zc=zc, WUP=WUP: e.dma_start(
                out=WUP, in_=env.l2up_h[l, 16 + zc]), deps=sWU.wdeps(),
                tag=f"l2a_zld{l}_{zc}")
            sWU.wrote(ldu)
            z_ops = []
            for t in range(4):
                bk = env.bank(t % 2)

                def zgrp(e, WUP=WUP, t=t, bk=bk):
                    r = None
                    for c in range(NCHUNK):
                        r = e.matmul(bk, WUP[:, 128 * c : 128 * (c + 1)],
                                     env.XLN[:, c * S + 512 * t : c * S + 512 * t + 512],
                                     start=(c == 0), stop=(c == NCHUNK - 1))
                    return r
                mm = P.op("tensor", zgrp, deps=[ldu] + lw_ops +
                          env.sB[t % 2].wdeps(), tag=f"l2a_z{l}_{zc}_{t}")
                env.sB[t % 2].wrote(mm)
                env.sXLN.read(mm)
                sWU.read(mm)
                sv = P.op("scalar", lambda e, t=t, bk=bk: e.activation(
                    XCB[:, 512 * t : 512 * (t + 1)], bk, AF.Silu),
                    deps=[mm] + (sXC.wdeps() if t == 0 else []),
                    tag=f"l2a_ze{l}_{zc}_{t}")
                env.sB[t % 2].read(sv)
                sXC.read(sv)
                z_ops.append(sv)
            sXC.wrote_many(z_ops)
            stz = P.dma("sync", lambda e, zc=zc: e.dma_start(
                out=env.szd_h[zc], in_=XCB), deps=z_ops, tag=f"l2a_zst{l}_{zc}")
            sXC.read(stz)
            z_st.append(stz)

        sPB = Slot("PB")
        pb_guard = env.sXLN.readers[-1:] + env.sXLN.writers[-1:]
        gb_ = P.op("vector", lambda e, l=l: e.tensor_scalar(
            GT[0:8, :], GT[0:8, :], env.spf("l2a_gb", p=8, w=1, coff=l), None,
            ALU.add), deps=gt_adds + sGT.wdeps(), tag=f"l2a_gb{l}")
        sGT.wrote(gb_)
        A8 = XLF[0:8, 0:2048]
        CS8 = XLF[0:8, 2048:4096]
        EIG = XLF[0:4, 4096:6144]
        EIGR = XLF[32:33, 2048:4096]
        NRrow = XLF[32:33, 4096:4608]
        CSROW = XLF[0:1, 6144:8192]
        NR0 = env.SC1[0:1, 0:512]
        ABSR = env.SC1[0:1, 1040:1552]
        MUR2 = env.SC1[0:1, 528:1040]
        NEGC = env.SC1[:, 512:528]
        s1 = P.op("scalar", lambda e: e.activation(A8, GT[0:8, :], AF.Abs),
                  deps=[gb_] + sPB.wdeps() + pb_guard, tag=f"l2a_ab{l}")
        sGT.read(s1)
        sPB.wrote(s1)
        s2 = P.op("scalar", lambda e: e.activation(A8, A8, AF.Exp,
                  scale=env.spf("c_neg1", p=8)), deps=[s1], tag=f"l2a_ex{l}")
        sPB.wrote(s2)
        s3 = P.op("scalar", lambda e: e.activation(A8, A8, AF.Ln,
                  bias=env.spf("c_one", p=8)), deps=[s2], tag=f"l2a_l1p{l}")
        sPB.wrote(s3)

        def csfn(e):
            e.tensor_scalar(CS8, GT[0:8, :], 0.0, None, ALU.min)
            e.drain()
            e.tensor_tensor(CS8, CS8, A8, ALU.subtract)
            e.drain()
            sh = 1
            while sh < 2048:
                e.tensor_tensor(CS8[0:8, sh:2048], CS8[0:8, sh:2048],
                                CS8[0:8, 0 : 2048 - sh], ALU.add)
                e.drain()
            return None
        def csfn2(e):
            e.tensor_scalar(CS8, GT[0:8, :], 0.0, None, ALU.min)
            e.drain()
            e.tensor_tensor(CS8, CS8, A8, ALU.subtract)
            sh = 1
            while sh < 2048:
                e.drain()
                e.tensor_tensor(CS8[0:8, sh:2048], CS8[0:8, sh:2048],
                                CS8[0:8, 0 : 2048 - sh], ALU.add)
                sh *= 2
            return None
        v1 = P.op("vector", csfn2, deps=[s3, gb_] + sPB.wdeps(), tag=f"l2a_cs{l}")
        sGT.read(v1)
        sPB.wrote(v1)
        s4 = P.op("scalar", lambda e: e.activation(EIG, GT[0:4, :], AF.Exp,
                  scale=env.spf("c_neg1", p=4)), deps=[gb_, v1], tag=f"l2a_eig{l}")
        sGT.read(s4)
        sPB.wrote(s4)

        QH = env.C[:, 0:8192]
        sQH = env.sC[0]
        KH = env.Dg[:, 0:8192]
        sKH = env.sD[0]
        VTL = env.Dg[:, 8192:16384]
        sVT2 = env.sD[1]
        HT4 = XLF[:, 0:2048]
        htd_st = []
        for h in range(4):
            ldq = P.dma("sync", lambda e, h=h: e.dma_start(
                out=QH.rearrange("p (a b) -> p a b", a=4),
                in_=env.qd_h[4 * h : 4 * h + 4].transpose([1, 0, 2])),
                deps=sQH.wdeps() + qk_st, tag=f"l2b_ldq{l}_{h}")
            sQH.wrote(ldq)
            ldk = P.dma("sync", lambda e, h=h: e.dma_start(
                out=KH.rearrange("p (a b) -> p a b", a=4),
                in_=env.kd_h[4 * h : 4 * h + 4].transpose([1, 0, 2])),
                deps=sKH.wdeps() + qk_st, tag=f"l2b_ldk{l}_{h}")
            sKH.wrote(ldk)
            ldv = P.dma("sync", lambda e, h=h: e.dma_start(
                out=VTL, in_=env.vtd_h[h]), deps=sVT2.wdeps() + vth_st,
                tag=f"l2b_ldv{l}_{h}")
            sVT2.wrote(ldv)
            rows_ops = []
            for st4 in range(4):
                bk7 = env.bank(7)
                m1 = P.op("tensor", lambda e, h=h, st4=st4, bk7=bk7: e.matmul(
                    bk7[0:1, 0:512], env.spf("id4", p=4, w=1, coff=h),
                    EIG[0:4, 512 * st4 : 512 * (st4 + 1)], start=True, stop=True),
                    deps=[s4] + env.sB[7].wdeps(), tag=f"l2b_eig{l}_{h}_{st4}")
                env.sB[7].wrote(m1)
                sPB.read(m1)
                c1 = P.op("vector", lambda e, st4=st4, bk7=bk7: e.tensor_copy(
                    EIGR[0:1, 512 * st4 : 512 * (st4 + 1)], bk7[0:1, 0:512]),
                    deps=[m1] + (sPB.wdeps() if st4 == 0 and h == 0 else []),
                    tag=f"l2b_eigc{l}_{h}_{st4}")
                env.sB[7].read(c1)
                sPB.read(c1)
                rows_ops.append(c1)
                bk6x = env.bank(6)
                m2 = P.op("tensor", lambda e, h=h, st4=st4, bk6x=bk6x: e.matmul(
                    bk6x[0:1, 0:512], env.spf("id8", p=8, w=1, coff=4 + h),
                    CS8[0:8, 512 * st4 : 512 * (st4 + 1)], start=True, stop=True),
                    deps=[v1] + env.sB[6].wdeps(), tag=f"l2b_csr{l}_{h}_{st4}")
                env.sB[6].wrote(m2)
                sPB.read(m2)
                c2 = P.op("vector", lambda e, st4=st4, bk6x=bk6x: e.tensor_copy(
                    CSROW[0:1, 512 * st4 : 512 * (st4 + 1)], bk6x[0:1, 0:512]),
                    deps=[m2], tag=f"l2b_csrc{l}_{h}_{st4}")
                env.sB[6].read(c2)
                sPB.read(c2)
                rows_ops.append(c2)
            eig_done = rows_ops
            negc_ops = []
            for tc in range(16):
                bk6 = env.bank(6)
                tr = P.op("tensor", lambda e, tc=tc, bk6=bk6: e.transpose(
                    bk6[0:128, 0:1], CSROW[0:1, 128 * tc : 128 * (tc + 1)],
                    env.spf("c_one", p=1, w=1)),
                    deps=rows_ops + env.sB[6].wdeps(), tag=f"l2b_nt{l}_{h}_{tc}")
                env.sB[6].wrote(tr)
                sPB.read(tr)
                nc_ = P.op("scalar", lambda e, tc=tc, bk6=bk6: e.activation(
                    NEGC[:, tc : tc + 1], bk6[0:128, 0:1], AF.Identity,
                    scale=env.spf("c_neg1")), deps=[tr] + env.sSC1.wdeps(),
                    tag=f"l2b_nc{l}_{h}_{tc}")
                env.sB[6].read(nc_)
                env.sSC1.read(nc_)
                negc_ops.append(nc_)
            for st in range(4):
                bk7 = env.bank(7)
                mcs = P.op("tensor", lambda e, st=st, bk7=bk7: e.matmul(
                    bk7, env.spf("ones_row"),
                    CSROW[0:1, 512 * st : 512 * (st + 1)], start=True, stop=True),
                    deps=rows_ops + env.sB[7].wdeps(), tag=f"l2b_cb{l}_{h}_{st}")
                env.sB[7].wrote(mcs)
                sPB.read(mcs)
                CSB = env.SC1[:, 0:512]
                cc = P.op("vector", lambda e, bk7=bk7, CSB=CSB: e.tensor_copy(
                    CSB, bk7), deps=[mcs] + env.sSC1.wdeps(), tag=f"l2b_cc{l}_{h}_{st}")
                env.sB[7].read(cc)
                env.sSC1.wrote(cc)
                ntc = 4 * (st + 1)
                h_started = False
                rs_mm = None
                hmms = []
                for tc in range(ntc):
                    EB = env.SC2[:, 0:512]
                    se = P.op("scalar", lambda e, tc=tc, EB=EB, CSB=CSB:
                              e.activation(EB, CSB, AF.Exp,
                                           bias=NEGC[:, tc : tc + 1]),
                              deps=[cc] + negc_ops + env.sSC2.wdeps(),
                              tag=f"l2b_E{l}_{h}_{st}_{tc}")
                    env.sSC1.read(se)
                    env.sSC2.wrote(se)
                    mdep = [se]
                    if tc // 4 == st:
                        gm = P.op("gpsimd", lambda e, st=st, tc=tc, EB=EB:
                                  e.affine_select(EB, EB, pattern=[[1, 512]],
                                                  compare_op=ALU.is_ge, fill=0.0,
                                                  base=512 * st - 128 * tc,
                                                  channel_multiplier=-1),
                                  deps=[se], tag=f"l2b_M{l}_{h}_{st}_{tc}")
                        env.sSC2.wrote(gm)
                        mdep = [gm]
                    bkq = env.bank(tc % 2)

                    def qkgrp(e, h=h, st=st, tc=tc, bkq=bkq):
                        r = None
                        for dc in range(4):
                            r = e.matmul(
                                bkq, KH[:, 2048 * dc + 128 * tc :
                                        2048 * dc + 128 * tc + 128],
                                QH[:, 2048 * dc + 512 * st :
                                      2048 * dc + 512 * st + 512],
                                start=(dc == 0), stop=(dc == 3))
                        return r
                    mq = P.op("tensor", qkgrp, deps=[ldq, ldk] +
                              env.sB[tc % 2].wdeps(), tag=f"l2b_qk{l}_{h}_{st}_{tc}")
                    env.sB[tc % 2].wrote(mq)
                    sQH.read(mq)
                    sKH.read(mq)
                    CBb = env.SC2.ap().bitcast(BF16)[:, 2048:2560]
                    vc = P.op("vector", lambda e, bkq=bkq, EB=EB, CBb=CBb:
                              e.tensor_tensor(CBb, bkq, EB, ALU.mult),
                              deps=[mq] + mdep, tag=f"l2b_C{l}_{h}_{st}_{tc}")
                    env.sB[tc % 2].read(vc)
                    env.sSC2.wrote(vc)
                    rs_mm = P.op("tensor", lambda e, CBb=CBb, st_=(tc == 0),
                                 sp_=(tc == ntc - 1):
                                 e.matmul(env.bank(6)[0:1, :],
                                          env.spb("ones_colb"), CBb,
                                          start=st_, stop=sp_),
                                 deps=[vc] + (env.sB[6].wdeps() if tc == 0 else []),
                                 tag=f"l2b_rs{l}_{h}_{st}_{tc}")
                    if tc == 0:
                        env.sB[6].wrote(rs_mm)
                    env.sSC2.read(rs_mm)
                    for dvc in range(4):
                        hm = P.op("tensor", lambda e, tc=tc, dvc=dvc, CBb=CBb,
                                  sp_=(tc == ntc - 1):
                                  e.matmul(env.bank(2 + dvc),
                                           VTL[:, 512 * tc + 128 * dvc :
                                               512 * tc + 128 * dvc + 128],
                                           CBb, start=(tc == 0),
                                           stop=sp_),
                                  deps=[vc, ldv] +
                                  (env.sB[2 + dvc].wdeps() if tc == 0 else []),
                                  tag=f"l2b_h{l}_{h}_{st}_{tc}_{dvc}")
                        if tc == 0:
                            env.sB[2 + dvc].wrote(hm)
                        sVT2.read(hm)
                        env.sSC2.read(hm)
                        hmms.append(hm)
                def nfn(e, st=st):
                    e.tensor_scalar(NRrow, env.bank(6)[0:1, :], -1.0, None,
                                    ALU.mult)
                    e.drain()
                    e.tensor_tensor(NRrow, NRrow, env.bank(6)[0:1, :], ALU.max)
                    e.drain()
                    e.tensor_tensor(NRrow, NRrow,
                                    EIGR[0:1, 512 * st : 512 * (st + 1)], ALU.max)
                    e.drain()
                    e.tensor_scalar(NRrow, NRrow, 1e-6, None, ALU.add)
                    e.drain()
                    e.reciprocal(NRrow, NRrow)
                    e.drain()
                    return e.tensor_copy(NR0, NRrow)
                vn = P.op("vector", nfn, deps=[rs_mm] + eig_done +
                          env.sSC1.wdeps(), tag=f"l2b_n{l}_{h}_{st}")
                env.sB[6].read(vn)
                sPB.read(vn)
                env.sSC1.wrote(vn)
                bk6b = env.bank(6)
                mnb = P.op("tensor", lambda e, bk6b=bk6b: e.matmul(
                    bk6b, env.spf("ones_row"), NR0, start=True, stop=True),
                    deps=[vn] + env.sB[6].wdeps(), tag=f"l2b_nb{l}_{h}_{st}")
                env.sB[6].wrote(mnb)
                env.sSC1.read(mnb)
                NRB = env.SC2[:, 0:512]
                cnb = P.op("vector", lambda e, bk6b=bk6b, NRB=NRB: e.tensor_copy(
                    NRB, bk6b), deps=[mnb] + env.sSC2.wdeps(),
                    tag=f"l2b_nbc{l}_{h}_{st}")
                env.sB[6].read(cnb)
                env.sSC2.wrote(cnb)
                ht_ops = []
                for dvc in range(4):
                    o = P.op("vector", lambda e, dvc=dvc, NRB=NRB: e.tensor_tensor(
                        HT4[:, 512 * dvc : 512 * (dvc + 1)],
                        env.bank(2 + dvc), NRB, ALU.mult),
                        deps=hmms + [cnb] + (sPB.wdeps() if dvc == 0 else []),
                        tag=f"l2b_he{l}_{h}_{st}_{dvc}")
                    env.sB[2 + dvc].read(o)
                    env.sSC2.read(o)
                    sPB.read(o)
                    ht_ops.append(o)
                bk0 = env.bank(0)
                ms_ = P.op("tensor", lambda e, bk0=bk0: [
                    e.matmul(bk0[0:1, :], env.spf("ones_col"),
                             HT4[:, 512 * dc : 512 * (dc + 1)],
                             start=(dc == 0), stop=(dc == 3)) for dc in range(4)][-1],
                    deps=ht_ops + env.sB[0].wdeps(), tag=f"l2b_os{l}_{h}_{st}")
                env.sB[0].wrote(ms_)
                sPB.read(ms_)
                sq_ops = []
                for dc in range(4):
                    sq = P.op("scalar", lambda e, dc=dc: e.activation(
                        env.SC2[:, 512:1024], HT4[:, 512 * dc : 512 * (dc + 1)],
                        AF.Square), deps=ht_ops + env.sSC2.wdeps(),
                        tag=f"l2b_sq{l}_{h}_{st}_{dc}")
                    env.sSC2.wrote(sq)
                    sPB.read(sq)
                    mq_ = P.op("tensor", lambda e, dc=dc: e.matmul(
                        env.bank(1)[0:1, :], env.spf("ones_col"),
                        env.SC2[:, 512:1024], start=(dc == 0), stop=(dc == 3)),
                        deps=[sq] + (env.sB[1].wdeps() if dc == 0 else []),
                        tag=f"l2b_qm{l}_{h}_{st}_{dc}")
                    if dc == 0:
                        env.sB[1].wrote(mq_)
                    env.sSC2.read(mq_)
                    sq_ops.append(mq_)
                def statfn(e):
                    e.tensor_scalar(MUR2, env.bank(0)[0:1, :], 1.0 / 512, None,
                                    ALU.mult)
                    e.tensor_scalar(ABSR, env.bank(1)[0:1, :], 1.0 / 512, None,
                                    ALU.mult)
                    e.drain()
                    e.tensor_tensor(env.SC1[0:1, 0:512], MUR2, MUR2, ALU.mult)
                    e.drain()
                    e.tensor_tensor(ABSR, ABSR, env.SC1[0:1, 0:512],
                                    ALU.subtract)
                    return None
                vs_ = P.op("vector", statfn, deps=[ms_, sq_ops[-1]] +
                           env.sSC1.wdeps(), tag=f"l2b_st{l}_{h}_{st}")
                env.sB[0].read(vs_)
                env.sB[1].read(vs_)
                env.sSC1.wrote(vs_)
                sPB.read(vs_)
                ssd = P.op("scalar", lambda e: e.activation(
                    env.SC1[0:1, 0:512], ABSR, AF.Sqrt,
                    bias=env.spf("c_epsln", p=1)), deps=[vs_], tag=f"l2b_sd{l}_{h}_{st}")
                env.sSC1.wrote(ssd)
                vrc = P.op("vector", lambda e: e.reciprocal(
                    ABSR, env.SC1[0:1, 0:512]), deps=[ssd], tag=f"l2b_rc{l}_{h}_{st}")
                env.sSC1.wrote(vrc)
                bmu = P.op("tensor", lambda e: e.matmul(
                    env.bank(0), env.spf("ones_row"), MUR2, start=True, stop=True),
                    deps=[vrc, vs_] + env.sB[0].wdeps(), tag=f"l2b_bm{l}_{h}_{st}")
                env.sB[0].wrote(bmu)
                sPB.read(bmu)
                brs = P.op("tensor", lambda e: e.matmul(
                    env.bank(1), env.spf("ones_row"), ABSR, start=True, stop=True),
                    deps=[vrc] + env.sB[1].wdeps(), tag=f"l2b_br{l}_{h}_{st}")
                env.sB[1].wrote(brs)
                sPB.read(brs)
                for dvc in range(4):
                    TMPV = env.SC2[:, 512:1024]
                    HTBb = env.SC2.ap().bitcast(BF16)[:, 3072:3584]
                    def onfn(e, dvc=dvc, l=l, h=h, TMPV=TMPV, HTBb=HTBb):
                        e.tensor_tensor(TMPV, HT4[:, 512 * dvc : 512 * (dvc + 1)],
                                        env.bank(0), ALU.subtract)
                        e.drain()
                        e.tensor_tensor(TMPV, TMPV, env.bank(1), ALU.mult)
                        e.drain()
                        return e.tensor_scalar(
                            HTBb, TMPV,
                            env.spf("l2a_onw", w=1, coff=l * 16 + 4 * h + dvc),
                            None, ALU.mult)
                    ev = P.op("vector", onfn, deps=[bmu, brs] + ht_ops +
                              env.sSC2.wdeps(), tag=f"l2b_on{l}_{h}_{st}_{dvc}")
                    env.sB[0].read(ev)
                    env.sB[1].read(ev)
                    env.sSC2.wrote(ev)
                    sPB.read(ev)
                    sth = P.dma("sync", lambda e, l=l, h=h, st=st, dvc=dvc,
                                HTBb=HTBb: e.dma_start(
                        out=env.htd_h[4 * h + dvc][:, 512 * st : 512 * (st + 1)],
                        in_=HTBb), deps=[ev], tag=f"l2b_hst{l}_{h}_{st}_{dvc}")
                    env.sSC2.read(sth)
                    htd_st.append(sth)

        XCL = env.Dg[:, 0:8192]
        SZL = env.Dg[:, 8192:16384]
        OUTB = env.C[:, 0:8192]
        sXCL, sSZL = env.sD[0], env.sD[1]
        sOUTB = env.sC[0]
        for t in range(4):
            l1 = P.dma("sync", lambda e, t=t: e.dma_start(
                out=XCL.rearrange("p (a b) -> p a b", a=16),
                in_=env.xcd_h[:, :, 512 * t : 512 * (t + 1)].transpose([1, 0, 2])),
                deps=sXCL.wdeps() + xc_st, tag=f"l2c_xc{l}_{t}")
            sXCL.wrote(l1)
            l2_ = P.dma("sync", lambda e, t=t: e.dma_start(
                out=SZL.rearrange("p (a b) -> p a b", a=16),
                in_=env.szd_h[:, :, 512 * t : 512 * (t + 1)].transpose([1, 0, 2])),
                deps=sSZL.wdeps() + z_st, tag=f"l2c_sz{l}_{t}")
            sSZL.wrote(l2_)
            l3_ = P.dma("sync", lambda e, t=t: e.dma_start(
                out=OUTB.rearrange("p (a b) -> p a b", a=16),
                in_=env.htd_h[:, :, 512 * t : 512 * (t + 1)].transpose([1, 0, 2])),
                deps=sOUTB.wdeps() + htd_st, tag=f"l2c_ht{l}_{t}")
            sOUTB.wrote(l3_)
            ob_ops = []
            for j in range(16):
                def obfn(e, j=j, l=l):
                    e.scalar_tensor_tensor(
                        env.SC2[:, 0:512], XCL[:, 512 * j : 512 * (j + 1)],
                        env.spf("l2a_skip", w=1, coff=l * 16 + j),
                        OUTB[:, 512 * j : 512 * (j + 1)], ALU.mult, ALU.add)
                    e.drain()
                    return e.tensor_tensor(OUTB[:, 512 * j : 512 * (j + 1)],
                                           env.SC2[:, 0:512],
                                           SZL[:, 512 * j : 512 * (j + 1)],
                                           ALU.mult)
                o = P.op("vector", obfn, deps=[l1, l2_, l3_] + env.sSC2.wdeps(),
                         tag=f"l2c_ob{l}_{t}_{j}")
                sXCL.read(o)
                sSZL.read(o)
                sOUTB.read(o)
                env.sSC2.wrote(o)
                ob_ops.append(o)
            sOUTB.wrote_many(ob_ops)
            for oc in range(NCHUNK):
                slb = oc % 2
                DNW = env.E[:, slb * 2048 : slb * 2048 + 2048]
                dnslots = [env.sE[0], env.sE[1]] if slb == 0 else [env.sE[2]]
                ldd = P.dma("gpsimd", lambda e, l=l, oc=oc, DNW=DNW: e.dma_start(
                    out=DNW, in_=env.l2dn_h[l, oc]),
                    deps=[d for s_ in dnslots for d in s_.wdeps()],
                    tag=f"l2c_dn{l}_{t}_{oc}")
                for s_ in dnslots:
                    s_.wrote(ldd)
                sDN = dnslots[0]
                bk = env.bank(2 + oc % 2)

                def dgrp(e, DNW=DNW, bk=bk):
                    r = None
                    for j in range(16):
                        r = e.matmul(bk, DNW[:, 128 * j : 128 * (j + 1)],
                                     OUTB[:, 512 * j : 512 * (j + 1)],
                                     start=(j == 0), stop=(j == 15))
                    return r
                mm = P.op("tensor", dgrp, deps=[ldd] + ob_ops +
                          env.sB[2 + oc % 2].wdeps(), tag=f"l2c_dm{l}_{t}_{oc}")
                env.sB[2 + oc % 2].wrote(mm)
                sOUTB.read(mm)
                for s_ in dnslots:
                    s_.read(mm)
                xsl = slice(oc * S + 512 * t, oc * S + 512 * t + 512)
                vx = P.op("vector", lambda e, xsl=xsl, bk=bk: e.tensor_tensor(
                    env.X[:, xsl], env.X[:, xsl], bk, ALU.add),
                    deps=[mm] + env.sX.wdeps() + env.sX.readers,
                    tag=f"l2c_x{l}_{t}_{oc}")
                env.sB[2 + oc % 2].read(vx)
                env.sX.wrote(vx)

    emit_ln(env)
    pw_ops = []
    for c in range(NCHUNK):
        o = P.op("vector", lambda e, c=c: e.tensor_scalar(
            env.XLN[:, c * S : (c + 1) * S], env.XLN[:, c * S : (c + 1) * S],
            env.spf("l2a_postw", w=1, coff=c), None, ALU.mult),
            deps=env.sXLN.rdeps() + env.sXLN.readers, tag=f"l2p_w{c}")
        env.sXLN.writers.append(o)
        pw_ops.append(o)
    for oc in range(NCHUNK):
        slb = oc % 2
        LW = env.E[:, slb * 1024 : slb * 1024 + 1024]
        sLW = env.sE[slb]
        ldl = P.dma("gpsimd", lambda e, oc=oc, LW=LW: e.dma_start(
            out=LW, in_=env.l2lat_h[oc]), deps=sLW.wdeps(), tag=f"l2p_ld{oc}")
        sLW.wrote(ldl)
        for t in range(4):
            bk = env.bank(t % 2)

            def lgrp(e, LW=LW, t=t, bk=bk):
                r = None
                for c in range(NCHUNK):
                    r = e.matmul(bk, LW[:, 128 * c : 128 * (c + 1)],
                                 env.XLN[:, c * S + 512 * t : c * S + 512 * t + 512],
                                 start=(c == 0), stop=(c == NCHUNK - 1))
                return r
            mm = P.op("tensor", lgrp, deps=[ldl] + pw_ops +
                      env.sB[t % 2].wdeps(), tag=f"l2p_m{oc}_{t}")
            env.sB[t % 2].wrote(mm)
            env.sXLN.read(mm)
            sLW.read(mm)
            sv = P.op("scalar", lambda e, bk=bk, oc=oc: e.activation(
                env.SC1[:, 0:512], bk, AF.Identity,
                bias=env.spf("l2a_latb", w=1, coff=oc)),
                deps=[mm] + env.sSC1.wdeps(), tag=f"l2p_b{oc}_{t}")
            env.sB[t % 2].read(sv)
            env.sSC1.wrote(sv)
            ld = P.dma("sync", lambda e, oc=oc, t=t: e.dma_start(
                out=env.SC2[:, 0:512],
                in_=env.xdump_h[:, oc * S + 512 * t : oc * S + 512 * t + 512]),
                deps=env.sSC2.wdeps() + [dx], tag=f"l2p_xr{oc}_{t}")
            env.sSC2.wrote(ld)
            xsl = slice(oc * S + 512 * t, oc * S + 512 * t + 512)
            vx = P.op("vector", lambda e, xsl=xsl: e.tensor_tensor(
                env.X[:, xsl], env.SC2[:, 0:512], env.SC1[:, 0:512], ALU.add),
                deps=[sv, ld] + env.sX.wdeps() + env.sX.readers,
                tag=f"l2p_x{oc}_{t}")
            env.sSC1.read(vx)
            env.sSC2.read(vx)
            env.sX.wrote(vx)


def emit_l3_ffn(env):
    P = env.P
    nc = env.nc
    call = _L3N[0]
    _L3N[0] += 1
    emit_ln(env)
    DgF = env.Dg.ap().bitcast(F32)

    if call == 0:
        PKI = env.Dg.ap().bitcast(I32)[:, 0:1024]
        ENC = env.Dg.ap().bitcast(I32)[:, 1024:2048]
        G1 = DgF[:, 2048:3072]
        G2 = DgF[:, 3072:4096]
        TBF = env.C[:, 0:1024]
        for ch in range(2):
            ld = P.dma("sync", lambda e, ch=ch: e.dma_start(
                out=PKI, in_=env.l3pk_h[128 * ch : 128 * (ch + 1), :]),
                deps=env.sD[0].wdeps() + env.sD[1].wdeps(), tag=f"l3u_ld{ch}")
            env.sD[0].wrote(ld)
            env.sD[1].wrote(ld)
            for d in range(4):
                def unp(e, d=d):
                    e.tensor_scalar(ENC, PKI, 2 * d, 3,
                                    ALU.logical_shift_right, ALU.bitwise_and)
                    e.drain()
                    e.tensor_scalar(G1, ENC, 1, None, ALU.is_ge)
                    e.drain()
                    e.tensor_scalar(G2, ENC, 2, None, ALU.is_ge)
                    e.drain()
                    e.tensor_tensor(G1, G1, G2, ALU.add)
                    e.drain()
                    return e.tensor_scalar(TBF, G1, 1.0, None, ALU.subtract)
                v = P.op("vector", unp, deps=[ld] + env.sC[0].wdeps() +
                         env.sD[0].wdeps()[:0], tag=f"l3u_{ch}_{d}")
                env.sD[0].read(v)
                env.sC[0].wrote(v)
                st = P.dma("sync", lambda e, ch=ch, d=d: e.dma_start(
                    out=env.l3wb_h.rearrange("(w d) o -> w d o", d=4)[
                        128 * ch : 128 * (ch + 1), d, :], in_=TBF),
                    deps=[v], tag=f"l3u_st{ch}_{d}")
                env.sC[0].read(st)
                env.tern3_st = getattr(env, "tern3_st", [])
                env.tern3_st.append(st)

    SROW = env.SC1[0:1, 0:2048]
    IROW = env.SC1[0:1, 2048:4096] if False else None
    IROW = env.MUR[0:1, 0:2048]
    srow_ops = []
    sg = env.sSC1.wdeps() + env.sMUR.wdeps()
    PSB = env.PS.ap().bitcast(BF16)
    for tt in range(16):
        bkt = PSB[:, 4096:4352]
        TMX = env.SC2[:, 0:16]
        tr_ops = []
        for c in range(NCHUNK):
            tr = P.op("tensor", lambda e, c=c, tt=tt, bkt=bkt: e.transpose(
                bkt[0:128, 0:128],
                env.XLN[:, c * S + 128 * tt : c * S + 128 * tt + 128],
                env.spb("identb")),
                deps=env.sXLN.rdeps() + env.sB[4].wdeps(), tag=f"l3q_t{tt}_{c}")
            env.sB[4].wrote(tr)
            env.sXLN.read(tr)
            rd = P.op("vector", lambda e, c=c, bkt=bkt, TMX=TMX: e.reduce_max(
                TMX[:, c : c + 1], bkt[0:128, 0:128], axis=AX.X,
                apply_absolute_value=True),
                deps=[tr] + (env.sSC2.wdeps() if c == 0 else []),
                tag=f"l3q_r{tt}_{c}")
            env.sB[4].read(rd)
            env.sSC2.read(rd)
            tr_ops.append(rd)
        env.sSC2.wrote_many(tr_ops)
        def sfn(e, TMX=TMX):
            e.reduce_max(TMX[:, 8:9], TMX[:, 0:8], axis=AX.X)
            e.drain()
            e.tensor_scalar(TMX[:, 8:9], TMX[:, 8:9], 1e-5, None, ALU.max)
            e.drain()
            e.tensor_scalar(TMX[:, 9:10], TMX[:, 8:9], 1.0 / 127.0, None,
                            ALU.mult)
            e.drain()
            return e.reciprocal(TMX[:, 10:11], TMX[:, 9:10])
        vs = P.op("vector", sfn, deps=tr_ops, tag=f"l3q_s{tt}")
        env.sSC2.wrote(vs)
        for which, coli, ROW in (("s", 10, SROW), ("i", 9, IROW)):
            bk5 = env.bank(5)
            tr2 = P.op("tensor", lambda e, coli=coli, TMX=TMX, bk5=bk5:
                       e.transpose(bk5[0:1, 0:128], TMX[:, coli : coli + 1],
                                   env.spf("ident")),
                       deps=[vs] + env.sB[5].wdeps(), tag=f"l3q_tr{which}{tt}")
            env.sB[5].wrote(tr2)
            env.sSC2.read(tr2)
            cp = P.op("vector", lambda e, tt=tt, ROW=ROW, bk5=bk5: e.tensor_copy(
                ROW[0:1, 128 * tt : 128 * (tt + 1)], bk5[0:1, 0:128]),
                deps=[tr2] + (sg if tt == 0 else []), tag=f"l3q_c{which}{tt}")
            env.sB[5].read(cp)
            (env.sSC1 if which == "s" else env.sMUR).read(cp)
            srow_ops.append(cp)
    env.sSC1.wrote_many([o for o in srow_ops])
    env.sMUR.wrote_many([o for o in srow_ops])
    wsf = P.op("vector", lambda e: e.tensor_scalar(
        IROW, IROW, env.spf("l3_ws", p=1), None, ALU.mult),
        deps=srow_ops, tag="l3q_ws")
    env.sMUR.wrote_many([wsf])

    q_ops = []
    for t in range(4):
        bks = env.bank(5)
        mbs = P.op("tensor", lambda e, t=t, bks=bks: e.matmul(
            bks, env.spf("ones_row"), SROW[0:1, 512 * t : 512 * (t + 1)],
            start=True, stop=True), deps=[wsf] + srow_ops + env.sB[5].wdeps(),
            tag=f"l3q_bs{t}")
        env.sB[5].wrote(mbs)
        env.sSC1.read(mbs)
        SB_ = env.SC2[:, 0:512]
        cbs = P.op("vector", lambda e, bks=bks, SB_=SB_: e.tensor_copy(SB_, bks),
                   deps=[mbs] + env.sSC2.wdeps(), tag=f"l3q_cb{t}")
        env.sB[5].read(cbs)
        env.sSC2.wrote(cbs)
        for c in range(NCHUNK):
            xsl = slice(c * S + 512 * t, c * S + 512 * t + 512)
            YB = env.SC2[:, 512:1024]
            DB = env.SC2[:, 1024:1536]
            MB = env.SC2[:, 1536:2048]
            eng = "vector"
            def qfn(e, xsl=xsl, SB_=SB_, YB=YB, DB=DB, MB=MB):
                e.tensor_tensor(YB, env.XLN[:, xsl], SB_, ALU.mult)
                e.drain()
                e.tensor_scalar(DB, YB, 128.5, None, ALU.add)
                for j in range(8, -1, -1):
                    e.drain()
                    e.tensor_scalar(MB, DB, float(2 ** j), None, ALU.is_ge)
                    e.drain()
                    e.scalar_tensor_tensor(DB, MB, -float(2 ** j), DB,
                                           ALU.mult, ALU.add)
                e.drain()
                e.tensor_scalar(YB, YB, 0.5, None, ALU.add)
                e.drain()
                e.tensor_tensor(YB, YB, DB, ALU.subtract)
                e.drain()
                e.tensor_scalar(YB, YB, 127.0, -128.0, ALU.min, ALU.max)
                e.drain()
                return e.tensor_copy(env.XLN[:, xsl], YB)
            o = P.op(eng, qfn, deps=[cbs] + env.sXLN.rdeps() +
                     env.sXLN.readers + env.sSC2.wdeps(), tag=f"l3q_q{t}_{c}")
            env.sSC2.read(o)
            env.sXLN.writers.append(o)
            q_ops.append(o)
    env.sXLN.wrote_many(q_ops)

    for oc in range(NCHUNK):
        slb = oc % 2
        WB = env.E[:, slb * 1024 : slb * 1024 + 1024]
        sWB = env.sE[slb]
        ldw = P.dma("sync", lambda e, oc=oc, WB=WB: e.dma_start(
            out=WB.rearrange("p (ic j) -> p ic j", ic=8),
            in_=env.l3wb_h.rearrange("(ic p) o -> p ic o", p=128)[
                :, :, 128 * oc : 128 * (oc + 1)]),
            deps=sWB.wdeps() + getattr(env, "tern3_st", []), tag=f"l3f_w{call}_{oc}")
        sWB.wrote(ldw)
        for t in range(4):
            bk = env.bank(t % 2)

            def mgrp(e, WB=WB, t=t, bk=bk):
                r = None
                for c in range(NCHUNK):
                    r = e.matmul(bk, WB[:, 128 * c : 128 * (c + 1)],
                                 env.XLN[:, c * S + 512 * t : c * S + 512 * t + 512],
                                 start=(c == 0), stop=(c == NCHUNK - 1))
                return r
            mm = P.op("tensor", mgrp, deps=[ldw] + q_ops +
                      env.sB[t % 2].wdeps(), tag=f"l3f_m{call}_{oc}_{t}")
            env.sB[t % 2].wrote(mm)
            env.sXLN.read(mm)
            sWB.read(mm)
            bki = env.bank(6)
            mbi = P.op("tensor", lambda e, t=t, bki=bki: e.matmul(
                bki, env.spf("ones_row"), IROW[0:1, 512 * t : 512 * (t + 1)],
                start=True, stop=True), deps=[wsf] + env.sB[6].wdeps(),
                tag=f"l3f_bi{call}_{oc}_{t}")
            env.sB[6].wrote(mbi)
            env.sMUR.read(mbi)
            TMP = env.SC1[:, 0:512] if False else env.SC2[:, 0:512]
            cpm = P.op("scalar", lambda e, bk=bk, TMP=TMP: e.activation(
                TMP, bk, AF.Identity),
                deps=[mm] + env.sSC2.wdeps(), tag=f"l3f_ev{call}_{oc}_{t}")
            env.sB[t % 2].read(cpm)
            env.sSC2.wrote(cpm)
            v1 = P.op("vector", lambda e, bki=bki, TMP=TMP:
                      e.tensor_tensor(TMP, TMP, bki, ALU.mult),
                      deps=[cpm, mbi], tag=f"l3f_v{call}_{oc}_{t}")
            env.sB[6].read(v1)
            env.sSC2.wrote(v1)
            sv = P.op("scalar", lambda e, oc=oc, TMP=TMP: e.activation(
                TMP, TMP, AF.Identity, bias=env.spf("l3_bias", w=1, coff=oc)),
                deps=[v1], tag=f"l3f_b{call}_{oc}_{t}")
            env.sSC2.wrote(sv)
            xsl = slice(oc * S + 512 * t, oc * S + 512 * t + 512)
            vx = P.op("vector", lambda e, xsl=xsl, TMP=TMP: e.scalar_tensor_tensor(
                env.X[:, xsl], TMP, 0.5, env.X[:, xsl], ALU.mult, ALU.add),
                deps=[sv] + env.sX.wdeps() + env.sX.readers,
                tag=f"l3f_x{call}_{oc}_{t}")
            env.sSC2.read(vx)
            env.sX.wrote(vx)


def emit_l3_attn(env):
    P = env.P
    nc = env.nc
    emit_ln(env)

    QF3 = env.C[:, 0:4096]
    KF3 = env.C[:, 4096:8192]
    VT3 = env.Dg[:, 0:2048]
    EB3 = env.Dg[:, 2048:3072]
    sQ3, sK3, sV3, sE3 = env.sC[0], env.sC[1], env.sD[0], env.sD[1]
    WB0 = env.E[:, 0:512]
    WB1 = env.E[:, 1024:1536]
    WBV = env.E[:, 2048:2560]
    sWB = [env.sE[0], env.sE[1]]

    for h in range(16):
        ldws = []
        for bi, blk in enumerate((h, 16 + h, 32 + h)):
            WB = (WB0, WB1, WBV)[bi]
            sW = (env.sE[0], env.sE[1], env.sE[2])[bi]
            o = P.dma("gpsimd", lambda e, blk=blk, WB=WB: e.dma_start(
                out=WB, in_=env.l3w_h[blk]), deps=sW.wdeps(),
                tag=f"l3a_w{h}_{blk}")
            sW.wrote(o)
            ldws.append(o)
        pops = {"q": [], "k": []}
        for wi, which in ((0, "q"), (1, "k")):
            WB = (WB0, WB1)[wi]
            OUT = (QF3, KF3)[wi]
            sOUT = (sQ3, sK3)[wi]
            og = sOUT.wdeps()
            for t in range(4):
                bk = env.bank(t % 2)

                def pgrp(e, WB=WB, t=t, bk=bk):
                    r = None
                    for c in range(NCHUNK):
                        r = e.matmul(bk[0:64, :], WB[:, 64 * c : 64 * (c + 1)],
                                     env.XLN[:, c * S + 512 * t : c * S + 512 * t + 512],
                                     start=(c == 0), stop=(c == NCHUNK - 1))
                    return r
                mm = P.op("tensor", pgrp, deps=[ldws[wi]] + env.sXLN.rdeps() +
                          env.sB[t % 2].wdeps(), tag=f"l3a_{which}m{h}_{t}")
                env.sB[t % 2].wrote(mm)
                env.sXLN.read(mm)
                (env.sE[0], env.sE[1])[wi].read(mm)
                sv = P.op("scalar", lambda e, OUT=OUT, t=t, bk=bk: e.copy(
                    OUT[0:64, 512 * t : 512 * (t + 1)], bk[0:64, :]),
                    deps=[mm] + (og if t == 0 else []), tag=f"l3a_{which}e{h}_{t}")
                env.sB[t % 2].read(sv)
                sOUT.read(sv)
                pops[which].append(sv)
            sOUT.wrote_many(pops[which])
        vops = []
        vg = sV3.wdeps()
        for tt in range(16):
            bk = env.bank(4)

            def vgrp(e, tt=tt, bk=bk):
                r = None
                for c in range(NCHUNK):
                    r = e.matmul(bk[0:128, 0:64],
                                 env.XLN[:, c * S + 128 * tt : c * S + 128 * tt + 128],
                                 env.E[:, 2048 + 64 * c : 2048 + 64 * c + 64],
                                 start=(c == 0), stop=(c == NCHUNK - 1))
                return r
            mm = P.op("tensor", vgrp, deps=[ldws[2]] + env.sXLN.rdeps() +
                      env.sB[4].wdeps(), tag=f"l3a_vm{h}_{tt}")
            env.sB[4].wrote(mm)
            env.sXLN.read(mm)
            env.sE[2].read(mm)
            cv = P.op("vector", lambda e, tt=tt, bk=bk: e.tensor_copy(
                VT3[:, 64 * tt : 64 * (tt + 1)], bk[0:128, 0:64]),
                deps=[mm] + (vg if tt == 0 else []), tag=f"l3a_ve{h}_{tt}")
            env.sB[4].read(cv)
            sV3.read(cv)
            vops.append(cv)
        sV3.wrote_many(vops)
        QN = env.SC1[0:1, 0:2048]
        KN = env.SC1[0:1, 2048:4096] if False else env.MUR[0:1, 0:2048]
        nb_ops = []
        for wi, (SRC, ROW, sROW) in enumerate(((QF3, QN, env.sSC1),
                                               (KF3, KN, env.sMUR))):
            for t in range(4):
                sq = P.op("scalar", lambda e, SRC=SRC, t=t: e.activation(
                    env.SC2[0:64, 0:512], SRC[0:64, 512 * t : 512 * (t + 1)],
                    AF.Square), deps=(sQ3 if wi == 0 else sK3).rdeps() +
                    env.sSC2.wdeps(), tag=f"l3a_n2{h}_{wi}_{t}")
                env.sSC2.wrote(sq)
                (sQ3 if wi == 0 else sK3).read(sq)
                bk5 = env.bank(5)
                mn = P.op("tensor", lambda e, bk5=bk5: e.matmul(
                    bk5[0:1, :], env.spf("ones_col", p=64),
                    env.SC2[0:64, 0:512], start=True, stop=True),
                    deps=[sq] + env.sB[5].wdeps(), tag=f"l3a_nm{h}_{wi}_{t}")
                env.sB[5].wrote(mn)
                env.sSC2.read(mn)
                cn = P.op("vector", lambda e, ROW=ROW, t=t, bk5=bk5:
                          e.tensor_copy(ROW[0:1, 512 * t : 512 * (t + 1)],
                                        bk5[0:1, :]),
                          deps=[mn] + (sROW.wdeps() if t == 0 else []),
                          tag=f"l3a_nc{h}_{wi}_{t}")
                env.sB[5].read(cn)
                sROW.read(cn)
                nb_ops.append(cn)
        env.sSC1.wrote_many(nb_ops)
        env.sMUR.wrote_many(nb_ops)
        BB = env.SC2[0:1, 0:4]
        def bfn(e):
            e.reduce_max(BB[0:1, 0:1], QN, axis=AX.X)
            e.drain()
            e.reduce_max(BB[0:1, 1:2], KN, axis=AX.X)
            e.drain()
            return None
        vb = P.op("vector", bfn, deps=nb_ops + env.sSC2.wdeps(), tag=f"l3a_b{h}")
        env.sSC1.read(vb)
        env.sMUR.read(vb)
        env.sSC2.wrote(vb)
        sb1 = P.op("scalar", lambda e: e.activation(BB[0:1, 0:2], BB[0:1, 0:2],
                   AF.Sqrt), deps=[vb], tag=f"l3a_bs{h}")
        env.sSC2.wrote(sb1)
        def bfn2(e):
            e.tensor_tensor(BB[0:1, 2:3], BB[0:1, 0:1], BB[0:1, 1:2], ALU.mult)
            e.drain()
            return e.tensor_scalar(BB[0:1, 2:3], BB[0:1, 2:3], -0.125, None,
                                   ALU.mult)
        vb2 = P.op("vector", bfn2, deps=[sb1], tag=f"l3a_b2{h}")
        env.sSC2.wrote(vb2)
        bk5 = env.bank(5)
        mbb = P.op("tensor", lambda e, bk5=bk5: e.matmul(
            bk5[0:128, 0:1], env.spf("ones_row"), BB[0:1, 2:3],
            start=True, stop=True), deps=[vb2] + env.sB[5].wdeps(),
            tag=f"l3a_bb{h}")
        env.sB[5].wrote(mbb)
        env.sSC2.read(mbb)
        NB = env.SC2[:, 512:513]
        cbb = P.op("vector", lambda e, NB=NB, bk5=bk5: e.tensor_copy(
            NB, bk5[0:128, 0:1]), deps=[mbb] + env.sSC2.wdeps(), tag=f"l3a_be{h}")
        env.sB[5].read(cbb)
        env.sSC2.wrote(cbb)
        for qt in range(4):
            den_mm = None
            hmms = []
            for kc in range(16):
                bkq = env.bank(kc % 2)
                mq = P.op("tensor", lambda e, kc=kc, qt=qt, bkq=bkq: e.matmul(
                    bkq, KF3[0:64, 128 * kc : 128 * (kc + 1)],
                    QF3[0:64, 512 * qt : 512 * (qt + 1)], start=True, stop=True),
                    deps=sK3.rdeps() + sQ3.rdeps() + env.sB[kc % 2].wdeps(),
                    tag=f"l3a_s{h}_{qt}_{kc}")
                env.sB[kc % 2].wrote(mq)
                sK3.read(mq)
                sQ3.read(mq)
                EBt = env.Dg.ap().bitcast(BF16)[:, 4096 + 1024 * (kc % 2) :
                                                4096 + 1024 * (kc % 2) + 512]
                se = P.op("scalar", lambda e, bkq=bkq, EBt=EBt, NB=NB:
                          e.activation(EBt, bkq, AF.Exp, bias=NB,
                                       scale=env.spf("c_inv8")),
                          deps=[mq, cbb] + sE3.wdeps(), tag=f"l3a_E{h}_{qt}_{kc}")
                env.sB[kc % 2].read(se)
                env.sSC2.read(se)
                sE3.read(se)
                den_mm = P.op("tensor", lambda e, EBt=EBt, st_=(kc == 0),
                              sp_=(kc == 15): e.matmul(
                    env.bank(6)[0:1, :], env.spb("ones_colb"), EBt,
                    start=st_, stop=sp_),
                    deps=[se] + (env.sB[6].wdeps() if kc == 0 else []),
                    tag=f"l3a_d{h}_{qt}_{kc}")
                if kc == 0:
                    env.sB[6].wrote(den_mm)
                sE3.read(den_mm)
                hm = P.op("tensor", lambda e, kc=kc, EBt=EBt, st_=(kc == 0),
                          sp_=(kc == 15): e.matmul(
                    env.bank(7)[0:64, :], VT3[:, 64 * kc : 64 * (kc + 1)], EBt,
                    start=st_, stop=sp_),
                    deps=[se] + sV3.rdeps() +
                    (env.sB[7].wdeps() if kc == 0 else []),
                    tag=f"l3a_h{h}_{qt}_{kc}")
                if kc == 0:
                    env.sB[7].wrote(hm)
                sV3.read(hm)
                sE3.read(hm)
                hmms.append(hm)
            RR = env.SC1[0:1, 0:512]
            def rfn(e, RR=RR):
                e.tensor_copy(RR, env.bank(6)[0:1, :])
                e.drain()
                return e.reciprocal(RR, RR)
            vr = P.op("vector", rfn, deps=[den_mm] + env.sSC1.wdeps(),
                      tag=f"l3a_r{h}_{qt}")
            env.sB[6].read(vr)
            env.sSC1.wrote(vr)
            bk6 = env.bank(6)
            mrb = P.op("tensor", lambda e, bk6=bk6, RR=RR: e.matmul(
                bk6[0:64, :], env.spf("ones_row", w=64), RR, start=True,
                stop=True), deps=[vr] + env.sB[6].wdeps(), tag=f"l3a_rb{h}_{qt}")
            env.sB[6].wrote(mrb)
            env.sSC1.read(mrb)
            TMP = env.SC2[0:64, 0:512]
            cph = P.op("scalar", lambda e, TMP=TMP: e.activation(
                TMP, env.bank(7)[0:64, :], AF.Identity),
                deps=hmms + env.sSC2.wdeps(), tag=f"l3a_hev{h}_{qt}")
            env.sB[7].read(cph)
            env.sSC2.wrote(cph)
            vh = P.op("vector", lambda e, TMP=TMP, bk6=bk6: e.tensor_tensor(
                TMP, TMP, bk6[0:64, :], ALU.mult),
                deps=[cph, mrb], tag=f"l3a_o{h}_{qt}")
            env.sB[6].read(vh)
            env.sSC2.wrote(vh)
            bp = 64 * (h % 2)
            xsl = slice((h // 2) * S + 512 * qt, (h // 2) * S + 512 * qt + 512)
            if bp:
                TMP2 = env.SC2[64:128, 512:1024]
                c64 = P.op("vector", lambda e, TMP=TMP, TMP2=TMP2:
                           e.tensor_copy(TMP2, TMP),
                           deps=[vh], tag=f"l3a_c64{h}_{qt}")
                env.sSC2.wrote(c64)
                TMP, vh = TMP2, c64
            vx = P.op("vector", lambda e, bp=bp, xsl=xsl, TMP=TMP:
                      e.tensor_tensor(env.X[bp : bp + 64, xsl],
                                      env.X[bp : bp + 64, xsl], TMP, ALU.add),
                      deps=[vh] + env.sX.wdeps() + env.sX.readers,
                      tag=f"l3a_x{h}_{qt}")
            env.sSC2.read(vx)
            env.sX.wrote(vx)


_SPF = None
_SPB = None
_SPF_SHAPE = None
_SPB_SHAPE = None
_NC_CACHE = {}
LAST_RESULTS = None
DEBUG_TAPS = False


def kernel(**inputs):
    global _SPF, _SPB, _SPF_SHAPE, _SPB_SHAPE
    _SPF, _SPB = _build_packs(inputs)
    _SPF_SHAPE = (128, _SPF.cols)
    _SPB_SHAPE = (128, _SPB.cols)

    if N_STAGES not in _NC_CACHE:
        _NC_CACHE[N_STAGES] = _build(N_STAGES)
    nc = _NC_CACHE[N_STAGES]

    x = np.asarray(inputs["L0_attn_x"], dtype=np.float32)
    spf_host = _SPF.host()
    spb_host = _SPB.host()
    w1t = np.asarray(inputs["L0_ffn_W1"]).transpose(0, 2, 1)
    l0w1t = _f32(w1t.reshape(8, 8, 128, 32, 128).transpose(0, 3, 2, 1, 4)
                 .reshape(8, 32, 128, 1024))
    w2t = np.asarray(inputs["L0_ffn_W2"]).transpose(0, 2, 1)
    l0w2t = _f32(w2t.reshape(8, 32, 128, 8, 128).transpose(0, 3, 2, 1, 4)
                 .reshape(8, 8, 128, 4096))

    def tile_sq(w):
        return _f32(np.asarray(w).reshape(8, 128, 8, 128).transpose(2, 1, 0, 3)
                    .reshape(8, 128, 1024))
    l1wq = tile_sq(inputs["L1_attn_Wq"])
    l1wk = tile_sq(inputs["L1_attn_Wk"])
    l1wv = tile_sq(inputs["L1_attn_Wv"])
    l1wo = tile_sq(inputs["L1_attn_Wo"])
    w1t = np.asarray(inputs["L1_ffn_W1"]).T
    l1w1t = _f32(w1t.reshape(8, 128, 32, 128).transpose(2, 1, 0, 3)
                 .reshape(32, 128, 1024))
    w2t = np.asarray(inputs["L1_ffn_W2"]).T
    l1w2t = _f32(w2t.reshape(32, 128, 8, 128).transpose(2, 1, 0, 3)
                 .reshape(8, 128, 4096))
    upw = np.asarray(inputs["L2_attn_up_w"]).transpose(0, 2, 1)
    l2up = _f32(upw.reshape(4, 8, 128, 32, 128).transpose(0, 3, 2, 1, 4)
                .reshape(4, 32, 128, 1024))
    dnw = np.asarray(inputs["L2_attn_down_w"]).transpose(0, 2, 1)
    l2dn = _f32(dnw.reshape(4, 16, 128, 8, 128).transpose(0, 3, 2, 1, 4)
                .reshape(4, 8, 128, 2048))
    latw = np.asarray(inputs["L2_attn_lat_w"]).T
    l2lat = _f32(latw.reshape(8, 128, 8, 128).transpose(2, 1, 0, 3)
                 .reshape(8, 128, 1024))

    def blockdiag(w):
        w = np.asarray(w, dtype=np.float32)
        B = np.zeros((4, 16, 128, 128), np.float32)
        for b in range(32):
            B[:, :, 4 * b : 4 * b + 4, 4 * b : 4 * b + 4] = \
                w.reshape(4, 16, 32, 4, 4)[:, :, b].transpose(0, 1, 3, 2)
        return np.ascontiguousarray(B)
    l2bq = blockdiag(inputs["L2_attn_q_w"])
    l2bk = blockdiag(inputs["L2_attn_k_w"])
    l2bv = blockdiag(inputs["L2_attn_v_w"])
    w1 = np.asarray(inputs["L2_ffn_w1"]).transpose(0, 2, 1)
    l2fw1 = _f32(w1.reshape(16, 8, 128, 4, 128).transpose(0, 3, 2, 1, 4)
                 .reshape(16, 4, 128, 1024))
    w2 = np.asarray(inputs["L2_ffn_w2"]).transpose(0, 2, 1)
    l2fw2 = _f32(w2.reshape(16, 4, 128, 8, 128).transpose(0, 3, 2, 1, 4)
                 .reshape(16, 8, 128, 512))
    sf1 = np.asarray(inputs["L2_ffn_sfc1"]).T
    l2sf1 = _f32(sf1.reshape(8, 128, 16, 128).transpose(2, 1, 0, 3)
                 .reshape(16, 128, 1024))
    sf2 = np.asarray(inputs["L2_ffn_sfc2"]).T
    l2sf2 = _f32(sf2.reshape(8, 128, 16, 128).transpose(2, 1, 0, 3)
                 .reshape(16, 128, 1024))
    sf3 = np.asarray(inputs["L2_ffn_sfc3"]).T
    l2sf3 = _f32(sf3.reshape(16, 128, 8, 128).transpose(2, 1, 0, 3)
                 .reshape(8, 128, 2048))
    l3pk = np.ascontiguousarray(
        np.asarray(inputs["L3_ffn_packed_weight"]).reshape(1024, 256).T)
    W3 = np.asarray(inputs["L3_attn_W"])
    perm = []
    for blk in range(3):
        for h in range(16):
            perm.extend(range(192 * h + 64 * blk, 192 * h + 64 * blk + 64))
    W3p = W3[:, perm]
    l3w = _f32(W3p.reshape(8, 128, 48, 64).transpose(2, 1, 0, 3)
               .reshape(48, 128, 512))
    in_maps = []
    for b in range(NCORE):
        in_maps.append({
            "x": np.ascontiguousarray(x[b].T),
            "spf": spf_host,
            "spb": spb_host,
            "l0w1t": l0w1t,
            "l0w2t": l0w2t,
            "l1w1t": l1w1t,
            "l1w2t": l1w2t,
            "l3pk": l3pk,
            "l3w": l3w,
            "l2up": l2up,
            "l2dn": l2dn,
            "l2lat": l2lat,
            "l2bq": l2bq,
            "l2bk": l2bk,
            "l2bv": l2bv,
            "l2fw1": l2fw1,
            "l2fw2": l2fw2,
            "l2sf1": l2sf1,
            "l2sf2": l2sf2,
            "l2sf3": l2sf3,
            "l1wq": l1wq,
            "l1wk": l1wk,
            "l1wv": l1wv,
            "l1wo": l1wo,
        })
    res = run_bass_kernel_spmd(nc, in_maps, core_ids=list(range(NCORE)))
    global LAST_RESULTS
    LAST_RESULTS = res.results
    out = np.stack([np.asarray(r["out"]).T for r in res.results])
    return np.ascontiguousarray(out, dtype=np.float32)


# version 05
# speedup vs baseline: 1.0118x; 1.0118x over previous
import numpy as np
import ml_dtypes
import concourse.bass as bass
import concourse.mybir as mybir
from concourse.bass_utils import run_bass_kernel_spmd

F32 = mybir.dt.float32
BF16 = mybir.dt.bfloat16
I32 = mybir.dt.int32
AF = mybir.ActivationFunctionType
ALU = mybir.AluOpType
AX = mybir.AxisListType

B, S, D = 8, 2048, 1024
NCORE = 8
NCHUNK = D // 128
NTTILE = S // 128
EPS_LN = 1e-5

N_STAGES = 12

ENGS = ("sync", "gpsimd", "vector", "scalar", "tensor")


class Op:
    __slots__ = ("eng", "kind", "fn", "deps", "inc", "val", "known", "waits", "tag")

    def __init__(self, eng, kind, fn, deps, tag):
        self.eng = eng
        self.kind = kind
        self.fn = fn
        self.deps = deps
        self.inc = False
        self.val = 0
        self.known = None
        self.waits = None
        self.tag = tag


class Slot:

    def __init__(self, name=""):
        self.name = name
        self.writers = []
        self.readers = []

    def wdeps(self):
        return self.writers + self.readers

    def rdeps(self):
        return list(self.writers)

    def wrote(self, op):
        self.writers = [op]
        self.readers = []

    def wrote_many(self, ops):
        self.writers = list(ops)
        self.readers = []

    def read(self, op):
        self.readers.append(op)


class Prog:
    def __init__(self, nc):
        self.nc = nc
        self.order = []
        self.streams = {e: [] for e in ENGS}
        self.sems = {}
        for e in ENGS:
            for k in ("c", "d"):
                self.sems[(e, k)] = nc.alloc_semaphore(f"sem_{e}_{k}")
        self.shield = nc.alloc_sbuf_tensor("shield", [1, 2], F32)

    def op(self, eng, fn, deps=(), tag="", kind="c"):
        dd = []
        for d in deps:
            if d is None:
                continue
            if isinstance(d, (list, tuple)):
                dd.extend(x for x in d if x is not None)
            else:
                dd.append(d)
        o = Op(eng, kind, fn, dd, tag)
        self.order.append(o)
        self.streams[eng].append(o)
        return o

    def dma(self, eng, fn, deps=(), tag=""):
        return self.op(eng, fn, deps, tag, kind="d")

    def tap(self, nc, name, ap, deps, shape=None, dtype=None):
        shape = shape or ap.shape
        h = nc.dram_tensor("tap_" + name, list(shape), dtype or ap.dtype,
                           kind="ExternalOutput").ap()
        o = self.dma("sync", lambda e: e.dma_start(out=h[:], in_=ap),
                     deps=deps, tag="tap_" + name)
        self.taps = getattr(self, "taps", [])
        self.taps.append(o)
        return o

    def finalize(self):
        for o in self.order:
            if o.kind == "d":
                o.inc = True
            for d in o.deps:
                if d.eng == o.eng and d.kind == "c" and o.kind == "c":
                    continue
                d.inc = True
        for e in ENGS:
            cc = 0
            cd = 0
            for o in self.streams[e]:
                if o.inc:
                    if o.kind == "d":
                        cd += 16
                        o.val = cd
                    else:
                        cc += 1
                        o.val = cc
        known = {e: {} for e in ENGS}
        for o in self.order:
            k = known[o.eng]
            waits = {}
            for d in o.deps:
                same_cc = d.eng == o.eng and d.kind == "c" and o.kind == "c"
                if not same_cc:
                    s = (d.eng, d.kind)
                    if d.val > k.get(s, 0):
                        if d.val > waits.get(s, 0):
                            waits[s] = d.val
                        k[s] = d.val
                if d.known:
                    for s2, v2 in d.known.items():
                        if v2 > k.get(s2, 0):
                            k[s2] = v2
            o.waits = list(waits.items())
            o.known = dict(k)
            if o.inc:
                o.known[(o.eng, o.kind)] = o.val

    def emit(self):
        self.finalize()
        blk = self.nc.Block()
        with blk:
            for e in ENGS:
                stream = self.streams[e]
                if not stream:
                    continue

                def body(eng, e=e, stream=stream):
                    prev1 = None
                    prev2 = None
                    for o in stream:
                        for (se, sk), v in o.waits:
                            eng.wait_ge(self.sems[(se, sk)], v)
                        if e != "sync" and any(
                                (d is prev1 or d is prev2) for d in o.deps):
                            eng.drain()
                        inst = o.fn(eng)
                        prev2 = prev1
                        prev1 = o
                        if o.inc:
                            if inst is None:
                                eng.drain()
                                eng.nop().then_inc(self.sems[(o.eng, "c")], 1)
                            else:
                                inst.then_inc(
                                    self.sems[(o.eng, o.kind)],
                                    16 if o.kind == "d" else 1)

                getattr(blk, e)(body)


class SPack:
    def __init__(self, dtype):
        self.dtype = dtype
        self.cols = 0
        self.reg = {}
        self.arrs = []

    def add(self, name, arr):
        a = np.asarray(arr, dtype=np.float32)
        assert a.ndim == 2 and a.shape[0] <= 128, (name, a.shape)
        self.reg[name] = (self.cols, a.shape[0], a.shape[1])
        self.cols += a.shape[1]
        self.arrs.append(a)

    def host(self):
        out = np.zeros((128, self.cols), dtype=np.float32)
        c = 0
        for a in self.arrs:
            out[: a.shape[0], c : c + a.shape[1]] = a
            c += a.shape[1]
        return out

    def ap(self, sbuf, name, p=None, w=None, coff=0):
        off, pp, ww = self.reg[name]
        return sbuf[0 : (p or pp), off + coff : off + coff + (w or ww)]


def _f32(x):
    return np.ascontiguousarray(x, dtype=np.float32)


def _build_packs(inputs):
    spf = SPack(np.float32)
    spb = SPack(np.float32)

    spf.add("ident", np.eye(128, dtype=np.float32))
    spb.add("identb", np.eye(128, dtype=np.float32))
    spf.add("ones_col", np.ones((128, 1), np.float32))
    spb.add("ones_colb", np.ones((128, 1), np.float32))
    spf.add("ones_row", np.ones((1, 128), np.float32))
    spf.add("id8", np.eye(8, dtype=np.float32))
    spf.add("c_epsln", np.full((128, 1), 1e-5, np.float32))
    spf.add("c_eps6", np.full((128, 1), 1e-6, np.float32))

    wrt = _f32(np.asarray(inputs["L0_ffn_Wr"]).T).reshape(8, 128, 8)
    spb.add("l0_wrt", wrt.transpose(1, 0, 2).reshape(128, 64))
    b1 = _f32(inputs["L0_ffn_b1"]).reshape(8, 32, 128)
    spf.add("l0_b1", b1.transpose(2, 0, 1).reshape(128, 256))
    b2 = _f32(inputs["L0_ffn_b2"]).reshape(8, 8, 128)
    spf.add("l0_b2", b2.transpose(2, 0, 1).reshape(128, 64))

    for nm in ("Wq", "Wk", "Wv"):
        w = _f32(np.asarray(inputs[f"L0_attn_{nm}"]).T).reshape(8, 128, 64)
        spb.add(f"l0a_{nm.lower()}t", w.transpose(1, 0, 2).reshape(128, 512))
    spb.add("l0a_wot", _f32(np.asarray(inputs["L0_attn_Wo"]).T))
    spf.add("l0a_bq", _f32(np.asarray(inputs["L0_attn_bq"]).reshape(64, 1)))
    spf.add("l0a_bk", _f32(np.asarray(inputs["L0_attn_bk"]).reshape(64, 1)))
    spf.add("l0a_bv", _f32(np.asarray(inputs["L0_attn_bv"]).reshape(64, 1)))
    spf.add("l0a_bo", _f32(np.asarray(inputs["L0_attn_bo"]).reshape(8, 128).T))
    spf.add("l0a_dec", _f32(np.asarray(inputs["L0_attn_decay"]).reshape(64, 1)))
    spf.add("l0a_decr", _f32(np.asarray(inputs["L0_attn_decay"]).reshape(1, 64)))
    spf.add("triu", np.triu(np.ones((128, 128), np.float32)))
    b1 = _f32(inputs["L1_ffn_b1"]).reshape(32, 128)
    spf.add("l1_b1", b1.T)
    spf.add("l1_b2", _f32(inputs["L1_ffn_b2"]).reshape(8, 128).T)
    for nm in ("bq", "bk", "bv", "bo"):
        spf.add(f"l1a_{nm}", _f32(inputs[f"L1_attn_{nm}"]).reshape(8, 128).T)
    spf.add("c_inv8", np.full((128, 1), 0.125, np.float32))
    spf.add("id16", np.eye(16, dtype=np.float32))
    gw = _f32(np.asarray(inputs["L2_ffn_gate_w"]).T).reshape(8, 128, 16)
    spb.add("l2f_gwt", gw.transpose(1, 0, 2).reshape(128, 128))

    lnw = _f32(inputs["L2_attn_ln_w"]).reshape(4, 8, 128)
    spf.add("l2a_lnw", lnw.transpose(2, 0, 1).reshape(128, 32))
    cw = _f32(inputs["L2_attn_conv_w"]).reshape(4, 16, 128, 4)
    spf.add("l2a_cw", cw.transpose(2, 0, 1, 3).reshape(128, 256))
    cb = _f32(inputs["L2_attn_conv_b"]).reshape(4, 16, 128)
    spf.add("l2a_cb", cb.transpose(2, 0, 1).reshape(128, 64))
    sk = _f32(inputs["L2_attn_skip"]).reshape(4, 16, 128)
    spf.add("l2a_skip", sk.transpose(2, 0, 1).reshape(128, 64))
    onw = _f32(inputs["L2_attn_onorm_w"]).reshape(4, 16, 128)
    spf.add("l2a_onw", onw.transpose(2, 0, 1).reshape(128, 64))
    gb = np.concatenate([_f32(inputs["L2_attn_ig_b"]),
                         _f32(inputs["L2_attn_fg_b"])], axis=1)
    spf.add("l2a_gb", gb.T)
    spf.add("l2a_postw", _f32(inputs["L2_attn_post_w"]).reshape(8, 128).T)
    spf.add("l2a_latb", _f32(inputs["L2_attn_lat_b"]).reshape(8, 128).T)
    sel4 = np.zeros((4, 4 * 128), np.float32)
    for h in range(4):
        sel4[h, h * 128 : (h + 1) * 128] = 1.0
    spf.add("sel4", sel4)
    spf.add("id4", np.eye(4, dtype=np.float32))
    spf.add("c_neg1", np.full((128, 1), -1.0, np.float32))
    spf.add("c_one", np.ones((128, 1), np.float32))
    spf.add("c_isq512", np.full((128, 1), 1.0 / np.sqrt(512.0), np.float32))
    spf.add("l3_ws", _f32(inputs["L3_ffn_weight_scale"]).reshape(1, 1))
    spf.add("l3_bias", _f32(inputs["L3_ffn_bias"]).reshape(8, 128).T)
    gg = np.concatenate([_f32(inputs["L2_attn_ig_w"]),
                         _f32(inputs["L2_attn_fg_w"])], axis=1)
    ggt = gg.transpose(0, 2, 1).reshape(4, 48, 128, 8)
    spb.add("l2a_gw", ggt.transpose(2, 0, 1, 3).reshape(128, 4 * 384))
    spf.add("iota_col", np.arange(128, dtype=np.float32).reshape(128, 1))
    return spf, spb


class Env:
    pass


def _build(n_stages):
    _L0N[0] = 0
    _L1N[0] = 0
    _L2N[0] = 0
    _L3N[0] = 0
    nc = bass.Bass("TRN2", target_bir_lowering=False)
    P = Prog(nc)
    env = Env()
    env.nc = nc
    env.P = P

    env.x_h = nc.dram_tensor("x", [D, S], F32, kind="ExternalInput").ap()
    env.out_h = nc.dram_tensor("out", [D, S], F32, kind="ExternalOutput").ap()
    env.spf_h = nc.dram_tensor("spf", list(_SPF_SHAPE), F32, kind="ExternalInput").ap()
    env.spb_h = nc.dram_tensor("spb", list(_SPB_SHAPE), F32, kind="ExternalInput").ap()
    env.l0w1_h = nc.dram_tensor("l0w1t", [8, 32, 128, 8 * 128], F32,
                                kind="ExternalInput").ap()
    env.l0w2_h = nc.dram_tensor("l0w2t", [8, 8, 128, 32 * 128], F32,
                                kind="ExternalInput").ap()
    env.l3pk_h = nc.dram_tensor("l3pk", [256, 1024], I32,
                                kind="ExternalInput").ap()
    env.l3w_h = nc.dram_tensor("l3w", [48, 128, 512], F32,
                               kind="ExternalInput").ap()
    env.l3wb_h = nc.dram_tensor("l3wb", [1024, 1024], BF16, kind="Internal").ap()
    env.l2up_h = nc.dram_tensor("l2up", [4, 32, 128, 1024], F32,
                                kind="ExternalInput").ap()
    env.l2dn_h = nc.dram_tensor("l2dn", [4, 8, 128, 2048], F32,
                                kind="ExternalInput").ap()
    env.l2lat_h = nc.dram_tensor("l2lat", [8, 128, 1024], F32,
                                 kind="ExternalInput").ap()
    env.l2bq_h = nc.dram_tensor("l2bq", [4, 16, 128, 128], F32,
                                kind="ExternalInput").ap()
    env.l2bk_h = nc.dram_tensor("l2bk", [4, 16, 128, 128], F32,
                                kind="ExternalInput").ap()
    env.l2bv_h = nc.dram_tensor("l2bv", [4, 16, 128, 128], F32,
                                kind="ExternalInput").ap()
    env.xcd_h = nc.dram_tensor("xcd", [16, 128, 2048], BF16, kind="Internal").ap()
    env.szd_h = nc.dram_tensor("szd", [16, 128, 2048], BF16, kind="Internal").ap()
    env.htd_h = nc.dram_tensor("htd", [16, 128, 2048], BF16, kind="Internal").ap()
    env.qd_h = nc.dram_tensor("qd", [16, 128, 2048], BF16, kind="Internal").ap()
    env.kd_h = nc.dram_tensor("kd", [16, 128, 2048], BF16, kind="Internal").ap()
    env.vtd_h = nc.dram_tensor("vtd", [4, 128, 8192], BF16, kind="Internal").ap()
    env.l2fw1_h = nc.dram_tensor("l2fw1", [16, 4, 128, 1024], F32,
                                 kind="ExternalInput").ap()
    env.l2fw2_h = nc.dram_tensor("l2fw2", [16, 8, 128, 512], F32,
                                 kind="ExternalInput").ap()
    env.l2sf1_h = nc.dram_tensor("l2sf1", [16, 128, 1024], F32,
                                 kind="ExternalInput").ap()
    env.l2sf2_h = nc.dram_tensor("l2sf2", [16, 128, 1024], F32,
                                 kind="ExternalInput").ap()
    env.l2sf3_h = nc.dram_tensor("l2sf3", [8, 128, 2048], F32,
                                 kind="ExternalInput").ap()
    for nm in ("l1wq", "l1wk", "l1wv", "l1wo"):
        setattr(env, nm + "_h", nc.dram_tensor(nm, [8, 128, 1024], F32,
                                               kind="ExternalInput").ap())
    env.l1w1_h = nc.dram_tensor("l1w1t", [32, 128, 1024], F32,
                                kind="ExternalInput").ap()
    env.l1w2_h = nc.dram_tensor("l1w2t", [8, 128, 4096], F32,
                                kind="ExternalInput").ap()
    env.l1w1b_h = nc.dram_tensor("l1w1b", [32, 128, 1024], BF16,
                                 kind="Internal").ap()
    env.l1w2b_h = nc.dram_tensor("l1w2b", [8, 128, 4096], BF16,
                                 kind="Internal").ap()
    env.xdump_h = nc.dram_tensor("xdump", [128, NCHUNK * S], F32, kind="Internal").ap()
    env.hdram_h = nc.dram_tensor("hdram", [4, 128, 32 * 512], BF16,
                                 kind="Internal").ap()

    env.X = nc.alloc_sbuf_tensor("X", [128, NCHUNK * S], F32)
    env.sX = Slot("X")
    env.XLN = nc.alloc_sbuf_tensor("XLN", [128, NCHUNK * S], BF16)
    env.sXLN = Slot("XLN")
    env.C = nc.alloc_sbuf_tensor("C", [128, 8192], BF16)
    env.sC = [Slot("C0"), Slot("C1")]
    env.Dg = nc.alloc_sbuf_tensor("Dg", [128, 16384], BF16)
    env.sD = [Slot("D0"), Slot("D1")]
    env.E = nc.alloc_sbuf_tensor("E", [128, 6144], BF16)
    env.sE = [Slot("E0"), Slot("E1"), Slot("E2"), Slot("E3")]
    env.MUR = nc.alloc_sbuf_tensor("MUR", [1, 2048], F32)
    env.sMUR = Slot("MUR")
    env.SC1 = nc.alloc_sbuf_tensor("SC1", [128, 2048], F32)
    env.sSC1 = Slot("SC1")
    env.SC2 = nc.alloc_sbuf_tensor("SC2", [128, 2048], F32)
    env.sSC2 = Slot("SC2")
    env.WTM = nc.alloc_sbuf_tensor("WTM", [16, S], F32)
    env.sWTM = Slot("WTM")
    env.SPF = nc.alloc_sbuf_tensor("SPF", list(_SPF_SHAPE), F32)
    env.sSPF = Slot("SPF")
    env.SPB = nc.alloc_sbuf_tensor("SPB", list(_SPB_SHAPE), BF16)
    env.sSPB = Slot("SPB")

    env.PS = nc.alloc_psum_tensor("PS", [128, 4096], F32)
    env.bank = lambda i: env.PS[:, 512 * i : 512 * (i + 1)]
    env.sB = [Slot(f"bank{i}") for i in range(8)]

    ld_x = P.dma("sync", lambda e: e.dma_start(
        out=env.X[:].rearrange("p (c s) -> p c s", c=NCHUNK),
        in_=env.x_h.rearrange("(c p) s -> p c s", p=128)), tag="ld_x")
    env.sX.wrote(ld_x)
    ld_spf = P.dma("sync", lambda e: e.dma_start(out=env.SPF[:], in_=env.spf_h[:]),
                   tag="ld_spf")
    env.sSPF.wrote(ld_spf)
    ld_spb = P.dma("gpsimd", lambda e: e.dma_start(out=env.SPB[:], in_=env.spb_h[:]),
                   tag="ld_spb")
    env.sSPB.wrote(ld_spb)

    env.spf = lambda name, p=None, w=None, coff=0: _SPF.ap(env.SPF, name, p, w, coff)
    env.spb = lambda name, p=None, w=None, coff=0: _SPB.ap(env.SPB, name, p, w, coff)

    env.BROWT = nc.alloc_sbuf_tensor("BROWT", [128, 128], F32)
    env.BROW = {}
    env.sBROW = Slot("BROW")

    def mk_brow(name, slot_col):
        bk = env.bank(6)
        t1 = P.op("tensor", lambda e: e.transpose(
            bk[0:1, 0:64], env.spf(name), env.spf("ident", p=64, w=64)),
            deps=[env.sSPF.writers] + env.sB[6].wdeps(), tag=f"brow_t_{name}")
        env.sB[6].wrote(t1)
        c1 = P.op("vector", lambda e: e.tensor_copy(
            env.SC1[0:1, 1024 + slot_col : 1088 + slot_col], bk[0:1, 0:64]),
            deps=[t1] + env.sSC1.wdeps(), tag=f"brow_c_{name}")
        env.sB[6].read(c1)
        env.sSC1.wrote(c1)
        t2 = P.op("tensor", lambda e: e.matmul(
            bk[:, 0:64], env.spf("ones_row"),
            env.SC1[0:1, 1024 + slot_col : 1088 + slot_col],
            start=True, stop=True), deps=[c1] + env.sB[6].wdeps(),
            tag=f"brow_b_{name}")
        env.sB[6].wrote(t2)
        env.sSC1.read(t2)
        c2 = P.op("vector", lambda e: e.tensor_copy(
            env.BROWT[:, slot_col : slot_col + 64], bk[:, 0:64]),
            deps=[t2] + env.sBROW.wdeps(), tag=f"brow_e_{name}")
        env.sB[6].read(c2)
        env.BROW[name] = env.BROWT[:, slot_col : slot_col + 64]
        env.sBROW.writers.append(c2)
    env.mk_brow = mk_brow

    stages = [
        lambda: emit_l0_ffn(env), lambda: emit_l0_attn(env), lambda: emit_l0_ffn(env),
        lambda: emit_l1_ffn(env), lambda: emit_l1_attn(env), lambda: emit_l1_ffn(env),
        lambda: emit_l2_ffn(env), lambda: emit_l2_attn(env), lambda: emit_l2_ffn(env),
        lambda: emit_l3_ffn(env), lambda: emit_l3_attn(env), lambda: emit_l3_ffn(env),
    ]
    env.stage_guard = []
    for i in range(min(n_stages, 12)):
        stages[i]()
        env.stage_guard = [P.streams[e][-1] for e in ENGS if P.streams[e]]

    st = P.dma("sync", lambda e: e.dma_start(
        out=env.out_h.rearrange("(c p) s -> p c s", p=128),
        in_=env.X[:].rearrange("p (c s) -> p c s", c=NCHUNK)),
        deps=env.sX.rdeps(), tag="st_out")
    env.sX.read(st)
    P.op("sync", lambda e: e.nop(), deps=[st], tag="fin")

    P.emit()
    return nc


def emit_ln(env):
    P = env.P
    src, out = env.X, env.XLN
    MU = env.MUR[0:1, 0:2048]
    RS = env.SC1[0:1, 0:2048]
    TMP = env.SC2[0:1, 0:2048]
    rv_guard = env.sMUR.wdeps() + env.sSC1.wdeps()
    out_guard = env.sXLN.wdeps()
    copies = []

    for j in range(4):
        cs = slice(512 * j, 512 * (j + 1))
        bk6, bk7 = env.bank(6), env.bank(7)

        def sum_grp(e, j=j, bk6=bk6):
            r = None
            for c in range(NCHUNK):
                r = e.matmul(bk6[0:1, :], env.spf("ones_col"),
                             src[:, c * S + 512 * j : c * S + 512 * j + 512],
                             start=(c == 0), stop=(c == NCHUNK - 1))
            return r
        o1 = P.op("tensor", sum_grp, deps=env.sX.rdeps() + [env.sSPF.writers] +
                  env.sB[6].wdeps(), tag=f"ln_sum{j}")
        env.sB[6].wrote(o1)
        env.sX.read(o1)
        c1 = P.op("vector", lambda e, cs=cs, bk6=bk6: e.tensor_copy(
            MU[0:1, cs], bk6[0:1, :]), deps=[o1] + rv_guard, tag=f"ln_scp{j}")
        env.sB[6].read(c1)
        env.sMUR.read(c1)
        copies.append(c1)

        sq_mm = []
        for c in range(NCHUNK):
            scol = 512 * (c % 4)
            sq = P.op("scalar", lambda e, c=c, j=j, scol=scol: e.activation(
                env.SC2[:, scol : scol + 512],
                src[:, c * S + 512 * j : c * S + 512 * j + 512], AF.Square),
                deps=env.sX.rdeps() + env.sSC2.wdeps(), tag=f"ln_sq{j}_{c}")
            env.sSC2.wrote(sq)
            env.sX.read(sq)
            mm = P.op("tensor", lambda e, c=c, scol=scol, bk7=bk7: e.matmul(
                bk7[0:1, :], env.spf("ones_col"), env.SC2[:, scol : scol + 512],
                start=(c == 0), stop=(c == NCHUNK - 1)),
                deps=[sq] + (env.sB[7].wdeps() if c == 0 else []),
                tag=f"ln_sqmm{j}_{c}")
            env.sSC2.read(mm)
            sq_mm.append(mm)
        env.sB[7].wrote(sq_mm[-1])
        c2 = P.op("vector", lambda e, cs=cs, bk7=bk7: e.tensor_copy(
            RS[0:1, cs], bk7[0:1, :]), deps=[sq_mm[-1]] + rv_guard +
            env.sSC1.wdeps(), tag=f"ln_qcp{j}")
        env.sB[7].read(c2)
        env.sSC1.read(c2)
        copies.append(c2)

    rm1 = P.op("vector", lambda e: e.tensor_scalar(MU, MU, 1.0 / D, None, ALU.mult),
               deps=copies, tag="ln_mu")
    rm2 = P.op("vector", lambda e: e.tensor_scalar(RS, RS, 1.0 / D, None, ALU.mult),
               deps=[rm1], tag="ln_msq")
    rm3 = P.op("vector", lambda e: e.tensor_tensor(TMP, MU, MU, ALU.mult),
               deps=[rm1] + env.sSC2.wdeps(), tag="ln_mu2")
    env.sSC2.wrote(rm3)
    rm4 = P.op("vector", lambda e: e.tensor_tensor(RS, RS, TMP, ALU.subtract),
               deps=[rm2, rm3], tag="ln_var")
    rm5 = P.op("scalar", lambda e: e.activation(TMP, RS, AF.Sqrt,
               bias=env.spf("c_epsln", p=1)), deps=[rm4], tag="ln_sd")
    rm6 = P.op("vector", lambda e: e.reciprocal(RS, TMP), deps=[rm5], tag="ln_rs")
    env.sMUR.wrote_many([rm1])
    env.sSC1.wrote_many([rm6])
    env.sSC2.read(rm5)

    outs = []
    for j in range(4):
        cs = slice(512 * j, 512 * (j + 1))
        bk6, bk7 = env.bank(6), env.bank(7)
        m1 = P.op("tensor", lambda e, cs=cs, bk6=bk6: e.matmul(
            bk6, env.spf("ones_row"), MU[0:1, cs], start=True, stop=True),
            deps=[rm6, rm1] + env.sB[6].wdeps(), tag=f"ln_bmu{j}")
        env.sB[6].wrote(m1)
        env.sMUR.read(m1)
        m2 = P.op("tensor", lambda e, cs=cs, bk7=bk7: e.matmul(
            bk7, env.spf("ones_row"), RS[0:1, cs], start=True, stop=True),
            deps=[rm6] + env.sB[7].wdeps(), tag=f"ln_brs{j}")
        env.sB[7].wrote(m2)
        env.sSC1.read(m2)
        for c in range(NCHUNK):
            sl = slice(c * S + 512 * j, c * S + 512 * j + 512)
            v1 = P.op("vector", lambda e, sl=sl, bk6=bk6: e.tensor_tensor(
                env.SC2[:, 0:512], src[:, sl], bk6, ALU.subtract),
                deps=[m1] + env.sX.rdeps() + env.sSC2.wdeps(), tag=f"ln_sub{j}_{c}")
            env.sSC2.wrote(v1)
            env.sX.read(v1)
            env.sB[6].read(v1)
            v2 = P.op("vector", lambda e, sl=sl, bk7=bk7: e.tensor_tensor(
                out[:, sl], env.SC2[:, 0:512], bk7, ALU.mult),
                deps=[m2, v1] + out_guard, tag=f"ln_mul{j}_{c}")
            env.sSC2.read(v2)
            env.sB[7].read(v2)
            outs.append(v2)
    env.sXLN.wrote_many(outs)
    return outs


_L0N = [0]


def emit_l0_ffn(env):
    P = env.P
    emit_ln(env)

    wtm_ops = []
    wtm_guard = env.sWTM.wdeps()
    for j in range(NTTILE):
        bk = env.bank(5)

        def rgrp(e, j=j, bk=bk):
            r = None
            for c in range(NCHUNK):
                r = e.matmul(bk[0:128, 0:8],
                             env.XLN[:, c * S + 128 * j : c * S + 128 * j + 128],
                             env.spb("l0_wrt")[:, c * 8 : c * 8 + 8],
                             start=(c == 0), stop=(c == NCHUNK - 1))
            return r
        mm = P.op("tensor", rgrp, deps=env.sXLN.rdeps() + [env.sSPB.writers] +
                  env.sB[5].wdeps(), tag=f"l0r_mm{j}")
        env.sB[5].wrote(mm)
        env.sXLN.read(mm)
        T8 = env.SC1[:, 0:8]
        LG = env.SC1[:, 32:40]
        M1 = env.SC1[:, 8:9]
        SUM = env.SC1[:, 9:10]
        PRB = env.SC1[:, 16:24]
        MSK = env.SC1[:, 24:32]
        g = env.sSC1.wdeps()
        v0 = P.op("vector", lambda e, bk=bk: e.tensor_copy(LG, bk[0:128, 0:8]),
                  deps=[mm] + g, tag=f"l0r_lgc{j}")
        env.sB[5].read(v0)
        v1 = P.op("vector", lambda e: e.reduce_max(
            M1, LG, axis=AX.X, negate=True), deps=[v0], tag=f"l0r_max{j}")
        s1 = P.op("scalar", lambda e: e.activation(
            T8, LG, AF.Exp, bias=M1),
            deps=[v0, v1] + g, tag=f"l0r_exp{j}")
        v2a = P.op("vector", lambda e: e.reduce_sum(SUM, T8, axis=AX.X),
                   deps=[s1], tag=f"l0r_sum{j}")
        if DEBUG_TAPS and j == 0 and _L0N[0] == 0:
            sa = P.op("vector", lambda e: e.tensor_copy(env.SC2[:, 120:128], T8),
                      deps=[v2a] + env.sSC2.wdeps(), tag="snap_t8")
            env.sSC2.wrote(sa)
            sb_ = P.op("vector", lambda e: e.tensor_copy(env.SC2[:, 128:129], SUM),
                       deps=[v2a], tag="snap_sum")
            env.sSC2.read(sb_)
            ta = P.tap(env.nc, "t8v", env.SC2[:, 120:129], [sa, sb_])
            env.sSC2.read(ta)
        v2 = P.op("vector", lambda e: e.reciprocal(M1, SUM), deps=[v2a], tag=f"l0r_rcp{j}")
        v3 = P.op("vector", lambda e: e.tensor_scalar(PRB, T8, M1, None, ALU.mult),
                  deps=[v2, s1], tag=f"l0r_prb{j}")
        v4 = P.op("vector", lambda e: e.reduce_max(M1, PRB, axis=AX.X), deps=[v3],
                  tag=f"l0r_m1{j}")
        v5 = P.op("vector", lambda e: e.tensor_scalar(MSK, PRB, M1, None, ALU.is_lt),
                  deps=[v4], tag=f"l0r_lt{j}")
        v6 = P.op("vector", lambda e: e.tensor_tensor(MSK, PRB, MSK, ALU.mult),
                  deps=[v5], tag=f"l0r_pm{j}")
        v7 = P.op("vector", lambda e: e.reduce_max(M1, MSK, axis=AX.X), deps=[v6],
                  tag=f"l0r_m2{j}")
        v8 = P.op("vector", lambda e: e.tensor_scalar(MSK, PRB, M1, None, ALU.is_ge),
                  deps=[v7], tag=f"l0r_ge{j}")
        v9 = P.op("vector", lambda e: e.tensor_tensor(T8, PRB, MSK, ALU.mult),
                  deps=[v8], tag=f"l0r_w{j}")
        env.sSC1.wrote(v9)
        if DEBUG_TAPS and j == 0 and _L0N[0] == 0:
            sn1 = P.op("vector", lambda e: e.tensor_copy(env.SC2[:, 100:101],
                       env.SC1[:, 8:9]), deps=[v1] + env.sSC2.wdeps(), tag="snap_m1")
            env.sSC2.wrote(sn1)
            sn2 = P.op("scalar", lambda e: e.tensor_copy(env.SC2[:, 104:112],
                       env.SC1[:, 0:8]) if False else e.copy(env.SC2[:, 104:112],
                       env.SC1[:, 0:8]), deps=[s1], tag="snap_e8")
            env.sSC2.read(sn2)
            sn3 = P.op("vector", lambda e: e.tensor_copy(env.SC2[:, 112:120],
                       env.SC1[:, 32:40]), deps=[v0], tag="snap_lg")
            env.sSC2.read(sn3)
            tm1 = P.tap(env.nc, "m1a", env.SC2[:, 100:101], [sn1])
            env.sSC2.read(tm1)
            te8 = P.tap(env.nc, "e8", env.SC2[:, 104:112], [sn2])
            env.sSC2.read(te8)
            tlg2 = P.tap(env.nc, "lg2", env.SC2[:, 112:120], [sn3])
            env.sSC2.read(tlg2)
            cpl = P.op("vector", lambda e, bk=bk: e.tensor_copy(
                env.SC2[:, 0:8], bk[0:128, 0:8]), deps=[mm] + env.sSC2.wdeps(),
                tag="dbg_lgcp")
            env.sB[5].read(cpl)
            env.sSC2.wrote(cpl)
            tpl = P.tap(env.nc, "lg0", env.SC2[:, 0:8], [cpl])
            env.sSC2.read(tpl)
            tpp = P.tap(env.nc, "prb0", env.SC1[:, 16:24], [v3, v9])
            env.sSC1.read(tpp)
            tpw = P.tap(env.nc, "w0", env.SC1[:, 0:8], [v9])
            env.sSC1.read(tpw)
        bkt = env.bank(4)
        t1 = P.op("tensor", lambda e, bkt=bkt: e.transpose(
            bkt[0:8, 0:128], T8, env.spf("ident")),
            deps=[v9] + env.sB[4].wdeps() + [env.sSPF.writers], tag=f"l0r_tr{j}")
        env.sB[4].wrote(t1)
        env.sSC1.read(t1)
        c1 = P.op("vector", lambda e, j=j, bkt=bkt: e.tensor_copy(
            env.WTM[0:8, 128 * j : 128 * (j + 1)], bkt[0:8, 0:128]),
            deps=[t1] + wtm_guard, tag=f"l0r_cp{j}")
        env.sB[4].read(c1)
        wtm_ops.append(c1)
    env.sWTM.wrote_many(wtm_ops)

    if DEBUG_TAPS:
        t_ = P.tap(env.nc, f"xln{_L0N[0]}", env.XLN[:], env.sXLN.rdeps())
        env.sXLN.read(t_)
        t_ = P.tap(env.nc, f"wtm{_L0N[0]}", env.WTM[:], env.sWTM.rdeps())
        env.sWTM.read(t_)
    dx = P.dma("sync", lambda e: e.dma_start(out=env.xdump_h[:], in_=env.X[:]),
               deps=env.sX.rdeps(), tag="l0_dumpx")
    env.sX.read(dx)
    YACC = env.X

    for e_i in range(8):
        h_store = []
        for hc in range(32):
            slot = hc % 2
            W1S = env.E[:, slot * 1024 : slot * 1024 + 1024]
            sW1 = env.sE[slot]
            ldw = P.dma("gpsimd", lambda e, e_i=e_i, hc=hc, W1S=W1S: e.dma_start(
                out=W1S, in_=env.l0w1_h[e_i, hc]),
                deps=sW1.wdeps() + [env.sSPB.writers], tag=f"l0w1ld_{e_i}_{hc}")
            sW1.wrote(ldw)
            hsl = 2 + (hc % 2)
            HC = env.E[:, 2048 + (hc % 2) * 2048 : 2048 + (hc % 2) * 2048 + 2048]
            sHC = env.sE[hsl]
            hc_guard = sHC.wdeps()
            silus = []
            for t in range(4):
                bk = env.bank(t % 2)

                def w1grp(e, W1S=W1S, t=t, bk=bk):
                    r = None
                    for c in range(NCHUNK):
                        r = e.matmul(bk, W1S[:, 128 * c : 128 * (c + 1)],
                                     env.XLN[:, c * S + 512 * t : c * S + 512 * t + 512],
                                     start=(c == 0), stop=(c == NCHUNK - 1))
                    return r
                mm = P.op("tensor", w1grp, deps=[ldw] + env.sXLN.rdeps() +
                          env.sB[t % 2].wdeps(), tag=f"l0w1mm_{e_i}_{hc}_{t}")
                env.sB[t % 2].wrote(mm)
                env.sXLN.read(mm)
                sW1.read(mm)
                b1ap = env.spf("l0_b1", w=1, coff=e_i * 32 + hc)
                sv = P.op("scalar", lambda e, t=t, bk=bk, HC=HC, b1ap=b1ap:
                          e.activation(HC[:, 512 * t : 512 * (t + 1)], bk, AF.Silu,
                                       bias=b1ap),
                          deps=[mm] + hc_guard, tag=f"l0w1si_{e_i}_{hc}_{t}")
                env.sB[t % 2].read(sv)
                silus.append(sv)
            sHC.wrote_many(silus)
            std = P.dma("sync", lambda e, hc=hc, HC=HC: e.dma_start(
                out=env.hdram_h[:, :, 512 * hc : 512 * (hc + 1)].transpose([1, 0, 2]),
                in_=HC[:].rearrange("p (t w) -> p t w", t=4)),
                deps=silus, tag=f"l0hst_{e_i}_{hc}")
            sHC.read(std)
            h_store.append(std)

        for t in range(4):
            hdeps = []
            for hh in range(2):
                hs = env.sD[hh]
                HT = env.Dg[:, hh * 8192 : hh * 8192 + 8192]
                ldh = P.dma("sync", lambda e, t=t, hh=hh, HT=HT: e.dma_start(
                    out=HT, in_=env.hdram_h[t, :, 8192 * hh : 8192 * (hh + 1)]),
                    deps=hs.wdeps() + h_store, tag=f"l0htld_{e_i}_{t}_{hh}")
                hs.wrote(ldh)
                hdeps.append(ldh)
            bkw = env.bank(4)
            WROW = env.SC1[0:1, 1536:2048]
            b1_ = P.op("tensor", lambda e, e_i=e_i, t=t, bkw=bkw: e.matmul(
                bkw[0:1, :], env.spf("id8", p=8, w=1, coff=e_i),
                env.WTM[0:8, 512 * t : 512 * (t + 1)], start=True, stop=True),
                deps=env.sWTM.rdeps() + env.sB[4].wdeps(), tag=f"l0wr_{e_i}_{t}")
            env.sB[4].wrote(b1_)
            env.sWTM.read(b1_)
            c1_ = P.op("vector", lambda e, bkw=bkw: e.tensor_copy(
                WROW, bkw[0:1, :]), deps=[b1_] + env.sSC1.wdeps(),
                tag=f"l0wrc_{e_i}_{t}")
            env.sB[4].read(c1_)
            env.sSC1.wrote(c1_)
            bw = P.op("tensor", lambda e, bkw=bkw: e.matmul(
                bkw, env.spf("ones_row"), WROW, start=True, stop=True),
                deps=[c1_] + env.sB[4].wdeps(), tag=f"l0wb_{e_i}_{t}")
            env.sB[4].wrote(bw)
            env.sSC1.read(bw)
            cw = P.op("vector", lambda e, bkw=bkw: e.tensor_copy(
                env.SC2[:, 0:512], bkw), deps=[bw] + env.sSC2.wdeps(),
                tag=f"l0wbc_{e_i}_{t}")
            env.sB[4].read(cw)
            env.sSC2.wrote(cw)
            for och in range(8):
                qs = och % 2
                W2S = env.C[:, qs * 4096 : qs * 4096 + 4096]
                sW2 = env.sC[qs]
                ldw2 = P.dma("gpsimd", lambda e, e_i=e_i, och=och, W2S=W2S:
                             e.dma_start(out=W2S, in_=env.l0w2_h[e_i, och]),
                             deps=sW2.wdeps(), tag=f"l0w2ld_{e_i}_{t}_{och}")
                sW2.wrote(ldw2)
                gc = och
                bk = env.bank(2 + och % 2)

                def w2grp(e, W2S=W2S, bk=bk):
                    r = None
                    for hcc in range(32):
                        r = e.matmul(
                            bk, W2S[:, 128 * hcc : 128 * (hcc + 1)],
                            env.Dg[:, (hcc // 16) * 8192 + (hcc % 16) * 512 :
                                   (hcc // 16) * 8192 + (hcc % 16) * 512 + 512],
                            start=(hcc == 0), stop=(hcc == 31))
                    return r
                mm = P.op("tensor", w2grp, deps=[ldw2] + hdeps +
                          env.sB[2 + och % 2].wdeps(), tag=f"l0w2mm_{e_i}_{t}_{och}")
                env.sB[2 + och % 2].wrote(mm)
                sW2.read(mm)
                env.sD[0].read(mm)
                env.sD[1].read(mm)
                b2ap = env.spf("l0_b2", w=1, coff=e_i * 8 + gc)
                sv = P.op("scalar", lambda e, bk=bk, b2ap=b2ap: e.activation(
                    env.SC1[:, 512:1024], bk, AF.Identity, bias=b2ap),
                    deps=[mm] + env.sSC1.wdeps(), tag=f"l0yb2_{e_i}_{t}_{och}")
                env.sB[2 + och % 2].read(sv)
                env.sSC1.wrote(sv)
                ysl = slice(gc * S + 512 * t, gc * S + 512 * t + 512)
                if e_i == 0:
                    yop = P.op("vector", lambda e, ysl=ysl: e.tensor_tensor(
                        YACC[:, ysl], env.SC1[:, 512:1024], env.SC2[:, 0:512],
                        ALU.mult), deps=[sv, cw, dx] + env.sX.wdeps(),
                        tag=f"l0y_{e_i}_{t}_{och}")
                else:
                    def yfn(e, ysl=ysl):
                        e.tensor_tensor(env.SC1[:, 1024:1536], env.SC1[:, 512:1024],
                                        env.SC2[:, 0:512], ALU.mult)
                        return e.tensor_tensor(YACC[:, ysl], YACC[:, ysl],
                                               env.SC1[:, 1024:1536], ALU.add)
                    yop = P.op("vector", yfn, deps=[sv, cw] + env.sX.wdeps(),
                               tag=f"l0y_{e_i}_{t}_{och}")
                env.sSC1.read(yop)
                env.sSC2.read(yop)
                env.sX.wrote(yop)

    _L0N[0] += 1

    for c in range(NCHUNK):
        ld = P.dma("sync", lambda e, c=c: e.dma_start(
            out=env.SC2[:, 0:2048], in_=env.xdump_h[:, c * S : (c + 1) * S]),
            deps=env.sSC2.wdeps() + [dx], tag=f"l0xrl_{c}")
        env.sSC2.wrote(ld)
        ad = P.op("vector", lambda e, c=c: e.scalar_tensor_tensor(
            env.X[:, c * S : (c + 1) * S], env.X[:, c * S : (c + 1) * S], 0.5,
            env.SC2[:, 0:2048], ALU.mult, ALU.add),
            deps=[ld] + env.sX.wdeps(), tag=f"l0xadd_{c}")
        env.sSC2.read(ad)
        env.sX.wrote(ad)


def emit_l0_attn(env):
    P = env.P
    nc = env.nc
    emit_ln(env)

    QF = env.C[:, 0:2048]
    KF = env.C[:, 2048:4096]
    SCN = env.C[:, 4096:6144]
    VT = env.Dg[:, 0:1024]
    KT = env.Dg[:, 1024:2048]
    AP_ = env.Dg[:, 2048 + 0:2048 + 256]
    VTL = env.Dg[:, 2560:2560 + 256]
    sQF, sKF, sSCN = env.sC[0], env.sC[1], Slot("SCN")
    sVT, sKT, sAP, sVTL = env.sD[0], env.sD[1], Slot("AP"), Slot("VTL")
    if not hasattr(env, "TST"):
        env.TST = nc.alloc_sbuf_tensor("TST", [64, 64], F32)
        env.DTB = nc.alloc_sbuf_tensor("DTB", [128, 256], F32)
        env.D64 = nc.alloc_sbuf_tensor("D64", [64, 192], F32)
        env.sTST = Slot("TST")
        env.sDTB = Slot("DTB")
        env.sD64 = Slot("D64")
    TST = env.TST

    if "l0a_bk" not in env.BROW:
        env.mk_brow("l0a_bk", 0)
        env.mk_brow("l0a_bv", 64)

    DposT = env.DTB[:, 0:64]
    DnegT = env.DTB[:, 64:128]
    DrevT = env.DTB[:, 128:192]
    LNDB = env.DTB[:, 192:256]
    DCOL = env.D64[:, 0:64]
    DROW = env.D64[:, 64:128]
    R64 = env.SC1[0:1, 0:64]
    g_dtb = env.sDTB.wdeps() + env.sD64.wdeps() + env.sSC1.wdeps()
    d1 = P.op("scalar", lambda e: e.activation(R64, env.spf("l0a_decr"),
              AF.Sigmoid), deps=[env.sSPF.writers] + g_dtb, tag="l0a_sig")
    d2 = P.op("scalar", lambda e: e.activation(R64, R64, AF.Ln), deps=[d1],
              tag="l0a_ln")
    bkd = env.bank(6)
    d3 = P.op("tensor", lambda e: e.matmul(bkd[:, 0:64], env.spf("ones_row"),
              R64, start=True, stop=True), deps=[d2] + env.sB[6].wdeps(),
              tag="l0a_blnd")
    env.sB[6].wrote(d3)
    env.sSC1.read(d3)
    d4 = P.op("vector", lambda e: e.tensor_copy(LNDB, bkd[:, 0:64]),
              deps=[d3], tag="l0a_lndcp")
    env.sB[6].read(d4)
    d5 = P.op("vector", lambda e: e.tensor_scalar(DposT, LNDB,
              env.spf("iota_col"), None, ALU.mult), deps=[d4], tag="l0a_t1")
    d5b = P.op("scalar", lambda e: e.activation(DposT, DposT, AF.Exp),
               deps=[d5], tag="l0a_t1e")
    d6 = P.op("vector", lambda e: e.tensor_scalar(DnegT, LNDB,
              env.spf("iota_col"), -1.0, ALU.mult, ALU.mult), deps=[d4],
              tag="l0a_t2")
    d6b = P.op("scalar", lambda e: e.activation(DnegT, DnegT, AF.Exp),
               deps=[d6], tag="l0a_t2e")
    def t3fn(e):
        e.tensor_scalar(env.SC2[:, 192:193], env.spf("iota_col"), -1.0, 127.0,
                        ALU.mult, ALU.add)
        e.drain()
        return e.tensor_scalar(DrevT, LNDB, env.SC2[:, 192:193], None, ALU.mult)
    d7 = P.op("vector", t3fn, deps=[d4] + env.sSC2.wdeps(), tag="l0a_t3")
    env.sSC2.wrote(d7)
    d7b = P.op("scalar", lambda e: e.activation(DrevT, DrevT, AF.Exp),
               deps=[d7], tag="l0a_t3e")
    d8 = P.op("vector", lambda e: e.tensor_scalar(env.SC1[0:1, 64:128], R64,
              128.0, None, ALU.mult), deps=[d2] + env.sSC1.wdeps(), tag="l0a_t4")
    d8b = P.op("scalar", lambda e: e.activation(env.SC1[0:1, 64:128],
               env.SC1[0:1, 64:128], AF.Exp), deps=[d8], tag="l0a_t4e")
    d9 = P.op("tensor", lambda e: e.matmul(bkd[0:64, 64:128],
              env.spf("ones_row", w=64), env.SC1[0:1, 64:128],
              start=True, stop=True),
              deps=[d8b] + env.sB[6].wdeps(), tag="l0a_bd128")
    env.sB[6].wrote(d9)
    env.sSC1.read(d9)
    d10 = P.op("vector", lambda e: e.tensor_copy(DCOL, bkd[0:64, 64:128]),
               deps=[d9], tag="l0a_dcolcp")
    env.sB[6].read(d10)
    dsg = P.op("scalar", lambda e: e.activation(env.SC1[0:1, 128:192],
               env.spf("l0a_decr"), AF.Sigmoid), deps=g_dtb, tag="l0a_sig2")
    d11 = P.op("tensor", lambda e: e.matmul(bkd[0:64, 128:192],
               env.spf("ones_row", w=64), env.SC1[0:1, 128:192],
               start=True, stop=True),
               deps=[dsg] + env.sB[6].wdeps(), tag="l0a_bdrow")
    env.sB[6].wrote(d11)
    env.sSC1.read(d11)
    d12 = P.op("vector", lambda e: e.tensor_copy(DROW, bkd[0:64, 128:192]),
               deps=[d11], tag="l0a_drowcp")
    env.sB[6].read(d12)
    env.sDTB.wrote_many([d5b, d6b, d7b])
    env.sD64.wrote_many([d10, d12])
    env.sSC1.wrote_many([d9, d11])

    pq = []
    for t in range(4):
        for which, W, bias, out in (("q", "l0a_wqt", "l0a_bq", QF),
                                    ("k", "l0a_wkt", "l0a_bk", KF)):
            bk = env.bank(t % 2)

            def pgrp(e, W=W, t=t, bk=bk):
                r = None
                for c in range(NCHUNK):
                    r = e.matmul(bk[0:64, :], env.spb(W)[:, 64 * c : 64 * c + 64],
                                 env.XLN[:, c * S + 512 * t : c * S + 512 * t + 512],
                                 start=(c == 0), stop=(c == NCHUNK - 1))
                return r
            mm = P.op("tensor", pgrp, deps=env.sXLN.rdeps() + [env.sSPB.writers] +
                      env.sB[t % 2].wdeps(), tag=f"l0a_p{which}{t}")
            env.sB[t % 2].wrote(mm)
            env.sXLN.read(mm)
            sl = sQF if which == "q" else sKF
            sv = P.op("scalar", lambda e, bk=bk, t=t, out=out, bias=bias:
                      e.activation(out[0:64, 512 * t : 512 * (t + 1)], bk[0:64, :],
                                   AF.Identity, bias=env.spf(bias)),
                      deps=[mm] + (sl.wdeps() if t == 0 else []),
                      tag=f"l0a_e{which}{t}")
            env.sB[t % 2].read(sv)
            sl.read(sv)
            pq.append(sv)
    sQF.wrote_many([o for o in pq[::2]])
    sKF.wrote_many([o for o in pq[1::2]])
    ptm = {"k": [], "v": []}
    for gch in range(16):
        for which, W, bias, out, sl in (("k", "l0a_wkt", "l0a_bk", KT, sKT),
                                        ("v", "l0a_wvt", "l0a_bv", VT, sVT)):
            bk = env.bank(2 + (gch % 2))

            def tgrp(e, W=W, gch=gch, bk=bk):
                r = None
                for c in range(NCHUNK):
                    r = e.matmul(bk[0:128, 0:64],
                                 env.XLN[:, c * S + 128 * gch : c * S + 128 * gch + 128],
                                 env.spb(W)[:, 64 * c : 64 * c + 64],
                                 start=(c == 0), stop=(c == NCHUNK - 1))
                return r
            mm = P.op("tensor", tgrp, deps=env.sXLN.rdeps() + [env.sSPB.writers] +
                      env.sB[2 + gch % 2].wdeps(), tag=f"l0a_t{which}{gch}")
            env.sB[2 + gch % 2].wrote(mm)
            env.sXLN.read(mm)
            sv = P.op("vector", lambda e, bk=bk, gch=gch, out=out, bias=bias:
                      e.tensor_tensor(out[:, 64 * gch : 64 * (gch + 1)],
                                      bk[0:128, 0:64], env.BROW[bias],
                                      ALU.add),
                      deps=[mm, env.sBROW.writers] + (sl.wdeps() if gch == 0 else []),
                      tag=f"l0a_et{which}{gch}")
            env.sB[2 + gch % 2].read(sv)
            sl.read(sv)
            ptm[which].append(sv)
    sKT.wrote_many(ptm["k"])
    sVT.wrote_many(ptm["v"])

    z0 = P.op("vector", lambda e: e.memset(TST[:], 0.0),
              deps=env.sTST.wdeps(), tag="l0a_z0")
    env.sTST.wrote(z0)
    scn_ops = []
    for gch in range(16):
        sl = gch % 2
        APb = env.Dg[:, 2048 + sl * 128 : 2048 + sl * 128 + 128]
        VTb = env.Dg[:, 2560 + sl * 64 : 2560 + sl * 64 + 64]
        VRb = env.Dg[:, 2688 + sl * 64 : 2688 + sl * 64 + 64]
        bka = env.bank(4 + sl)
        mma = P.op("tensor", lambda e, gch=gch, bka=bka: e.matmul(
            bka[:, 0:128], KF[0:64, 128 * gch : 128 * (gch + 1)],
            QF[0:64, 128 * gch : 128 * (gch + 1)], start=True, stop=True),
            deps=sQF.rdeps() + sKF.rdeps() + env.sB[4 + sl].wdeps(),
            tag=f"l0a_A{gch}")
        env.sB[4 + sl].wrote(mma)
        sQF.read(mma)
        sKF.read(mma)
        va = P.op("vector", lambda e, APb=APb, bka=bka: e.tensor_tensor(
            APb, bka[:, 0:128], env.spf("triu"), ALU.mult),
            deps=[mma] + sAP.wdeps(), tag=f"l0a_Am{gch}")
        env.sB[4 + sl].read(va)
        sAP.wrote(va)
        vv = P.op("vector", lambda e, VTb=VTb, gch=gch: e.tensor_tensor(
            VTb, VT[:, 64 * gch : 64 * (gch + 1)], DnegT, ALU.mult),
            deps=sVT.rdeps() + [env.sDTB.writers] + sVTL.wdeps(),
            tag=f"l0a_vt{gch}")
        sVT.read(vv)
        vr = P.op("vector", lambda e, VRb=VRb, gch=gch: e.tensor_tensor(
            VRb, VT[:, 64 * gch : 64 * (gch + 1)], DrevT, ALU.mult),
            deps=sVT.rdeps() + [env.sDTB.writers] + sVTL.wdeps(),
            tag=f"l0a_vr{gch}")
        sVT.read(vr)
        sVTL.wrote_many([vv, vr])
        TT_ = env.Dg[0:64, 2816:2880]
        vt2 = P.op("vector", lambda e, TT_=TT_: e.tensor_tensor(
            TT_, TST[:], DROW, ALU.mult),
            deps=env.sTST.rdeps() + [env.sD64.writers] + env.sSC2.wdeps(),
            tag=f"l0a_ttilde{gch}")
        env.sTST.read(vt2)
        env.sSC2.wrote(vt2)
        bko = env.bank(6)
        mo1 = P.op("tensor", lambda e, gch=gch, bko=bko, TT_=TT_: e.matmul(
            bko[:, 0:64], QF[0:64, 128 * gch : 128 * (gch + 1)], TT_,
            start=True, stop=False), deps=[vt2] + sQF.rdeps() +
            env.sB[6].wdeps(), tag=f"l0a_o1{gch}")
        mo2 = P.op("tensor", lambda e, APb=APb, VTb=VTb, bko=bko: e.matmul(
            bko[:, 0:64], APb, VTb, start=False, stop=True),
            deps=[mo1, va, vv], tag=f"l0a_o2{gch}")
        env.sB[6].wrote(mo2)
        env.sSC2.read(mo1)
        sAP.read(mo2)
        sVTL.read(mo2)
        stm = env.SC2[:, 128:192]
        ve = P.op("vector", lambda e, stm=stm, bko=bko: e.tensor_tensor(
            stm, bko[:, 0:64], DposT, ALU.mult),
            deps=[mo2] + env.sSC2.wdeps(), tag=f"l0a_oscl{gch}")
        env.sB[6].read(ve)
        env.sSC2.wrote(ve)
        bkt = env.bank(7)
        tr = P.op("tensor", lambda e, stm=stm, bkt=bkt: e.transpose(
            bkt[0:64, 0:128], stm, env.spf("ident")),
            deps=[ve] + env.sB[7].wdeps(), tag=f"l0a_tr{gch}")
        env.sB[7].wrote(tr)
        env.sSC2.read(tr)
        ce = P.op("vector", lambda e, gch=gch, bkt=bkt: e.tensor_copy(
            SCN[0:64, 128 * gch : 128 * (gch + 1)], bkt[0:64, 0:128]),
            deps=[tr] + (sSCN.wdeps() if gch == 0 else []), tag=f"l0a_scn{gch}")
        env.sB[7].read(ce)
        scn_ops.append(ce)
        bks = env.bank(4 + sl)
        ms = P.op("tensor", lambda e, gch=gch, bks=bks, VRb=VRb: e.matmul(
            bks[0:64, 0:64], KT[:, 64 * gch : 64 * (gch + 1)], VRb,
            start=True, stop=True), deps=[vr, va] + sKT.rdeps() +
            env.sB[4 + sl].wdeps(), tag=f"l0a_su{gch}")
        env.sB[4 + sl].wrote(ms)
        sKT.read(ms)
        sVTL.read(ms)
        def upd(e, bks=bks):
            e.tensor_tensor(TST[:], TST[:], DCOL, ALU.mult)
            e.drain()
            return e.tensor_tensor(TST[:], TST[:], bks[0:64, 0:64], ALU.add)
        vu = P.op("vector", upd, deps=[ms, vt2, mo1] + env.sTST.wdeps(),
                  tag=f"l0a_tupd{gch}")
        env.sB[4 + sl].read(vu)
        env.sTST.wrote(vu)
    sSCN.wrote_many(scn_ops)

    for t in range(4):
        for oc in range(NCHUNK):
            bk = env.bank(t % 2)
            mm = P.op("tensor", lambda e, t=t, oc=oc, bk=bk: e.matmul(
                bk, env.spb("l0a_wot")[0:64, 128 * oc : 128 * (oc + 1)],
                SCN[0:64, 512 * t : 512 * (t + 1)], start=True, stop=True),
                deps=sSCN.rdeps() + env.sB[t % 2].wdeps(), tag=f"l0a_om{t}_{oc}")
            env.sB[t % 2].wrote(mm)
            sSCN.read(mm)
            sv = P.op("scalar", lambda e, bk=bk, oc=oc: e.activation(
                env.SC1[:, 0:512], bk, AF.Identity,
                bias=env.spf("l0a_bo", w=1, coff=oc)),
                deps=[mm] + env.sSC1.wdeps(), tag=f"l0a_ob{t}_{oc}")
            env.sB[t % 2].read(sv)
            env.sSC1.wrote(sv)
            xsl = slice(oc * S + 512 * t, oc * S + 512 * t + 512)
            def xadd(e, xsl=xsl):
                e.tensor_tensor(env.SC1[:, 512:1024], env.SC1[:, 0:512],
                                env.XLN[:, xsl], ALU.add)
                e.drain()
                return e.tensor_tensor(env.X[:, xsl], env.X[:, xsl],
                                       env.SC1[:, 512:1024], ALU.add)
            vx = P.op("vector", xadd, deps=[sv] + env.sXLN.rdeps() +
                      env.sX.wdeps() + env.sX.readers, tag=f"l0a_x{t}_{oc}")
            env.sSC1.read(vx)
            env.sXLN.read(vx)
            env.sX.wrote(vx)


_L1N = [0]


def emit_l1_ffn(env):
    P = env.P
    nc = env.nc
    call = _L1N[0]
    _L1N[0] += 1
    emit_ln(env)
    DgF = env.Dg.ap().bitcast(F32)

    if call == 0:
        if not hasattr(env, "TH"):
            env.TH = nc.alloc_sbuf_tensor("TH", [128, 4], F32)
            env.sTH = Slot("TH")
        for wi, (wh, nsl, nelem) in enumerate(
                ((env.l1w1_h, 32, 4096 * 1024), (env.l1w2_h, 8, 4096 * 1024))):
            ACC = env.SC1[:, 0:1]
            accops = []
            z0 = P.op("vector", lambda e, ACC=ACC: e.memset(ACC, 0.0),
                      deps=env.sSC1.wdeps(), tag=f"l1s_z{wi}")
            env.sSC1.wrote(z0)
            for s_i in range(nsl):
                slb = s_i % 2
                wcols = 1024 if wi == 0 else 4096
                FB = DgF[:, slb * 4096 : slb * 4096 + 1024] if wi == 0 else                     DgF[:, 0:4096]
                sFB = env.sD[slb if wi == 0 else 0]
                ld = P.dma("sync", lambda e, wh=wh, s_i=s_i, FB=FB: e.dma_start(
                    out=FB, in_=wh[s_i]), deps=sFB.wdeps(), tag=f"l1s_ld{wi}_{s_i}")
                sFB.wrote(ld)
                tcol = 1 if s_i % 2 == 0 else 3
                def redfn(e, FB=FB, ACC=ACC, tcol=tcol):
                    e.drain()
                    e.reduce_sum(env.SC1[:, tcol : tcol + 1], FB, axis=AX.X,
                                 apply_absolute_value=True)
                    e.drain()
                    return e.tensor_tensor(ACC, ACC, env.SC1[:, tcol : tcol + 1],
                                           ALU.add)
                rd = P.op("vector", redfn, deps=[ld, z0] + env.sSC1.wdeps(),
                          tag=f"l1s_r{wi}_{s_i}")
                sFB.read(rd)
                env.sSC1.wrote(rd)
                accops.append(rd)
            bk = env.bank(6)
            m1 = P.op("tensor", lambda e, bk=bk, ACC=ACC: e.matmul(
                bk[0:1, 0:1], env.spf("ones_col"), ACC, start=True, stop=True),
                deps=accops + env.sB[6].wdeps(), tag=f"l1s_t{wi}")
            env.sB[6].wrote(m1)
            env.sSC1.read(m1)
            def thfn(e, bk=bk, nelem=nelem):
                e.tensor_scalar(env.SC1[0:1, 2:3], bk[0:1, 0:1], 1.0 / nelem,
                                1e-8, ALU.mult, ALU.max)
                e.drain()
                return e.tensor_scalar(env.SC1[0:1, 2:3], env.SC1[0:1, 2:3],
                                       0.5, None, ALU.mult)
            t1 = P.op("vector", thfn, deps=[m1] + env.sSC1.wdeps(), tag=f"l1s_h{wi}")
            env.sB[6].read(t1)
            env.sSC1.wrote(t1)
            b1_ = P.op("tensor", lambda e, bk=bk: e.matmul(
                bk[:, 0:1], env.spf("ones_row"), env.SC1[0:1, 2:3],
                start=True, stop=True), deps=[t1] + env.sB[6].wdeps(),
                tag=f"l1s_b{wi}")
            env.sB[6].wrote(b1_)
            env.sSC1.read(b1_)
            c1 = P.op("vector", lambda e, bk=bk, wi=wi: e.tensor_copy(
                env.TH[:, wi : wi + 1], bk[:, 0:1]),
                deps=[b1_] + env.sTH.wdeps(), tag=f"l1s_c{wi}")
            env.sB[6].read(c1)
            env.sTH.writers.append(c1)
        if DEBUG_TAPS:
            P.tap(nc, "l1th", env.TH[0:1, 0:4], deps=list(env.sTH.writers))

    def ternarize(s_i, wh, wbh, thcol, cols, tag):
        slb = s_i % 2
        if cols == 1024:
            FB = DgF[:, slb * 4096 : slb * 4096 + cols]
            sFB = env.sD[slb]
            TB = env.E[:, slb * 1024 : slb * 1024 + cols]
            sTB = env.sE[slb]
        else:
            FB = DgF[:, 0:4096]
            sFB = env.sD[0]
            TB = env.C[:, slb * 4096 : slb * 4096 + 4096]
            sTB = env.sC[slb]
        ld = P.dma("gpsimd", lambda e, wh=wh, s_i=s_i, FB=FB: e.dma_start(
            out=FB, in_=wh[s_i]), deps=sFB.wdeps(), tag=f"{tag}ld{s_i}")
        sFB.wrote(ld)
        v1s = []
        tb_guard = sTB.wdeps()
        for h0 in range(0, cols, 2048):
            w_ = min(2048, cols - h0)
            FBh = FB[:, h0 : h0 + w_]
            TBh = TB[:, h0 : h0 + w_]
            SG = env.SC2[:, 0:w_]
            s1 = P.op("scalar", lambda e, FBh=FBh, SG=SG: e.activation(
                SG, FBh, AF.Sign), deps=[ld] + env.sSC2.wdeps(),
                tag=f"{tag}sg{s_i}_{h0}")
            env.sSC2.wrote(s1)
            sFB.read(s1)
            s2 = P.op("scalar", lambda e, FBh=FBh: e.activation(FBh, FBh, AF.Abs),
                      deps=[ld, s1], tag=f"{tag}ab{s_i}_{h0}")
            sFB.read(s2)
            def mkt(e, FBh=FBh, SG=SG, TBh=TBh, thcol=thcol):
                e.tensor_scalar(FBh, FBh, env.TH[:, thcol : thcol + 1], None,
                                ALU.is_gt)
                e.drain()
                return e.tensor_tensor(TBh, SG, FBh, ALU.mult)
            v1 = P.op("vector", mkt, deps=[s2, s1, env.sTH.writers] + tb_guard,
                      tag=f"{tag}tn{s_i}_{h0}")
            sFB.read(v1)
            env.sSC2.read(v1)
            v1s.append(v1)
        sFB.wrote_many(v1s)
        sTB.wrote_many(v1s)
        v1 = v1s
        st = P.dma("sync", lambda e, wbh=wbh, s_i=s_i, TB=TB: e.dma_start(
            out=wbh[s_i], in_=TB), deps=[v1], tag=f"{tag}st{s_i}")
        sTB.read(st)
        env.tern_st = getattr(env, "tern_st", [])
        env.tern_st.append(st)
        return TB, sTB, [v1, st]

    h_store = []
    for hc in range(32):
        if call == 0:
            W1S, sW1, rdy = ternarize(hc, env.l1w1_h, env.l1w1b_h, 0, 1024, "l1a")
        else:
            slb = hc % 2
            W1S = env.E[:, slb * 1024 : slb * 1024 + 1024]
            sW1 = env.sE[slb]
            ld = P.dma("gpsimd", lambda e, hc=hc, W1S=W1S: e.dma_start(
                out=W1S, in_=env.l1w1b_h[hc]),
                deps=sW1.wdeps() + getattr(env, "tern_st", []),
                tag=f"l1w1ld_{hc}")
            sW1.wrote(ld)
            rdy = [ld]
        hsl = 2 + (hc % 2)
        HC = env.E[:, 2048 + (hc % 2) * 2048 : 2048 + (hc % 2) * 2048 + 2048]
        sHC = env.sE[hsl]
        hc_guard = sHC.wdeps()
        silus = []
        for t in range(4):
            bk = env.bank(t % 2)

            def w1grp(e, W1S=W1S, t=t, bk=bk):
                r = None
                for c in range(NCHUNK):
                    r = e.matmul(bk, W1S[:, 128 * c : 128 * (c + 1)],
                                 env.XLN[:, c * S + 512 * t : c * S + 512 * t + 512],
                                 start=(c == 0), stop=(c == NCHUNK - 1))
                return r
            mm = P.op("tensor", w1grp, deps=rdy + env.sXLN.rdeps() +
                      env.sB[t % 2].wdeps(), tag=f"l1w1mm_{call}_{hc}_{t}")
            env.sB[t % 2].wrote(mm)
            env.sXLN.read(mm)
            sW1.read(mm)
            b1ap = env.spf("l1_b1", w=1, coff=hc)
            sv = P.op("scalar", lambda e, t=t, bk=bk, HC=HC, b1ap=b1ap:
                      e.activation(HC[:, 512 * t : 512 * (t + 1)], bk, AF.Gelu,
                                   bias=b1ap),
                      deps=[mm] + hc_guard, tag=f"l1w1g_{call}_{hc}_{t}")
            env.sB[t % 2].read(sv)
            silus.append(sv)
        sHC.wrote_many(silus)
        std = P.dma("sync", lambda e, hc=hc, HC=HC: e.dma_start(
            out=env.hdram_h[:, :, 512 * hc : 512 * (hc + 1)].transpose([1, 0, 2]),
            in_=HC[:].rearrange("p (t w) -> p t w", t=4)),
            deps=silus, tag=f"l1hst_{call}_{hc}")
        sHC.read(std)
        h_store.append(std)

    if call == 0:
        for och in range(8):
            ternarize(och, env.l1w2_h, env.l1w2b_h, 1, 4096, "l1b")

    for t in range(4):
        hdeps = []
        for hh in range(2):
            hs = env.sD[hh]
            HT = env.Dg[:, hh * 8192 : hh * 8192 + 8192]
            ldh = P.dma("sync", lambda e, t=t, hh=hh, HT=HT: e.dma_start(
                out=HT, in_=env.hdram_h[t, :, 8192 * hh : 8192 * (hh + 1)]),
                deps=hs.wdeps() + h_store, tag=f"l1htld_{call}_{t}_{hh}")
            hs.wrote(ldh)
            hdeps.append(ldh)
        for och in range(8):
            qs = och % 2
            W2S = env.C[:, qs * 4096 : qs * 4096 + 4096]
            sW2 = env.sC[qs]
            ldw2 = P.dma("gpsimd", lambda e, och=och, W2S=W2S: e.dma_start(
                out=W2S, in_=env.l1w2b_h[och]),
                deps=sW2.wdeps() + getattr(env, "tern_st", []),
                tag=f"l1w2ld_{call}_{t}_{och}")
            sW2.wrote(ldw2)
            rdy2 = [ldw2]
            bk = env.bank(2 + och % 2)

            def w2grp(e, W2S=W2S, bk=bk):
                r = None
                for hcc in range(32):
                    r = e.matmul(
                        bk, W2S[:, 128 * hcc : 128 * (hcc + 1)],
                        env.Dg[:, (hcc // 16) * 8192 + (hcc % 16) * 512 :
                               (hcc // 16) * 8192 + (hcc % 16) * 512 + 512],
                        start=(hcc == 0), stop=(hcc == 31))
                return r
            mm = P.op("tensor", w2grp, deps=rdy2 + hdeps +
                      env.sB[2 + och % 2].wdeps(), tag=f"l1w2mm_{call}_{t}_{och}")
            env.sB[2 + och % 2].wrote(mm)
            sW2.read(mm)
            env.sD[0].read(mm)
            env.sD[1].read(mm)
            sv = P.op("scalar", lambda e, bk=bk, och=och: e.activation(
                env.SC1[:, 512:1024], bk, AF.Identity,
                bias=env.spf("l1_b2", w=1, coff=och)),
                deps=[mm] + env.sSC1.wdeps(), tag=f"l1y_{call}_{t}_{och}")
            env.sB[2 + och % 2].read(sv)
            env.sSC1.wrote(sv)
            xsl = slice(och * S + 512 * t, och * S + 512 * t + 512)
            vx = P.op("vector", lambda e, xsl=xsl: e.scalar_tensor_tensor(
                env.X[:, xsl], env.SC1[:, 512:1024], 0.5, env.X[:, xsl],
                ALU.mult, ALU.add),
                deps=[sv] + env.sX.wdeps() + env.sX.readers, tag=f"l1x_{call}_{t}_{och}")
            env.sSC1.read(vx)
            env.sX.wrote(vx)


def emit_l1_attn(env):
    P = env.P
    nc = env.nc
    emit_ln(env)

    SMF = env.E[:, 0:256]
    KS = env.E[:, 256:512]
    VS = env.E[:, 512:768]
    VTS = env.E[0:32, 768:1792]
    WB0 = env.E[:, 2048:3072]
    WB1 = env.E[:, 3072:4096]
    QS = env.Dg
    HB = env.C[:, 0:4096]
    sSM, sKS, sVS, sVTS = Slot("SM"), Slot("KS"), Slot("VS"), Slot("VTS")
    sQS = env.sD[0]
    sHB = env.sC[0]
    sWB = [env.sE[0], env.sE[1]]

    smops = []
    smg = sSM.wdeps() + [x for s_ in env.sE for x in s_.wdeps()]
    for c in range(NCHUNK):
        def smfn(e, c=c):
            e.tensor_copy(SMF[:, 32 * c : 32 * c + 1],
                          env.XLN[:, c * S : c * S + 1])
            e.reduce_sum(env.SC2[:, 0:31],
                         env.XLN[:, c * S + 1 : c * S + 1985].rearrange(
                             "p (w i) -> p w i", i=64), axis=AX.X)
            e.drain()
            return e.tensor_scalar(SMF[:, 32 * c + 1 : 32 * c + 32],
                                   env.SC2[:, 0:31],
                                   1.0 / 64.0, None, ALU.mult)
        o = P.op("vector", smfn, deps=env.sXLN.rdeps() + smg +
                 env.sSC2.wdeps(), tag=f"l1s_sm{c}")
        env.sXLN.read(o)
        env.sSC2.wrote(o)
        smops.append(o)
    sSM.wrote_many(smops)

    for which, wh, bias, out, sl, scale in (
            ("k", env.l1wk_h, "l1a_bk", KS, sKS, "c_inv8"),
            ("v", env.l1wv_h, "l1a_bv", VS, sVS, None)):
        for oc in range(NCHUNK):
            slb = oc % 2
            WB = (WB0, WB1)[slb]
            ld = P.dma("gpsimd", lambda e, wh=wh, oc=oc, WB=WB: e.dma_start(
                out=WB, in_=wh[oc]), deps=sWB[slb].wdeps(), tag=f"l1a_{which}ld{oc}")
            sWB[slb].wrote(ld)
            bk = env.bank(6)

            def kgrp(e, WB=WB, bk=bk):
                r = None
                for c in range(NCHUNK):
                    r = e.matmul(bk[:, 0:32], WB[:, 128 * c : 128 * (c + 1)],
                                 SMF[:, 32 * c : 32 * (c + 1)],
                                 start=(c == 0), stop=(c == NCHUNK - 1))
                return r
            mm = P.op("tensor", kgrp, deps=[ld] + sSM.rdeps() + env.sB[6].wdeps(),
                      tag=f"l1a_{which}mm{oc}")
            env.sB[6].wrote(mm)
            sSM.read(mm)
            sWB[slb].read(mm)
            kw = dict(bias=env.spf(bias, w=1, coff=oc))
            if scale:
                kw["scale"] = env.spf(scale)
            sv = P.op("scalar", lambda e, bk=bk, oc=oc, out=out, kw=kw:
                      e.activation(out[:, 32 * oc : 32 * (oc + 1)], bk[:, 0:32],
                                   AF.Identity, **kw),
                      deps=[mm] + (sl.wdeps() if oc == 0 else []),
                      tag=f"l1a_{which}e{oc}")
            env.sB[6].read(sv)
            sl.read(sv)
        sl.wrote_many(sl.readers[-NCHUNK:])
    vtops = []
    for h in range(16):
        bp = 64 * (h % 2)
        col = 32 * (h // 2)
        def trfn(e, h=h, bp=bp, col=col):
            e.transpose(VTS[0:32, 64 * h : 64 * h + 32],
                        VS[bp : bp + 32, col : col + 32])
            e.drain()
            return e.transpose(VTS[0:32, 64 * h + 32 : 64 * h + 64],
                               VS[bp + 32 : bp + 64, col : col + 32])
        o = P.op("vector", trfn, deps=sVS.rdeps() + sVTS.wdeps(), tag=f"l1a_vt{h}")
        sVS.read(o)
        vtops.append(o)
    sVTS.wrote_many(vtops)

    qops = []
    for oc in range(NCHUNK):
        slb = oc % 2
        WB = (WB0, WB1)[slb]
        ld = P.dma("gpsimd", lambda e, oc=oc, WB=WB: e.dma_start(
            out=WB, in_=env.l1wq_h[oc]), deps=sWB[slb].wdeps(), tag=f"l1a_qld{oc}")
        sWB[slb].wrote(ld)
        for t in range(4):
            bk = env.bank(t % 2)

            def qgrp(e, WB=WB, t=t, bk=bk):
                r = None
                for c in range(NCHUNK):
                    r = e.matmul(bk, WB[:, 128 * c : 128 * (c + 1)],
                                 env.XLN[:, c * S + 512 * t : c * S + 512 * t + 512],
                                 start=(c == 0), stop=(c == NCHUNK - 1))
                return r
            mm = P.op("tensor", qgrp, deps=[ld] + env.sXLN.rdeps() +
                      env.sB[t % 2].wdeps(), tag=f"l1a_qmm{oc}_{t}")
            env.sB[t % 2].wrote(mm)
            env.sXLN.read(mm)
            sWB[slb].read(mm)
            sv = P.op("scalar", lambda e, bk=bk, oc=oc, t=t: e.activation(
                QS[:, oc * S + 512 * t : oc * S + 512 * t + 512], bk, AF.Identity,
                bias=env.spf("l1a_bq", w=1, coff=oc)),
                deps=[mm] + (sQS.wdeps() if oc == 0 and t == 0 else []),
                tag=f"l1a_qe{oc}_{t}")
            env.sB[t % 2].read(sv)
            sQS.read(sv)
            qops.append(sv)
    sQS.wrote_many(qops)

    for t in range(4):
        hb_ops = []
        hb_guard = sHB.wdeps()
        for h in range(16):
            bp = 64 * (h % 2)
            qcol = (h // 2) * S + 512 * t
            bkS = env.bank(4)
            mmS = P.op("tensor", lambda e, h=h, bp=bp, qcol=qcol, bkS=bkS:
                       e.matmul(bkS[0:32, :], KS[bp : bp + 64, 32 * (h // 2) :
                                                 32 * (h // 2) + 32],
                                QS[bp : bp + 64, qcol : qcol + 512],
                                start=True, stop=True),
                       deps=sKS.rdeps() + sQS.rdeps() + env.sB[4].wdeps(),
                       tag=f"l1a_s{t}_{h}")
            env.sB[4].wrote(mmS)
            sKS.read(mmS)
            sQS.read(mmS)
            SCR = env.SC2[0:32, 0:512]
            c1 = P.op("vector", lambda e, bkS=bkS, SCR=SCR: e.tensor_copy(
                SCR, bkS[0:32, :]), deps=[mmS] + env.sSC2.wdeps(),
                tag=f"l1a_sc{t}_{h}")
            env.sB[4].read(c1)
            env.sSC2.wrote(c1)
            EB = env.SC2[0:32, 512:1024]
            s1 = P.op("scalar", lambda e, SCR=SCR, EB=EB: e.activation(
                EB, SCR, AF.Exp), deps=[c1], tag=f"l1a_exp{t}_{h}")
            env.sSC2.wrote(s1)
            g1 = P.op("gpsimd", lambda e, t=t, EB=EB: e.affine_select(
                EB, EB, pattern=[[1, 512]], compare_op=ALU.is_ge,
                fill=0.0, base=512 * t, channel_multiplier=-64),
                deps=[s1], tag=f"l1a_msk{t}_{h}")
            env.sSC2.wrote(g1)
            EBb = env.SC2.ap().bitcast(BF16)[0:32, 2048:2560]
            c2 = P.op("vector", lambda e, EB=EB, EBb=EBb: e.tensor_copy(EBb, EB),
                      deps=[g1], tag=f"l1a_ebb{t}_{h}")
            env.sSC2.wrote(c2)
            bkR = env.bank(5)
            mmR = P.op("tensor", lambda e, EBb=EBb, bkR=bkR: e.matmul(
                bkR[0:1, :], env.spb("ones_colb", p=32), EBb, start=True,
                stop=True), deps=[c2] + env.sB[5].wdeps(), tag=f"l1a_cs{t}_{h}")
            env.sB[5].wrote(mmR)
            env.sSC2.read(mmR)
            RR = env.SC1[0:1, 0:512]
            def rcp(e, bkR=bkR, RR=RR):
                e.tensor_copy(RR, bkR[0:1, :])
                e.drain()
                return e.reciprocal(RR, RR)
            v1 = P.op("vector", rcp, deps=[mmR] + env.sSC1.wdeps(),
                      tag=f"l1a_rcp{t}_{h}")
            env.sB[5].read(v1)
            env.sSC1.wrote(v1)
            bkA = env.bank(6)
            mmA = P.op("tensor", lambda e, h=h, EBb=EBb, bkA=bkA: e.matmul(
                bkA[0:64, :], VTS[0:32, 64 * h : 64 * (h + 1)], EBb,
                start=True, stop=True),
                deps=[c2] + sVTS.rdeps() + env.sB[6].wdeps(), tag=f"l1a_av{t}_{h}")
            env.sB[6].wrote(mmA)
            sVTS.read(mmA)
            env.sSC2.read(mmA)
            bkB = env.bank(7)
            mmB = P.op("tensor", lambda e, RR=RR, bkB=bkB: e.matmul(
                bkB[0:64, :], env.spf("ones_row", w=64), RR, start=True,
                stop=True), deps=[v1] + env.sB[7].wdeps(), tag=f"l1a_bc{t}_{h}")
            env.sB[7].wrote(mmB)
            env.sSC1.read(mmB)
            hcol = 512 * (h % 8) if False else 0
            dst = HB[bp : bp + 64, 512 * (h // 2) : 512 * (h // 2) + 512]
            RB = env.SC2[0:64, 1280:1792]
            cpA = P.op("scalar", lambda e, bkA=bkA, RB=RB: e.activation(
                RB, bkA[0:64, :], AF.Identity),
                deps=[mmA] + env.sSC2.wdeps(), tag=f"l1a_av_ev{t}_{h}")
            env.sB[6].read(cpA)
            env.sSC2.wrote(cpA)
            vh = P.op("vector", lambda e, dst=dst, RB=RB, bkB=bkB:
                      e.tensor_tensor(dst, RB, bkB[0:64, :], ALU.mult),
                      deps=[cpA, mmB] + hb_guard, tag=f"l1a_hb{t}_{h}")
            env.sB[7].read(vh)
            env.sSC2.read(vh)
            sHB.read(vh)
            hb_ops.append(vh)
        sHB.wrote_many(hb_ops)
        for oc in range(NCHUNK):
            slb = oc % 2
            WB = (WB0, WB1)[slb]
            ld = P.dma("gpsimd", lambda e, oc=oc, WB=WB: e.dma_start(
                out=WB, in_=env.l1wo_h[oc]), deps=sWB[slb].wdeps(),
                tag=f"l1a_old{t}_{oc}")
            sWB[slb].wrote(ld)
            bk = env.bank(t % 2)

            def ogrp(e, WB=WB, bk=bk):
                r = None
                for c in range(NCHUNK):
                    r = e.matmul(bk, WB[:, 128 * c : 128 * (c + 1)],
                                 HB[:, 512 * c : 512 * (c + 1)],
                                 start=(c == 0), stop=(c == NCHUNK - 1))
                return r
            mm = P.op("tensor", ogrp, deps=[ld] + sHB.rdeps() +
                      env.sB[t % 2].wdeps(), tag=f"l1a_om{t}_{oc}")
            env.sB[t % 2].wrote(mm)
            sHB.read(mm)
            sWB[slb].read(mm)
            sv = P.op("scalar", lambda e, bk=bk, oc=oc: e.activation(
                env.SC1[:, 512:1024], bk, AF.Identity,
                bias=env.spf("l1a_bo", w=1, coff=oc)),
                deps=[mm] + env.sSC1.wdeps(), tag=f"l1a_ob{t}_{oc}")
            env.sB[t % 2].read(sv)
            env.sSC1.wrote(sv)
            xsl = slice(oc * S + 512 * t, oc * S + 512 * t + 512)
            vx = P.op("vector", lambda e, xsl=xsl: e.tensor_tensor(
                env.X[:, xsl], env.X[:, xsl], env.SC1[:, 512:1024], ALU.add),
                deps=[sv] + env.sX.wdeps() + env.sX.readers, tag=f"l1a_x{t}_{oc}")
            env.sSC1.read(vx)
            env.sX.wrote(vx)


_L2N = [0]
_L3N = [0]


def emit_l2_ffn(env):
    P = env.P
    nc = env.nc
    call = _L2N[0]
    _L2N[0] += 1
    emit_ln(env)

    wtm_ops = []
    wtm_guard = env.sWTM.wdeps()
    for j in range(NTTILE):
        bk = env.bank(5)

        def rgrp(e, j=j, bk=bk):
            r = None
            for c in range(NCHUNK):
                r = e.matmul(bk[0:128, 0:16],
                             env.XLN[:, c * S + 128 * j : c * S + 128 * j + 128],
                             env.spb("l2f_gwt")[:, c * 16 : c * 16 + 16],
                             start=(c == 0), stop=(c == NCHUNK - 1))
            return r
        mm = P.op("tensor", rgrp, deps=env.sXLN.rdeps() + [env.sSPB.writers] +
                  env.sB[5].wdeps(), tag=f"l2r_mm{j}")
        env.sB[5].wrote(mm)
        env.sXLN.read(mm)
        LG = env.SC1[:, 0:16]
        M1 = env.SC1[:, 16:17]
        M2 = env.SC1[:, 17:18]
        P1 = env.SC1[:, 18:19]
        P2 = env.SC1[:, 19:20]
        AA = env.SC1[:, 20:21]
        GE1 = env.SC1[:, 32:48]
        GE2 = env.SC1[:, 48:64]
        WW = env.SC1[:, 64:80]
        g = env.sSC1.wdeps()
        v0 = P.op("vector", lambda e, bk=bk: e.tensor_copy(LG, bk[0:128, 0:16]),
                  deps=[mm] + g, tag=f"l2r_lg{j}")
        env.sB[5].read(v0)
        v1 = P.op("vector", lambda e: e.reduce_max(M1, LG, axis=AX.X),
                  deps=[v0], tag=f"l2r_m1{j}")
        v2 = P.op("vector", lambda e: e.tensor_scalar(GE1, LG, M1, None, ALU.is_ge),
                  deps=[v1], tag=f"l2r_ge1{j}")
        v3 = P.op("vector", lambda e: e.scalar_tensor_tensor(
            GE2, GE1, -1e30, LG, ALU.mult, ALU.add), deps=[v2], tag=f"l2r_msk{j}")
        v4 = P.op("vector", lambda e: e.reduce_max(M2, GE2, axis=AX.X),
                  deps=[v3], tag=f"l2r_m2{j}")
        def pfn(e):
            e.tensor_tensor(P1, M1, M2, ALU.subtract)
            return None
        v5 = P.op("vector", pfn, deps=[v4], tag=f"l2r_d{j}")
        s1 = P.op("scalar", lambda e: e.activation(P1, P1, AF.Sigmoid),
                  deps=[v5], tag=f"l2r_sg{j}")
        def wfn(e):
            e.tensor_scalar(P2, P1, -1.0, 1.0, ALU.mult, ALU.add)
            e.tensor_tensor(AA, P1, P2, ALU.subtract)
            e.drain()
            e.tensor_scalar(GE2, LG, M2, None, ALU.is_ge)
            e.drain()
            e.tensor_scalar(GE2, GE2, P2, None, ALU.mult)
            e.tensor_scalar(GE1, GE1, AA, None, ALU.mult)
            e.drain()
            return e.tensor_tensor(WW, GE1, GE2, ALU.add)
        v6 = P.op("vector", wfn, deps=[s1, v2], tag=f"l2r_w{j}")
        env.sSC1.wrote(v6)
        bkt = env.bank(4)
        t1 = P.op("tensor", lambda e, bkt=bkt: e.transpose(
            bkt[0:16, 0:128], WW, env.spf("ident")),
            deps=[v6] + env.sB[4].wdeps() + [env.sSPF.writers], tag=f"l2r_tr{j}")
        env.sB[4].wrote(t1)
        env.sSC1.read(t1)
        c1 = P.op("vector", lambda e, j=j, bkt=bkt: e.tensor_copy(
            env.WTM[0:16, 128 * j : 128 * (j + 1)], bkt[0:16, 0:128]),
            deps=[t1] + wtm_guard, tag=f"l2r_cp{j}")
        env.sB[4].read(c1)
        wtm_ops.append(c1)
    env.sWTM.wrote_many(wtm_ops)

    dx = P.dma("sync", lambda e: e.dma_start(out=env.xdump_h[:], in_=env.X[:]),
               deps=env.sX.rdeps(), tag=f"l2f_dumpx{call}")
    env.sX.read(dx)
    YACC = env.X

    h_store = []
    for hc in range(16):
        slb = hc % 2
        W1S = env.E[:, slb * 1024 : slb * 1024 + 1024]
        W2S = env.E[:, 2048 + slb * 1024 : 2048 + slb * 1024 + 1024]
        sW1 = env.sE[slb]
        sW2 = env.sE[2 + slb]
        ld1 = P.dma("gpsimd", lambda e, hc=hc, W1S=W1S: e.dma_start(
            out=W1S, in_=env.l2sf1_h[hc]), deps=sW1.wdeps(), tag=f"l2s1ld{call}_{hc}")
        sW1.wrote(ld1)
        ld2 = P.dma("gpsimd", lambda e, hc=hc, W2S=W2S: e.dma_start(
            out=W2S, in_=env.l2sf2_h[hc]), deps=sW2.wdeps(), tag=f"l2s2ld{call}_{hc}")
        sW2.wrote(ld2)
        GC = env.C[:, (hc % 2) * 2048 : (hc % 2) * 2048 + 2048]
        sGC = env.sC[hc % 2]
        gc_guard = sGC.wdeps()
        gops = []
        for t in range(4):
            bka = env.bank(t % 2)
            bkb = env.bank(2 + t % 2)

            def agrp(e, W1S=W1S, t=t, bka=bka):
                r = None
                for c in range(NCHUNK):
                    r = e.matmul(bka, W1S[:, 128 * c : 128 * (c + 1)],
                                 env.XLN[:, c * S + 512 * t : c * S + 512 * t + 512],
                                 start=(c == 0), stop=(c == NCHUNK - 1))
                return r
            mma = P.op("tensor", agrp, deps=[ld1] + env.sXLN.rdeps() +
                       env.sB[t % 2].wdeps(), tag=f"l2sa{call}_{hc}_{t}")
            env.sB[t % 2].wrote(mma)
            env.sXLN.read(mma)
            sW1.read(mma)

            def bgrp(e, W2S=W2S, t=t, bkb=bkb):
                r = None
                for c in range(NCHUNK):
                    r = e.matmul(bkb, W2S[:, 128 * c : 128 * (c + 1)],
                                 env.XLN[:, c * S + 512 * t : c * S + 512 * t + 512],
                                 start=(c == 0), stop=(c == NCHUNK - 1))
                return r
            mmb = P.op("tensor", bgrp, deps=[ld2] + env.sXLN.rdeps() +
                       env.sB[2 + t % 2].wdeps(), tag=f"l2sb{call}_{hc}_{t}")
            env.sB[2 + t % 2].wrote(mmb)
            env.sXLN.read(mmb)
            sW2.read(mmb)
            sv = P.op("scalar", lambda e, t=t, bka=bka: e.activation(
                env.SC1[:, 0:512], bka, AF.Silu),
                deps=[mma] + env.sSC1.wdeps(), tag=f"l2ss{call}_{hc}_{t}")
            env.sB[t % 2].read(sv)
            env.sSC1.wrote(sv)
            vg = P.op("vector", lambda e, t=t, GC=GC, bkb=bkb: e.tensor_tensor(
                GC[:, 512 * t : 512 * (t + 1)], env.SC1[:, 0:512], bkb, ALU.mult),
                deps=[sv, mmb] + gc_guard, tag=f"l2sg{call}_{hc}_{t}")
            env.sB[2 + t % 2].read(vg)
            env.sSC1.read(vg)
            sGC.read(vg)
            gops.append(vg)
        sGC.wrote_many(gops)
        std = P.dma("sync", lambda e, hc=hc, GC=GC: e.dma_start(
            out=env.hdram_h[:, :, 512 * hc : 512 * (hc + 1)].transpose([1, 0, 2]),
            in_=GC[:].rearrange("p (t w) -> p t w", t=4)),
            deps=gops, tag=f"l2gst{call}_{hc}")
        sGC.read(std)
        h_store.append(std)
    for t in range(4):
        hdeps = []
        hs = env.sD[0]
        HT = env.Dg[:, 0:8192]
        ldh = P.dma("sync", lambda e, t=t, HT=HT: e.dma_start(
            out=HT, in_=env.hdram_h[t, :, 0:8192]),
            deps=hs.wdeps() + h_store, tag=f"l2gld{call}_{t}")
        hs.wrote(ldh)
        hdeps.append(ldh)
        for och in range(NCHUNK):
            slb = och % 2
            W3S = env.E[:, slb * 2048 : slb * 2048 + 2048]
            if slb == 0:
                w3slots = [env.sE[0], env.sE[1]]
            else:
                w3slots = [env.sE[2]]
            ld3 = P.dma("gpsimd", lambda e, och=och, W3S=W3S: e.dma_start(
                out=W3S, in_=env.l2sf3_h[och]),
                deps=[d for s_ in w3slots for d in s_.wdeps()],
                tag=f"l2s3ld{call}_{t}_{och}")
            for s_ in w3slots:
                s_.wrote(ld3)
            sW3 = w3slots[0]
            bk = env.bank(4 + och % 2)

            def w3grp(e, W3S=W3S, bk=bk):
                r = None
                for hcc in range(16):
                    r = e.matmul(bk, W3S[:, 128 * hcc : 128 * (hcc + 1)],
                                 env.Dg[:, 512 * hcc : 512 * hcc + 512],
                                 start=(hcc == 0), stop=(hcc == 15))
                return r
            mm = P.op("tensor", w3grp, deps=[ld3] + hdeps +
                      env.sB[4 + och % 2].wdeps(), tag=f"l2s3m{call}_{t}_{och}")
            env.sB[4 + och % 2].wrote(mm)
            for s_ in w3slots:
                s_.read(mm)
            hs.read(mm)
            ysl = slice(och * S + 512 * t, och * S + 512 * t + 512)
            yop = P.op("vector", lambda e, ysl=ysl, bk=bk: e.tensor_copy(
                YACC[:, ysl], bk), deps=[mm, dx] + env.sX.wdeps(),
                tag=f"l2sy{call}_{t}_{och}")
            env.sB[4 + och % 2].read(yop)
            env.sX.wrote(yop)

    for e_i in range(16):
        HE = env.Dg[:, 8192:16384]
        sHE = env.sD[1]
        heops = []
        he_guard = sHE.wdeps()
        for hc in range(4):
            slb = hc % 2
            W1S = env.E[:, slb * 1024 : slb * 1024 + 1024]
            sW1 = env.sE[slb]
            ld = P.dma("gpsimd", lambda e, e_i=e_i, hc=hc, W1S=W1S: e.dma_start(
                out=W1S, in_=env.l2fw1_h[e_i, hc]), deps=sW1.wdeps(),
                tag=f"l2w1ld{call}_{e_i}_{hc}")
            sW1.wrote(ld)
            for t in range(4):
                bk = env.bank(t % 2)

                def w1grp(e, W1S=W1S, t=t, bk=bk):
                    r = None
                    for c in range(NCHUNK):
                        r = e.matmul(bk, W1S[:, 128 * c : 128 * (c + 1)],
                                     env.XLN[:, c * S + 512 * t : c * S + 512 * t + 512],
                                     start=(c == 0), stop=(c == NCHUNK - 1))
                    return r
                mm = P.op("tensor", w1grp, deps=[ld] + env.sXLN.rdeps() +
                          env.sB[t % 2].wdeps(), tag=f"l2w1m{call}_{e_i}_{hc}_{t}")
                env.sB[t % 2].wrote(mm)
                env.sXLN.read(mm)
                sW1.read(mm)
                sv = P.op("scalar", lambda e, hc=hc, t=t, bk=bk, HE=HE:
                          e.activation(HE[:, 2048 * hc + 512 * t :
                                          2048 * hc + 512 * t + 512], bk, AF.Silu),
                          deps=[mm] + he_guard, tag=f"l2hs{call}_{e_i}_{hc}_{t}")
                env.sB[t % 2].read(sv)
                sHE.read(sv)
                heops.append(sv)
        sHE.wrote_many(heops)
        for t in range(4):
            bkw = env.bank(4)
            WROW = env.SC1[0:1, 1536:2048]
            b1_ = P.op("tensor", lambda e, e_i=e_i, t=t, bkw=bkw: e.matmul(
                bkw[0:1, :], env.spf("id16", p=16, w=1, coff=e_i),
                env.WTM[0:16, 512 * t : 512 * (t + 1)], start=True, stop=True),
                deps=env.sWTM.rdeps() + env.sB[4].wdeps(),
                tag=f"l2wr{call}_{e_i}_{t}")
            env.sB[4].wrote(b1_)
            env.sWTM.read(b1_)
            c1_ = P.op("vector", lambda e, bkw=bkw: e.tensor_copy(
                WROW, bkw[0:1, :]), deps=[b1_] + env.sSC1.wdeps(),
                tag=f"l2wrc{call}_{e_i}_{t}")
            env.sB[4].read(c1_)
            env.sSC1.wrote(c1_)
            bw = P.op("tensor", lambda e, bkw=bkw: e.matmul(
                bkw, env.spf("ones_row"), WROW, start=True, stop=True),
                deps=[c1_] + env.sB[4].wdeps(), tag=f"l2wb{call}_{e_i}_{t}")
            env.sB[4].wrote(bw)
            env.sSC1.read(bw)
            cw = P.op("vector", lambda e, bkw=bkw: e.tensor_copy(
                env.SC2[:, 0:512], bkw), deps=[bw] + env.sSC2.wdeps(),
                tag=f"l2wbc{call}_{e_i}_{t}")
            env.sB[4].read(cw)
            env.sSC2.wrote(cw)
            for och in range(NCHUNK):
                slb = och % 2
                W2S = env.E[:, 2048 + slb * 512 : 2048 + slb * 512 + 512]
                sW2 = env.sE[2]
                ldw2 = P.dma("gpsimd", lambda e, e_i=e_i, och=och, W2S=W2S:
                             e.dma_start(out=W2S, in_=env.l2fw2_h[e_i, och]),
                             deps=sW2.wdeps(), tag=f"l2w2ld{call}_{e_i}_{t}_{och}")
                sW2.wrote(ldw2)
                bk = env.bank(6 + och % 2)

                def w2grp(e, W2S=W2S, t=t, bk=bk, HE=HE):
                    r = None
                    for hcc in range(4):
                        r = e.matmul(bk, W2S[:, 128 * hcc : 128 * (hcc + 1)],
                                     HE[:, 2048 * hcc + 512 * t :
                                        2048 * hcc + 512 * t + 512],
                                     start=(hcc == 0), stop=(hcc == 3))
                    return r
                mm = P.op("tensor", w2grp, deps=[ldw2] + sHE.rdeps() +
                          env.sB[6 + och % 2].wdeps(),
                          tag=f"l2w2m{call}_{e_i}_{t}_{och}")
                env.sB[6 + och % 2].wrote(mm)
                sHE.read(mm)
                sW2.read(mm)
                ysl = slice(och * S + 512 * t, och * S + 512 * t + 512)
                def yfn(e, ysl=ysl, bk=bk):
                    e.tensor_tensor(env.SC1[:, 1024:1536], bk, env.SC2[:, 0:512],
                                    ALU.mult)
                    e.drain()
                    return e.tensor_tensor(YACC[:, ysl], YACC[:, ysl],
                                           env.SC1[:, 1024:1536], ALU.add)
                yop = P.op("vector", yfn, deps=[mm, cw] + env.sX.wdeps(),
                           tag=f"l2y{call}_{e_i}_{t}_{och}")
                env.sB[6 + och % 2].read(yop)
                env.sSC1.wrote(yop)
                env.sSC2.read(yop)
                env.sX.wrote(yop)

    for c in range(NCHUNK):
        ld = P.dma("sync", lambda e, c=c: e.dma_start(
            out=env.SC2[:, 0:2048], in_=env.xdump_h[:, c * S : (c + 1) * S]),
            deps=env.sSC2.wdeps() + [dx], tag=f"l2xrl{call}_{c}")
        env.sSC2.wrote(ld)
        ad = P.op("vector", lambda e, c=c: e.scalar_tensor_tensor(
            env.X[:, c * S : (c + 1) * S], env.X[:, c * S : (c + 1) * S], 0.5,
            env.SC2[:, 0:2048], ALU.mult, ALU.add),
            deps=[ld] + env.sX.wdeps(), tag=f"l2xadd{call}_{c}")
        env.sSC2.read(ad)
        env.sX.wrote(ad)


def emit_l2_attn(env):
    P = env.P
    nc = env.nc
    emit_ln(env)
    dx = P.dma("sync", lambda e: e.dma_start(out=env.xdump_h[:], in_=env.X[:]),
               deps=env.sX.rdeps(), tag="l2a_dumpx")
    env.sX.read(dx)
    xi_ops = []
    for c in range(NCHUNK):
        o = P.op("vector", lambda e, c=c: e.tensor_copy(
            env.X[:, c * S : (c + 1) * S], env.XLN[:, c * S : (c + 1) * S]),
            deps=[dx] + env.sX.wdeps() + env.sXLN.rdeps(), tag=f"l2a_xi{c}")
        env.sXLN.read(o)
        env.sX.writers.append(o)
        xi_ops.append(o)
    env.sX.wrote_many(env.sX.writers[-NCHUNK:])

    CF = env.C.ap().bitcast(F32)
    XLF = env.XLN.ap().bitcast(F32)
    GT = env.WTM
    sGT = env.sWTM

    for l in range(4):
        if l > 0:
            emit_ln(env)
        lw_ops = []
        for c in range(NCHUNK):
            o = P.op("vector", lambda e, c=c, l=l: e.tensor_scalar(
                env.XLN[:, c * S : (c + 1) * S], env.XLN[:, c * S : (c + 1) * S],
                env.spf("l2a_lnw", w=1, coff=l * 8 + c), None, ALU.mult),
                deps=env.sXLN.rdeps() + env.sXLN.readers, tag=f"l2a_g{l}_{c}")
            env.sXLN.writers.append(o)
            lw_ops.append(o)

        XMB = env.Dg[:, 2048:4096]
        XCB = env.Dg[:, 0:2048]
        XCF = CF[:, 0:2048]
        VTH = env.Dg[:, 4096:12288]
        QFB = env.C[:, 0:2048]
        KFB = env.C[:, 2048:4096]
        VFB = env.C[:, 4096:6144]
        sXM, sXC, sXCF = Slot("XM"), Slot("XC"), Slot("XCF")
        sVTH, sQF, sKF, sVF = Slot("VTH"), Slot("QF2"), Slot("KF2"), Slot("VF2")
        gz = P.op("vector", lambda e: e.memset(GT[0:8, :], 0.0),
                  deps=sGT.wdeps(), tag=f"l2a_gz{l}")
        sGT.wrote(gz)
        gt_adds = []
        vth_st = []
        xc_st = []
        qk_st = []
        for j in range(16):
            h = j // 4
            slb = j % 2
            WUP = env.E[:, slb * 1024 : slb * 1024 + 1024]
            sWU = env.sE[slb]
            ldu = P.dma("gpsimd", lambda e, l=l, j=j, WUP=WUP: e.dma_start(
                out=WUP, in_=env.l2up_h[l, j]), deps=sWU.wdeps(),
                tag=f"l2a_upld{l}_{j}")
            sWU.wrote(ldu)
            BQ = env.E[:, 2048 + slb * 384 : 2048 + slb * 384 + 128]
            BK = env.E[:, 2048 + slb * 384 + 128 : 2048 + slb * 384 + 256]
            BV = env.E[:, 2048 + slb * 384 + 256 : 2048 + slb * 384 + 384]
            sBD = env.sE[2]
            bd_guard = sBD.wdeps()
            ldb = []
            for W_, hsrc in ((BQ, env.l2bq_h), (BK, env.l2bk_h), (BV, env.l2bv_h)):
                o = P.dma("gpsimd", lambda e, W_=W_, hsrc=hsrc, l=l, j=j:
                          e.dma_start(out=W_, in_=hsrc[l, j]),
                          deps=bd_guard, tag=f"l2a_bd{l}_{j}")
                ldb.append(o)
            sBD.wrote_many(ldb)
            xm_ops = []
            for t in range(4):
                bk = env.bank(t % 2)

                def ugrp(e, WUP=WUP, t=t, bk=bk):
                    r = None
                    for c in range(NCHUNK):
                        r = e.matmul(bk, WUP[:, 128 * c : 128 * (c + 1)],
                                     env.XLN[:, c * S + 512 * t : c * S + 512 * t + 512],
                                     start=(c == 0), stop=(c == NCHUNK - 1))
                    return r
                mm = P.op("tensor", ugrp, deps=[ldu] + lw_ops +
                          env.sB[t % 2].wdeps(), tag=f"l2a_up{l}_{j}_{t}")
                env.sB[t % 2].wrote(mm)
                env.sXLN.read(mm)
                sWU.read(mm)
                sv = P.op("scalar", lambda e, t=t, bk=bk: e.copy(
                    XMB[:, 512 * t : 512 * (t + 1)], bk),
                    deps=[mm] + (sXM.wdeps() if t == 0 else []),
                    tag=f"l2a_xm{l}_{j}_{t}")
                env.sB[t % 2].read(sv)
                sXM.read(sv)
                xm_ops.append(sv)
            sXM.wrote_many(xm_ops)

            def convfn(e, l=l, j=j):
                cwc = l * 64 + j * 4
                e.tensor_scalar(XCF, XMB,
                                env.spf("l2a_cw", w=1, coff=cwc + 3),
                                env.spf("l2a_cb", w=1, coff=l * 16 + j),
                                ALU.mult, ALU.add)
                e.drain()
                e.scalar_tensor_tensor(XCF[:, 1:2048], XMB[:, 0:2047],
                                       env.spf("l2a_cw", w=1, coff=cwc + 2),
                                       XCF[:, 1:2048], ALU.mult, ALU.add)
                e.drain()
                e.scalar_tensor_tensor(XCF[:, 2:2048], XMB[:, 0:2046],
                                       env.spf("l2a_cw", w=1, coff=cwc + 1),
                                       XCF[:, 2:2048], ALU.mult, ALU.add)
                e.drain()
                return e.scalar_tensor_tensor(XCF[:, 3:2048], XMB[:, 0:2045],
                                              env.spf("l2a_cw", w=1, coff=cwc + 0),
                                              XCF[:, 3:2048], ALU.mult, ALU.add)
            cv = P.op("vector", convfn, deps=xm_ops + sXCF.wdeps(),
                      tag=f"l2a_cv{l}_{j}")
            sXM.read(cv)
            sXCF.wrote(cv)
            si = P.op("scalar", lambda e: e.activation(XCB, XCF, AF.Silu),
                      deps=[cv] + sXC.wdeps(), tag=f"l2a_si{l}_{j}")
            sXCF.read(si)
            sXC.wrote(si)
            stc = P.dma("sync", lambda e, j=j: e.dma_start(
                out=env.xcd_h[j], in_=XCB), deps=[si], tag=f"l2a_xcst{l}_{j}")
            sXC.read(stc)
            xc_st.append(stc)
            for which, BW, OUTB, sOUT, dh, sc in (
                    ("q", BQ, QFB, sQF, env.qd_h, "c_isq512"),
                    ("k", BK, KFB, sKF, env.kd_h, None),
                    ("v", BV, VFB, sVF, None, None)):
                if which in ("q", "k"):
                    SRC, sSRC, src_dep = XCB, sXC, [si]
                else:
                    SRC, sSRC, src_dep = XMB, sXM, xm_ops
                evs = []
                for t in range(4):
                    bk = env.bank(4 + t % 2)
                    mm = P.op("tensor", lambda e, BW=BW, SRC=SRC, t=t, bk=bk:
                              e.matmul(bk, BW, SRC[:, 512 * t : 512 * (t + 1)],
                                       start=True, stop=True),
                              deps=ldb + src_dep + env.sB[4 + t % 2].wdeps(),
                              tag=f"l2a_{which}m{l}_{j}_{t}")
                    env.sB[4 + t % 2].wrote(mm)
                    sBD.read(mm)
                    sSRC.read(mm)
                    kw = {}
                    if sc:
                        kw["scale"] = env.spf(sc)
                    sv = P.op("scalar", lambda e, OUTB=OUTB, t=t, bk=bk, kw=kw:
                              e.activation(OUTB[:, 512 * t : 512 * (t + 1)], bk,
                                           AF.Identity, **kw),
                              deps=[mm] + (sOUT.wdeps() if t == 0 else []),
                              tag=f"l2a_{which}e{l}_{j}_{t}")
                    env.sB[4 + t % 2].read(sv)
                    sOUT.read(sv)
                    evs.append(sv)
                sOUT.wrote_many(evs)
                if dh is not None:
                    std = P.dma("sync", lambda e, dh=dh, j=j, OUTB=OUTB:
                                e.dma_start(out=dh[j], in_=OUTB),
                                deps=evs, tag=f"l2a_{which}st{l}_{j}")
                    sOUT.read(std)
                    qk_st.append(std)
                blk = {"q": j, "k": 16 + j, "v": 32 + j}[which]
                for t in range(4):
                    bkg = env.bank(6)
                    mg = P.op("tensor", lambda e, blk=blk, l=l, t=t, bkg=bkg,
                              OUTB=OUTB: e.matmul(
                        bkg[0:8, 0:512],
                        env.spb("l2a_gw")[:, (l * 48 + blk) * 8 :
                                          (l * 48 + blk) * 8 + 8],
                        OUTB[:, 512 * t : 512 * (t + 1)],
                        start=True, stop=True),
                        deps=evs + [env.sSPB.writers] + env.sB[6].wdeps(),
                        tag=f"l2a_gm{which}{l}_{j}_{t}")
                    env.sB[6].wrote(mg)
                    sOUT.read(mg)
                    ga = P.op("vector", lambda e, t=t, bkg=bkg: e.tensor_tensor(
                        GT[0:8, 512 * t : 512 * (t + 1)],
                        GT[0:8, 512 * t : 512 * (t + 1)], bkg[0:8, 0:512],
                        ALU.add),
                        deps=[mg, gz] + sGT.wdeps(), tag=f"l2a_ga{which}{l}_{j}_{t}")
                    env.sB[6].read(ga)
                    sGT.wrote(ga)
                    gt_adds.append(ga)
            vte = []
            for tt in range(16):
                bk = env.bank(7)
                mm = P.op("tensor", lambda e, tt=tt, BV=BV, bk=bk: e.matmul(
                    bk[0:128, 0:128], XMB[:, 128 * tt : 128 * (tt + 1)], BV,
                    start=True, stop=True),
                    deps=ldb + xm_ops + env.sB[7].wdeps(),
                    tag=f"l2a_vt{l}_{j}_{tt}")
                env.sB[7].wrote(mm)
                sXM.read(mm)
                sBD.read(mm)
                dcol = 512 * tt + 128 * (j % 4)
                cvt = P.op("vector", lambda e, dcol=dcol, bk=bk: e.tensor_copy(
                    VTH[:, dcol : dcol + 128], bk[0:128, 0:128]),
                    deps=[mm] + (sVTH.wdeps() if j % 4 == 0 and tt == 0 else []),
                    tag=f"l2a_vte{l}_{j}_{tt}")
                env.sB[7].read(cvt)
                sVTH.read(cvt)
                vte.append(cvt)
            if j % 4 == 3:
                sVTH.wrote_many(sVTH.readers[-64:])
                stv = P.dma("sync", lambda e, h=h: e.dma_start(
                    out=env.vtd_h[h], in_=VTH), deps=sVTH.writers + vte,
                    tag=f"l2a_vtst{l}_{h}")
                sVTH.read(stv)
                vth_st.append(stv)
        z_st = []
        for zc in range(16):
            slb = zc % 2
            WUP = env.E[:, slb * 1024 : slb * 1024 + 1024]
            sWU = env.sE[slb]
            ldu = P.dma("gpsimd", lambda e, l=l, zc=zc, WUP=WUP: e.dma_start(
                out=WUP, in_=env.l2up_h[l, 16 + zc]), deps=sWU.wdeps(),
                tag=f"l2a_zld{l}_{zc}")
            sWU.wrote(ldu)
            z_ops = []
            for t in range(4):
                bk = env.bank(t % 2)

                def zgrp(e, WUP=WUP, t=t, bk=bk):
                    r = None
                    for c in range(NCHUNK):
                        r = e.matmul(bk, WUP[:, 128 * c : 128 * (c + 1)],
                                     env.XLN[:, c * S + 512 * t : c * S + 512 * t + 512],
                                     start=(c == 0), stop=(c == NCHUNK - 1))
                    return r
                mm = P.op("tensor", zgrp, deps=[ldu] + lw_ops +
                          env.sB[t % 2].wdeps(), tag=f"l2a_z{l}_{zc}_{t}")
                env.sB[t % 2].wrote(mm)
                env.sXLN.read(mm)
                sWU.read(mm)
                sv = P.op("scalar", lambda e, t=t, bk=bk: e.activation(
                    XCB[:, 512 * t : 512 * (t + 1)], bk, AF.Silu),
                    deps=[mm] + (sXC.wdeps() if t == 0 else []),
                    tag=f"l2a_ze{l}_{zc}_{t}")
                env.sB[t % 2].read(sv)
                sXC.read(sv)
                z_ops.append(sv)
            sXC.wrote_many(z_ops)
            stz = P.dma("sync", lambda e, zc=zc: e.dma_start(
                out=env.szd_h[zc], in_=XCB), deps=z_ops, tag=f"l2a_zst{l}_{zc}")
            sXC.read(stz)
            z_st.append(stz)

        sPB = Slot("PB")
        pb_guard = env.sXLN.readers[-1:] + env.sXLN.writers[-1:]
        gb_ = P.op("vector", lambda e, l=l: e.tensor_scalar(
            GT[0:8, :], GT[0:8, :], env.spf("l2a_gb", p=8, w=1, coff=l), None,
            ALU.add), deps=gt_adds + sGT.wdeps(), tag=f"l2a_gb{l}")
        sGT.wrote(gb_)
        A8 = XLF[0:8, 0:2048]
        CS8 = XLF[0:8, 2048:4096]
        EIG = XLF[0:4, 4096:6144]
        EIGR = XLF[32:33, 2048:4096]
        NRrow = XLF[32:33, 4096:4608]
        CSROW = XLF[0:1, 6144:8192]
        NR0 = env.SC1[0:1, 0:512]
        ABSR = env.SC1[0:1, 1040:1552]
        MUR2 = env.SC1[0:1, 528:1040]
        NEGC = env.SC1[:, 512:528]
        s1 = P.op("scalar", lambda e: e.activation(A8, GT[0:8, :], AF.Abs),
                  deps=[gb_] + sPB.wdeps() + pb_guard, tag=f"l2a_ab{l}")
        sGT.read(s1)
        sPB.wrote(s1)
        s2 = P.op("scalar", lambda e: e.activation(A8, A8, AF.Exp,
                  scale=env.spf("c_neg1", p=8)), deps=[s1], tag=f"l2a_ex{l}")
        sPB.wrote(s2)
        s3 = P.op("scalar", lambda e: e.activation(A8, A8, AF.Ln,
                  bias=env.spf("c_one", p=8)), deps=[s2], tag=f"l2a_l1p{l}")
        sPB.wrote(s3)

        def csfn(e):
            e.tensor_scalar(CS8, GT[0:8, :], 0.0, None, ALU.min)
            e.drain()
            e.tensor_tensor(CS8, CS8, A8, ALU.subtract)
            e.drain()
            sh = 1
            while sh < 2048:
                e.tensor_tensor(CS8[0:8, sh:2048], CS8[0:8, sh:2048],
                                CS8[0:8, 0 : 2048 - sh], ALU.add)
                e.drain()
            return None
        def csfn2(e):
            e.tensor_scalar(CS8, GT[0:8, :], 0.0, None, ALU.min)
            e.drain()
            e.tensor_tensor(CS8, CS8, A8, ALU.subtract)
            sh = 1
            while sh < 2048:
                e.drain()
                e.tensor_tensor(CS8[0:8, sh:2048], CS8[0:8, sh:2048],
                                CS8[0:8, 0 : 2048 - sh], ALU.add)
                sh *= 2
            return None
        v1 = P.op("vector", csfn2, deps=[s3, gb_] + sPB.wdeps(), tag=f"l2a_cs{l}")
        sGT.read(v1)
        sPB.wrote(v1)
        s4 = P.op("scalar", lambda e: e.activation(EIG, GT[0:4, :], AF.Exp,
                  scale=env.spf("c_neg1", p=4)), deps=[gb_, v1], tag=f"l2a_eig{l}")
        sGT.read(s4)
        sPB.wrote(s4)

        QH = env.C[:, 0:8192]
        sQH = env.sC[0]
        KH = env.Dg[:, 0:8192]
        sKH = env.sD[0]
        VTL = env.Dg[:, 8192:16384]
        sVT2 = env.sD[1]
        HT4 = XLF[:, 0:2048]
        htd_st = []
        for h in range(4):
            ldq = P.dma("sync", lambda e, h=h: e.dma_start(
                out=QH.rearrange("p (a b) -> p a b", a=4),
                in_=env.qd_h[4 * h : 4 * h + 4].transpose([1, 0, 2])),
                deps=sQH.wdeps() + qk_st, tag=f"l2b_ldq{l}_{h}")
            sQH.wrote(ldq)
            ldk = P.dma("sync", lambda e, h=h: e.dma_start(
                out=KH.rearrange("p (a b) -> p a b", a=4),
                in_=env.kd_h[4 * h : 4 * h + 4].transpose([1, 0, 2])),
                deps=sKH.wdeps() + qk_st, tag=f"l2b_ldk{l}_{h}")
            sKH.wrote(ldk)
            ldv = P.dma("sync", lambda e, h=h: e.dma_start(
                out=VTL, in_=env.vtd_h[h]), deps=sVT2.wdeps() + vth_st,
                tag=f"l2b_ldv{l}_{h}")
            sVT2.wrote(ldv)
            rows_ops = []
            for st4 in range(4):
                bk7 = env.bank(7)
                m1 = P.op("tensor", lambda e, h=h, st4=st4, bk7=bk7: e.matmul(
                    bk7[0:1, 0:512], env.spf("id4", p=4, w=1, coff=h),
                    EIG[0:4, 512 * st4 : 512 * (st4 + 1)], start=True, stop=True),
                    deps=[s4] + env.sB[7].wdeps(), tag=f"l2b_eig{l}_{h}_{st4}")
                env.sB[7].wrote(m1)
                sPB.read(m1)
                c1 = P.op("vector", lambda e, st4=st4, bk7=bk7: e.tensor_copy(
                    EIGR[0:1, 512 * st4 : 512 * (st4 + 1)], bk7[0:1, 0:512]),
                    deps=[m1] + (sPB.wdeps() if st4 == 0 and h == 0 else []),
                    tag=f"l2b_eigc{l}_{h}_{st4}")
                env.sB[7].read(c1)
                sPB.read(c1)
                rows_ops.append(c1)
                bk6x = env.bank(6)
                m2 = P.op("tensor", lambda e, h=h, st4=st4, bk6x=bk6x: e.matmul(
                    bk6x[0:1, 0:512], env.spf("id8", p=8, w=1, coff=4 + h),
                    CS8[0:8, 512 * st4 : 512 * (st4 + 1)], start=True, stop=True),
                    deps=[v1] + env.sB[6].wdeps(), tag=f"l2b_csr{l}_{h}_{st4}")
                env.sB[6].wrote(m2)
                sPB.read(m2)
                c2 = P.op("vector", lambda e, st4=st4, bk6x=bk6x: e.tensor_copy(
                    CSROW[0:1, 512 * st4 : 512 * (st4 + 1)], bk6x[0:1, 0:512]),
                    deps=[m2], tag=f"l2b_csrc{l}_{h}_{st4}")
                env.sB[6].read(c2)
                sPB.read(c2)
                rows_ops.append(c2)
            eig_done = rows_ops
            negc_ops = []
            for tc in range(16):
                bk6 = env.bank(6)
                tr = P.op("tensor", lambda e, tc=tc, bk6=bk6: e.transpose(
                    bk6[0:128, 0:1], CSROW[0:1, 128 * tc : 128 * (tc + 1)],
                    env.spf("c_one", p=1, w=1)),
                    deps=rows_ops + env.sB[6].wdeps(), tag=f"l2b_nt{l}_{h}_{tc}")
                env.sB[6].wrote(tr)
                sPB.read(tr)
                nc_ = P.op("scalar", lambda e, tc=tc, bk6=bk6: e.activation(
                    NEGC[:, tc : tc + 1], bk6[0:128, 0:1], AF.Identity,
                    scale=env.spf("c_neg1")), deps=[tr] + env.sSC1.wdeps(),
                    tag=f"l2b_nc{l}_{h}_{tc}")
                env.sB[6].read(nc_)
                env.sSC1.read(nc_)
                negc_ops.append(nc_)
            for st in range(4):
                bk7 = env.bank(7)
                mcs = P.op("tensor", lambda e, st=st, bk7=bk7: e.matmul(
                    bk7, env.spf("ones_row"),
                    CSROW[0:1, 512 * st : 512 * (st + 1)], start=True, stop=True),
                    deps=rows_ops + env.sB[7].wdeps(), tag=f"l2b_cb{l}_{h}_{st}")
                env.sB[7].wrote(mcs)
                sPB.read(mcs)
                CSB = env.SC1[:, 0:512]
                cc = P.op("vector", lambda e, bk7=bk7, CSB=CSB: e.tensor_copy(
                    CSB, bk7), deps=[mcs] + env.sSC1.wdeps(), tag=f"l2b_cc{l}_{h}_{st}")
                env.sB[7].read(cc)
                env.sSC1.wrote(cc)
                ntc = 4 * (st + 1)
                h_started = False
                rs_mm = None
                hmms = []
                for tc in range(ntc):
                    EB = env.SC2[:, 0:512]
                    se = P.op("scalar", lambda e, tc=tc, EB=EB, CSB=CSB:
                              e.activation(EB, CSB, AF.Exp,
                                           bias=NEGC[:, tc : tc + 1]),
                              deps=[cc] + negc_ops + env.sSC2.wdeps(),
                              tag=f"l2b_E{l}_{h}_{st}_{tc}")
                    env.sSC1.read(se)
                    env.sSC2.wrote(se)
                    mdep = [se]
                    if tc // 4 == st:
                        gm = P.op("gpsimd", lambda e, st=st, tc=tc, EB=EB:
                                  e.affine_select(EB, EB, pattern=[[1, 512]],
                                                  compare_op=ALU.is_ge, fill=0.0,
                                                  base=512 * st - 128 * tc,
                                                  channel_multiplier=-1),
                                  deps=[se], tag=f"l2b_M{l}_{h}_{st}_{tc}")
                        env.sSC2.wrote(gm)
                        mdep = [gm]
                    bkq = env.bank(tc % 2)

                    def qkgrp(e, h=h, st=st, tc=tc, bkq=bkq):
                        r = None
                        for dc in range(4):
                            r = e.matmul(
                                bkq, KH[:, 2048 * dc + 128 * tc :
                                        2048 * dc + 128 * tc + 128],
                                QH[:, 2048 * dc + 512 * st :
                                      2048 * dc + 512 * st + 512],
                                start=(dc == 0), stop=(dc == 3))
                        return r
                    mq = P.op("tensor", qkgrp, deps=[ldq, ldk] +
                              env.sB[tc % 2].wdeps(), tag=f"l2b_qk{l}_{h}_{st}_{tc}")
                    env.sB[tc % 2].wrote(mq)
                    sQH.read(mq)
                    sKH.read(mq)
                    CBb = env.SC2.ap().bitcast(BF16)[:, 2048:2560]
                    vc = P.op("vector", lambda e, bkq=bkq, EB=EB, CBb=CBb:
                              e.tensor_tensor(CBb, bkq, EB, ALU.mult),
                              deps=[mq] + mdep, tag=f"l2b_C{l}_{h}_{st}_{tc}")
                    env.sB[tc % 2].read(vc)
                    env.sSC2.wrote(vc)
                    rs_mm = P.op("tensor", lambda e, CBb=CBb, st_=(tc == 0),
                                 sp_=(tc == ntc - 1):
                                 e.matmul(env.bank(6)[0:1, :],
                                          env.spb("ones_colb"), CBb,
                                          start=st_, stop=sp_),
                                 deps=[vc] + (env.sB[6].wdeps() if tc == 0 else []),
                                 tag=f"l2b_rs{l}_{h}_{st}_{tc}")
                    if tc == 0:
                        env.sB[6].wrote(rs_mm)
                    env.sSC2.read(rs_mm)
                    for dvc in range(4):
                        hm = P.op("tensor", lambda e, tc=tc, dvc=dvc, CBb=CBb,
                                  sp_=(tc == ntc - 1):
                                  e.matmul(env.bank(2 + dvc),
                                           VTL[:, 512 * tc + 128 * dvc :
                                               512 * tc + 128 * dvc + 128],
                                           CBb, start=(tc == 0),
                                           stop=sp_),
                                  deps=[vc, ldv] +
                                  (env.sB[2 + dvc].wdeps() if tc == 0 else []),
                                  tag=f"l2b_h{l}_{h}_{st}_{tc}_{dvc}")
                        if tc == 0:
                            env.sB[2 + dvc].wrote(hm)
                        sVT2.read(hm)
                        env.sSC2.read(hm)
                        hmms.append(hm)
                def nfn(e, st=st):
                    e.tensor_scalar(NRrow, env.bank(6)[0:1, :], -1.0, None,
                                    ALU.mult)
                    e.drain()
                    e.tensor_tensor(NRrow, NRrow, env.bank(6)[0:1, :], ALU.max)
                    e.drain()
                    e.tensor_tensor(NRrow, NRrow,
                                    EIGR[0:1, 512 * st : 512 * (st + 1)], ALU.max)
                    e.drain()
                    e.tensor_scalar(NRrow, NRrow, 1e-6, None, ALU.add)
                    e.drain()
                    e.reciprocal(NRrow, NRrow)
                    e.drain()
                    return e.tensor_copy(NR0, NRrow)
                vn = P.op("vector", nfn, deps=[rs_mm] + eig_done +
                          env.sSC1.wdeps(), tag=f"l2b_n{l}_{h}_{st}")
                env.sB[6].read(vn)
                sPB.read(vn)
                env.sSC1.wrote(vn)
                bk6b = env.bank(6)
                mnb = P.op("tensor", lambda e, bk6b=bk6b: e.matmul(
                    bk6b, env.spf("ones_row"), NR0, start=True, stop=True),
                    deps=[vn] + env.sB[6].wdeps(), tag=f"l2b_nb{l}_{h}_{st}")
                env.sB[6].wrote(mnb)
                env.sSC1.read(mnb)
                NRB = env.SC2[:, 0:512]
                cnb = P.op("vector", lambda e, bk6b=bk6b, NRB=NRB: e.tensor_copy(
                    NRB, bk6b), deps=[mnb] + env.sSC2.wdeps(),
                    tag=f"l2b_nbc{l}_{h}_{st}")
                env.sB[6].read(cnb)
                env.sSC2.wrote(cnb)
                ht_ops = []
                for dvc in range(4):
                    o = P.op("vector", lambda e, dvc=dvc, NRB=NRB: e.tensor_tensor(
                        HT4[:, 512 * dvc : 512 * (dvc + 1)],
                        env.bank(2 + dvc), NRB, ALU.mult),
                        deps=hmms + [cnb] + (sPB.wdeps() if dvc == 0 else []),
                        tag=f"l2b_he{l}_{h}_{st}_{dvc}")
                    env.sB[2 + dvc].read(o)
                    env.sSC2.read(o)
                    sPB.read(o)
                    ht_ops.append(o)
                bk0 = env.bank(0)
                ms_ = P.op("tensor", lambda e, bk0=bk0: [
                    e.matmul(bk0[0:1, :], env.spf("ones_col"),
                             HT4[:, 512 * dc : 512 * (dc + 1)],
                             start=(dc == 0), stop=(dc == 3)) for dc in range(4)][-1],
                    deps=ht_ops + env.sB[0].wdeps(), tag=f"l2b_os{l}_{h}_{st}")
                env.sB[0].wrote(ms_)
                sPB.read(ms_)
                sq_ops = []
                for dc in range(4):
                    sq = P.op("scalar", lambda e, dc=dc: e.activation(
                        env.SC2[:, 512:1024], HT4[:, 512 * dc : 512 * (dc + 1)],
                        AF.Square), deps=ht_ops + env.sSC2.wdeps(),
                        tag=f"l2b_sq{l}_{h}_{st}_{dc}")
                    env.sSC2.wrote(sq)
                    sPB.read(sq)
                    mq_ = P.op("tensor", lambda e, dc=dc: e.matmul(
                        env.bank(1)[0:1, :], env.spf("ones_col"),
                        env.SC2[:, 512:1024], start=(dc == 0), stop=(dc == 3)),
                        deps=[sq] + (env.sB[1].wdeps() if dc == 0 else []),
                        tag=f"l2b_qm{l}_{h}_{st}_{dc}")
                    if dc == 0:
                        env.sB[1].wrote(mq_)
                    env.sSC2.read(mq_)
                    sq_ops.append(mq_)
                def statfn(e):
                    e.tensor_scalar(MUR2, env.bank(0)[0:1, :], 1.0 / 512, None,
                                    ALU.mult)
                    e.tensor_scalar(ABSR, env.bank(1)[0:1, :], 1.0 / 512, None,
                                    ALU.mult)
                    e.drain()
                    e.tensor_tensor(env.SC1[0:1, 0:512], MUR2, MUR2, ALU.mult)
                    e.drain()
                    e.tensor_tensor(ABSR, ABSR, env.SC1[0:1, 0:512],
                                    ALU.subtract)
                    return None
                vs_ = P.op("vector", statfn, deps=[ms_, sq_ops[-1]] +
                           env.sSC1.wdeps(), tag=f"l2b_st{l}_{h}_{st}")
                env.sB[0].read(vs_)
                env.sB[1].read(vs_)
                env.sSC1.wrote(vs_)
                sPB.read(vs_)
                ssd = P.op("scalar", lambda e: e.activation(
                    env.SC1[0:1, 0:512], ABSR, AF.Sqrt,
                    bias=env.spf("c_epsln", p=1)), deps=[vs_], tag=f"l2b_sd{l}_{h}_{st}")
                env.sSC1.wrote(ssd)
                vrc = P.op("vector", lambda e: e.reciprocal(
                    ABSR, env.SC1[0:1, 0:512]), deps=[ssd], tag=f"l2b_rc{l}_{h}_{st}")
                env.sSC1.wrote(vrc)
                bmu = P.op("tensor", lambda e: e.matmul(
                    env.bank(0), env.spf("ones_row"), MUR2, start=True, stop=True),
                    deps=[vrc, vs_] + env.sB[0].wdeps(), tag=f"l2b_bm{l}_{h}_{st}")
                env.sB[0].wrote(bmu)
                sPB.read(bmu)
                brs = P.op("tensor", lambda e: e.matmul(
                    env.bank(1), env.spf("ones_row"), ABSR, start=True, stop=True),
                    deps=[vrc] + env.sB[1].wdeps(), tag=f"l2b_br{l}_{h}_{st}")
                env.sB[1].wrote(brs)
                sPB.read(brs)
                for dvc in range(4):
                    TMPV = env.SC2[:, 512:1024]
                    HTBb = env.SC2.ap().bitcast(BF16)[:, 3072:3584]
                    def onfn(e, dvc=dvc, l=l, h=h, TMPV=TMPV, HTBb=HTBb):
                        e.tensor_tensor(TMPV, HT4[:, 512 * dvc : 512 * (dvc + 1)],
                                        env.bank(0), ALU.subtract)
                        e.drain()
                        e.tensor_tensor(TMPV, TMPV, env.bank(1), ALU.mult)
                        e.drain()
                        return e.tensor_scalar(
                            HTBb, TMPV,
                            env.spf("l2a_onw", w=1, coff=l * 16 + 4 * h + dvc),
                            None, ALU.mult)
                    ev = P.op("vector", onfn, deps=[bmu, brs] + ht_ops +
                              env.sSC2.wdeps(), tag=f"l2b_on{l}_{h}_{st}_{dvc}")
                    env.sB[0].read(ev)
                    env.sB[1].read(ev)
                    env.sSC2.wrote(ev)
                    sPB.read(ev)
                    sth = P.dma("sync", lambda e, l=l, h=h, st=st, dvc=dvc,
                                HTBb=HTBb: e.dma_start(
                        out=env.htd_h[4 * h + dvc][:, 512 * st : 512 * (st + 1)],
                        in_=HTBb), deps=[ev], tag=f"l2b_hst{l}_{h}_{st}_{dvc}")
                    env.sSC2.read(sth)
                    htd_st.append(sth)

        XCL = env.Dg[:, 0:8192]
        SZL = env.Dg[:, 8192:16384]
        OUTB = env.C[:, 0:8192]
        sXCL, sSZL = env.sD[0], env.sD[1]
        sOUTB = env.sC[0]
        for t in range(4):
            l1 = P.dma("sync", lambda e, t=t: e.dma_start(
                out=XCL.rearrange("p (a b) -> p a b", a=16),
                in_=env.xcd_h[:, :, 512 * t : 512 * (t + 1)].transpose([1, 0, 2])),
                deps=sXCL.wdeps() + xc_st, tag=f"l2c_xc{l}_{t}")
            sXCL.wrote(l1)
            l2_ = P.dma("sync", lambda e, t=t: e.dma_start(
                out=SZL.rearrange("p (a b) -> p a b", a=16),
                in_=env.szd_h[:, :, 512 * t : 512 * (t + 1)].transpose([1, 0, 2])),
                deps=sSZL.wdeps() + z_st, tag=f"l2c_sz{l}_{t}")
            sSZL.wrote(l2_)
            l3_ = P.dma("sync", lambda e, t=t: e.dma_start(
                out=OUTB.rearrange("p (a b) -> p a b", a=16),
                in_=env.htd_h[:, :, 512 * t : 512 * (t + 1)].transpose([1, 0, 2])),
                deps=sOUTB.wdeps() + htd_st, tag=f"l2c_ht{l}_{t}")
            sOUTB.wrote(l3_)
            ob_ops = []
            for j in range(16):
                def obfn(e, j=j, l=l):
                    e.scalar_tensor_tensor(
                        env.SC2[:, 0:512], XCL[:, 512 * j : 512 * (j + 1)],
                        env.spf("l2a_skip", w=1, coff=l * 16 + j),
                        OUTB[:, 512 * j : 512 * (j + 1)], ALU.mult, ALU.add)
                    e.drain()
                    return e.tensor_tensor(OUTB[:, 512 * j : 512 * (j + 1)],
                                           env.SC2[:, 0:512],
                                           SZL[:, 512 * j : 512 * (j + 1)],
                                           ALU.mult)
                o = P.op("vector", obfn, deps=[l1, l2_, l3_] + env.sSC2.wdeps(),
                         tag=f"l2c_ob{l}_{t}_{j}")
                sXCL.read(o)
                sSZL.read(o)
                sOUTB.read(o)
                env.sSC2.wrote(o)
                ob_ops.append(o)
            sOUTB.wrote_many(ob_ops)
            for oc in range(NCHUNK):
                slb = oc % 2
                DNW = env.E[:, slb * 2048 : slb * 2048 + 2048]
                dnslots = [env.sE[0], env.sE[1]] if slb == 0 else [env.sE[2]]
                ldd = P.dma("gpsimd", lambda e, l=l, oc=oc, DNW=DNW: e.dma_start(
                    out=DNW, in_=env.l2dn_h[l, oc]),
                    deps=[d for s_ in dnslots for d in s_.wdeps()],
                    tag=f"l2c_dn{l}_{t}_{oc}")
                for s_ in dnslots:
                    s_.wrote(ldd)
                sDN = dnslots[0]
                bk = env.bank(2 + oc % 2)

                def dgrp(e, DNW=DNW, bk=bk):
                    r = None
                    for j in range(16):
                        r = e.matmul(bk, DNW[:, 128 * j : 128 * (j + 1)],
                                     OUTB[:, 512 * j : 512 * (j + 1)],
                                     start=(j == 0), stop=(j == 15))
                    return r
                mm = P.op("tensor", dgrp, deps=[ldd] + ob_ops +
                          env.sB[2 + oc % 2].wdeps(), tag=f"l2c_dm{l}_{t}_{oc}")
                env.sB[2 + oc % 2].wrote(mm)
                sOUTB.read(mm)
                for s_ in dnslots:
                    s_.read(mm)
                xsl = slice(oc * S + 512 * t, oc * S + 512 * t + 512)
                vx = P.op("vector", lambda e, xsl=xsl, bk=bk: e.tensor_tensor(
                    env.X[:, xsl], env.X[:, xsl], bk, ALU.add),
                    deps=[mm] + env.sX.wdeps() + env.sX.readers,
                    tag=f"l2c_x{l}_{t}_{oc}")
                env.sB[2 + oc % 2].read(vx)
                env.sX.wrote(vx)

    emit_ln(env)
    pw_ops = []
    for c in range(NCHUNK):
        o = P.op("vector", lambda e, c=c: e.tensor_scalar(
            env.XLN[:, c * S : (c + 1) * S], env.XLN[:, c * S : (c + 1) * S],
            env.spf("l2a_postw", w=1, coff=c), None, ALU.mult),
            deps=env.sXLN.rdeps() + env.sXLN.readers, tag=f"l2p_w{c}")
        env.sXLN.writers.append(o)
        pw_ops.append(o)
    for oc in range(NCHUNK):
        slb = oc % 2
        LW = env.E[:, slb * 1024 : slb * 1024 + 1024]
        sLW = env.sE[slb]
        ldl = P.dma("gpsimd", lambda e, oc=oc, LW=LW: e.dma_start(
            out=LW, in_=env.l2lat_h[oc]), deps=sLW.wdeps(), tag=f"l2p_ld{oc}")
        sLW.wrote(ldl)
        for t in range(4):
            bk = env.bank(t % 2)

            def lgrp(e, LW=LW, t=t, bk=bk):
                r = None
                for c in range(NCHUNK):
                    r = e.matmul(bk, LW[:, 128 * c : 128 * (c + 1)],
                                 env.XLN[:, c * S + 512 * t : c * S + 512 * t + 512],
                                 start=(c == 0), stop=(c == NCHUNK - 1))
                return r
            mm = P.op("tensor", lgrp, deps=[ldl] + pw_ops +
                      env.sB[t % 2].wdeps(), tag=f"l2p_m{oc}_{t}")
            env.sB[t % 2].wrote(mm)
            env.sXLN.read(mm)
            sLW.read(mm)
            sv = P.op("scalar", lambda e, bk=bk, oc=oc: e.activation(
                env.SC1[:, 0:512], bk, AF.Identity,
                bias=env.spf("l2a_latb", w=1, coff=oc)),
                deps=[mm] + env.sSC1.wdeps(), tag=f"l2p_b{oc}_{t}")
            env.sB[t % 2].read(sv)
            env.sSC1.wrote(sv)
            ld = P.dma("sync", lambda e, oc=oc, t=t: e.dma_start(
                out=env.SC2[:, 0:512],
                in_=env.xdump_h[:, oc * S + 512 * t : oc * S + 512 * t + 512]),
                deps=env.sSC2.wdeps() + [dx], tag=f"l2p_xr{oc}_{t}")
            env.sSC2.wrote(ld)
            xsl = slice(oc * S + 512 * t, oc * S + 512 * t + 512)
            vx = P.op("vector", lambda e, xsl=xsl: e.tensor_tensor(
                env.X[:, xsl], env.SC2[:, 0:512], env.SC1[:, 0:512], ALU.add),
                deps=[sv, ld] + env.sX.wdeps() + env.sX.readers,
                tag=f"l2p_x{oc}_{t}")
            env.sSC1.read(vx)
            env.sSC2.read(vx)
            env.sX.wrote(vx)


def emit_l3_ffn(env):
    P = env.P
    nc = env.nc
    call = _L3N[0]
    _L3N[0] += 1
    emit_ln(env)
    DgF = env.Dg.ap().bitcast(F32)

    if call == 0:
        PKI = env.Dg.ap().bitcast(I32)[:, 0:1024]
        ENC = env.Dg.ap().bitcast(I32)[:, 1024:2048]
        G1 = DgF[:, 2048:3072]
        G2 = DgF[:, 3072:4096]
        TBF = env.C[:, 0:1024]
        for ch in range(2):
            ld = P.dma("sync", lambda e, ch=ch: e.dma_start(
                out=PKI, in_=env.l3pk_h[128 * ch : 128 * (ch + 1), :]),
                deps=env.sD[0].wdeps() + env.sD[1].wdeps(), tag=f"l3u_ld{ch}")
            env.sD[0].wrote(ld)
            env.sD[1].wrote(ld)
            for d in range(4):
                def unp(e, d=d):
                    e.tensor_scalar(ENC, PKI, 2 * d, 3,
                                    ALU.logical_shift_right, ALU.bitwise_and)
                    e.drain()
                    e.tensor_scalar(G1, ENC, 1, None, ALU.is_ge)
                    e.drain()
                    e.tensor_scalar(G2, ENC, 2, None, ALU.is_ge)
                    e.drain()
                    e.tensor_tensor(G1, G1, G2, ALU.add)
                    e.drain()
                    return e.tensor_scalar(TBF, G1, 1.0, None, ALU.subtract)
                v = P.op("vector", unp, deps=[ld] + env.sC[0].wdeps() +
                         env.sD[0].wdeps()[:0], tag=f"l3u_{ch}_{d}")
                env.sD[0].read(v)
                env.sC[0].wrote(v)
                st = P.dma("sync", lambda e, ch=ch, d=d: e.dma_start(
                    out=env.l3wb_h.rearrange("(w d) o -> w d o", d=4)[
                        128 * ch : 128 * (ch + 1), d, :], in_=TBF),
                    deps=[v], tag=f"l3u_st{ch}_{d}")
                env.sC[0].read(st)
                env.tern3_st = getattr(env, "tern3_st", [])
                env.tern3_st.append(st)

    SROW = env.SC1[0:1, 0:2048]
    IROW = env.SC1[0:1, 2048:4096] if False else None
    IROW = env.MUR[0:1, 0:2048]
    srow_ops = []
    sg = env.sSC1.wdeps() + env.sMUR.wdeps()
    PSB = env.PS.ap().bitcast(BF16)
    for tt in range(16):
        bkt = PSB[:, 4096:4352]
        TMX = env.SC2[:, 0:16]
        tr_ops = []
        for c in range(NCHUNK):
            tr = P.op("tensor", lambda e, c=c, tt=tt, bkt=bkt: e.transpose(
                bkt[0:128, 0:128],
                env.XLN[:, c * S + 128 * tt : c * S + 128 * tt + 128],
                env.spb("identb")),
                deps=env.sXLN.rdeps() + env.sB[4].wdeps(), tag=f"l3q_t{tt}_{c}")
            env.sB[4].wrote(tr)
            env.sXLN.read(tr)
            rd = P.op("vector", lambda e, c=c, bkt=bkt, TMX=TMX: e.reduce_max(
                TMX[:, c : c + 1], bkt[0:128, 0:128], axis=AX.X,
                apply_absolute_value=True),
                deps=[tr] + (env.sSC2.wdeps() if c == 0 else []),
                tag=f"l3q_r{tt}_{c}")
            env.sB[4].read(rd)
            env.sSC2.read(rd)
            tr_ops.append(rd)
        env.sSC2.wrote_many(tr_ops)
        def sfn(e, TMX=TMX):
            e.reduce_max(TMX[:, 8:9], TMX[:, 0:8], axis=AX.X)
            e.drain()
            e.tensor_scalar(TMX[:, 8:9], TMX[:, 8:9], 1e-5, None, ALU.max)
            e.drain()
            e.tensor_scalar(TMX[:, 9:10], TMX[:, 8:9], 1.0 / 127.0, None,
                            ALU.mult)
            e.drain()
            return e.reciprocal(TMX[:, 10:11], TMX[:, 9:10])
        vs = P.op("vector", sfn, deps=tr_ops, tag=f"l3q_s{tt}")
        env.sSC2.wrote(vs)
        for which, coli, ROW in (("s", 10, SROW), ("i", 9, IROW)):
            bk5 = env.bank(5)
            tr2 = P.op("tensor", lambda e, coli=coli, TMX=TMX, bk5=bk5:
                       e.transpose(bk5[0:1, 0:128], TMX[:, coli : coli + 1],
                                   env.spf("ident")),
                       deps=[vs] + env.sB[5].wdeps(), tag=f"l3q_tr{which}{tt}")
            env.sB[5].wrote(tr2)
            env.sSC2.read(tr2)
            cp = P.op("vector", lambda e, tt=tt, ROW=ROW, bk5=bk5: e.tensor_copy(
                ROW[0:1, 128 * tt : 128 * (tt + 1)], bk5[0:1, 0:128]),
                deps=[tr2] + (sg if tt == 0 else []), tag=f"l3q_c{which}{tt}")
            env.sB[5].read(cp)
            (env.sSC1 if which == "s" else env.sMUR).read(cp)
            srow_ops.append(cp)
    env.sSC1.wrote_many([o for o in srow_ops])
    env.sMUR.wrote_many([o for o in srow_ops])
    wsf = P.op("vector", lambda e: e.tensor_scalar(
        IROW, IROW, env.spf("l3_ws", p=1), None, ALU.mult),
        deps=srow_ops, tag="l3q_ws")
    env.sMUR.wrote_many([wsf])

    q_ops = []
    for t in range(4):
        bks = env.bank(5)
        mbs = P.op("tensor", lambda e, t=t, bks=bks: e.matmul(
            bks, env.spf("ones_row"), SROW[0:1, 512 * t : 512 * (t + 1)],
            start=True, stop=True), deps=[wsf] + srow_ops + env.sB[5].wdeps(),
            tag=f"l3q_bs{t}")
        env.sB[5].wrote(mbs)
        env.sSC1.read(mbs)
        SB_ = env.SC2[:, 0:512]
        cbs = P.op("vector", lambda e, bks=bks, SB_=SB_: e.tensor_copy(SB_, bks),
                   deps=[mbs] + env.sSC2.wdeps(), tag=f"l3q_cb{t}")
        env.sB[5].read(cbs)
        env.sSC2.wrote(cbs)
        for c in range(NCHUNK):
            xsl = slice(c * S + 512 * t, c * S + 512 * t + 512)
            YB = env.SC2[:, 512:1024]
            DB = env.SC2[:, 1024:1536]
            MB = env.SC2[:, 1536:2048]
            eng = "vector"
            def qfn(e, xsl=xsl, SB_=SB_, YB=YB, DB=DB, MB=MB):
                e.tensor_tensor(YB, env.XLN[:, xsl], SB_, ALU.mult)
                e.drain()
                e.tensor_scalar(DB, YB, 128.5, None, ALU.add)
                for j in range(8, -1, -1):
                    e.drain()
                    e.tensor_scalar(MB, DB, float(2 ** j), None, ALU.is_ge)
                    e.drain()
                    e.scalar_tensor_tensor(DB, MB, -float(2 ** j), DB,
                                           ALU.mult, ALU.add)
                e.drain()
                e.tensor_scalar(YB, YB, 0.5, None, ALU.add)
                e.drain()
                e.tensor_tensor(YB, YB, DB, ALU.subtract)
                e.drain()
                e.tensor_scalar(YB, YB, 127.0, -128.0, ALU.min, ALU.max)
                e.drain()
                return e.tensor_copy(env.XLN[:, xsl], YB)
            o = P.op(eng, qfn, deps=[cbs] + env.sXLN.rdeps() +
                     env.sXLN.readers + env.sSC2.wdeps(), tag=f"l3q_q{t}_{c}")
            env.sSC2.read(o)
            env.sXLN.writers.append(o)
            q_ops.append(o)
    env.sXLN.wrote_many(q_ops)

    for oc in range(NCHUNK):
        slb = oc % 2
        WB = env.E[:, slb * 1024 : slb * 1024 + 1024]
        sWB = env.sE[slb]
        ldw = P.dma("sync", lambda e, oc=oc, WB=WB: e.dma_start(
            out=WB.rearrange("p (ic j) -> p ic j", ic=8),
            in_=env.l3wb_h.rearrange("(ic p) o -> p ic o", p=128)[
                :, :, 128 * oc : 128 * (oc + 1)]),
            deps=sWB.wdeps() + getattr(env, "tern3_st", []), tag=f"l3f_w{call}_{oc}")
        sWB.wrote(ldw)
        for t in range(4):
            bk = env.bank(t % 2)

            def mgrp(e, WB=WB, t=t, bk=bk):
                r = None
                for c in range(NCHUNK):
                    r = e.matmul(bk, WB[:, 128 * c : 128 * (c + 1)],
                                 env.XLN[:, c * S + 512 * t : c * S + 512 * t + 512],
                                 start=(c == 0), stop=(c == NCHUNK - 1))
                return r
            mm = P.op("tensor", mgrp, deps=[ldw] + q_ops +
                      env.sB[t % 2].wdeps(), tag=f"l3f_m{call}_{oc}_{t}")
            env.sB[t % 2].wrote(mm)
            env.sXLN.read(mm)
            sWB.read(mm)
            bki = env.bank(6)
            mbi = P.op("tensor", lambda e, t=t, bki=bki: e.matmul(
                bki, env.spf("ones_row"), IROW[0:1, 512 * t : 512 * (t + 1)],
                start=True, stop=True), deps=[wsf] + env.sB[6].wdeps(),
                tag=f"l3f_bi{call}_{oc}_{t}")
            env.sB[6].wrote(mbi)
            env.sMUR.read(mbi)
            TMP = env.SC1[:, 0:512] if False else env.SC2[:, 0:512]
            cpm = P.op("scalar", lambda e, bk=bk, TMP=TMP: e.activation(
                TMP, bk, AF.Identity),
                deps=[mm] + env.sSC2.wdeps(), tag=f"l3f_ev{call}_{oc}_{t}")
            env.sB[t % 2].read(cpm)
            env.sSC2.wrote(cpm)
            v1 = P.op("vector", lambda e, bki=bki, TMP=TMP:
                      e.tensor_tensor(TMP, TMP, bki, ALU.mult),
                      deps=[cpm, mbi], tag=f"l3f_v{call}_{oc}_{t}")
            env.sB[6].read(v1)
            env.sSC2.wrote(v1)
            sv = P.op("scalar", lambda e, oc=oc, TMP=TMP: e.activation(
                TMP, TMP, AF.Identity, bias=env.spf("l3_bias", w=1, coff=oc)),
                deps=[v1], tag=f"l3f_b{call}_{oc}_{t}")
            env.sSC2.wrote(sv)
            xsl = slice(oc * S + 512 * t, oc * S + 512 * t + 512)
            vx = P.op("vector", lambda e, xsl=xsl, TMP=TMP: e.scalar_tensor_tensor(
                env.X[:, xsl], TMP, 0.5, env.X[:, xsl], ALU.mult, ALU.add),
                deps=[sv] + env.sX.wdeps() + env.sX.readers,
                tag=f"l3f_x{call}_{oc}_{t}")
            env.sSC2.read(vx)
            env.sX.wrote(vx)


def emit_l3_attn(env):
    P = env.P
    nc = env.nc
    emit_ln(env)

    QF3 = env.C[:, 0:4096]
    KF3 = env.C[:, 4096:8192]
    VT3 = env.Dg[:, 0:2048]
    EB3 = env.Dg[:, 2048:3072]
    sQ3, sK3, sV3, sE3 = env.sC[0], env.sC[1], env.sD[0], env.sD[1]
    WB0 = env.E[:, 0:512]
    WB1 = env.E[:, 1024:1536]
    WBV = env.E[:, 2048:2560]
    sWB = [env.sE[0], env.sE[1]]

    for h in range(16):
        ldws = []
        for bi, blk in enumerate((h, 16 + h, 32 + h)):
            WB = (WB0, WB1, WBV)[bi]
            sW = (env.sE[0], env.sE[1], env.sE[2])[bi]
            o = P.dma("gpsimd", lambda e, blk=blk, WB=WB: e.dma_start(
                out=WB, in_=env.l3w_h[blk]), deps=sW.wdeps(),
                tag=f"l3a_w{h}_{blk}")
            sW.wrote(o)
            ldws.append(o)
        pops = {"q": [], "k": []}
        for wi, which in ((0, "q"), (1, "k")):
            WB = (WB0, WB1)[wi]
            OUT = (QF3, KF3)[wi]
            sOUT = (sQ3, sK3)[wi]
            og = sOUT.wdeps()
            for t in range(4):
                bk = env.bank(t % 2)

                def pgrp(e, WB=WB, t=t, bk=bk):
                    r = None
                    for c in range(NCHUNK):
                        r = e.matmul(bk[0:64, :], WB[:, 64 * c : 64 * (c + 1)],
                                     env.XLN[:, c * S + 512 * t : c * S + 512 * t + 512],
                                     start=(c == 0), stop=(c == NCHUNK - 1))
                    return r
                mm = P.op("tensor", pgrp, deps=[ldws[wi]] + env.sXLN.rdeps() +
                          env.sB[t % 2].wdeps(), tag=f"l3a_{which}m{h}_{t}")
                env.sB[t % 2].wrote(mm)
                env.sXLN.read(mm)
                (env.sE[0], env.sE[1])[wi].read(mm)
                sv = P.op("scalar", lambda e, OUT=OUT, t=t, bk=bk: e.copy(
                    OUT[0:64, 512 * t : 512 * (t + 1)], bk[0:64, :]),
                    deps=[mm] + (og if t == 0 else []), tag=f"l3a_{which}e{h}_{t}")
                env.sB[t % 2].read(sv)
                sOUT.read(sv)
                pops[which].append(sv)
            sOUT.wrote_many(pops[which])
        vops = []
        vg = sV3.wdeps()
        for tt in range(16):
            bk = env.bank(4)

            def vgrp(e, tt=tt, bk=bk):
                r = None
                for c in range(NCHUNK):
                    r = e.matmul(bk[0:128, 0:64],
                                 env.XLN[:, c * S + 128 * tt : c * S + 128 * tt + 128],
                                 env.E[:, 2048 + 64 * c : 2048 + 64 * c + 64],
                                 start=(c == 0), stop=(c == NCHUNK - 1))
                return r
            mm = P.op("tensor", vgrp, deps=[ldws[2]] + env.sXLN.rdeps() +
                      env.sB[4].wdeps(), tag=f"l3a_vm{h}_{tt}")
            env.sB[4].wrote(mm)
            env.sXLN.read(mm)
            env.sE[2].read(mm)
            cv = P.op("vector", lambda e, tt=tt, bk=bk: e.tensor_copy(
                VT3[:, 64 * tt : 64 * (tt + 1)], bk[0:128, 0:64]),
                deps=[mm] + (vg if tt == 0 else []), tag=f"l3a_ve{h}_{tt}")
            env.sB[4].read(cv)
            sV3.read(cv)
            vops.append(cv)
        sV3.wrote_many(vops)
        QN = env.SC1[0:1, 0:2048]
        KN = env.SC1[0:1, 2048:4096] if False else env.MUR[0:1, 0:2048]
        nb_ops = []
        for wi, (SRC, ROW, sROW) in enumerate(((QF3, QN, env.sSC1),
                                               (KF3, KN, env.sMUR))):
            for t in range(4):
                sq = P.op("scalar", lambda e, SRC=SRC, t=t: e.activation(
                    env.SC2[0:64, 0:512], SRC[0:64, 512 * t : 512 * (t + 1)],
                    AF.Square), deps=(sQ3 if wi == 0 else sK3).rdeps() +
                    env.sSC2.wdeps(), tag=f"l3a_n2{h}_{wi}_{t}")
                env.sSC2.wrote(sq)
                (sQ3 if wi == 0 else sK3).read(sq)
                bk5 = env.bank(5)
                mn = P.op("tensor", lambda e, bk5=bk5: e.matmul(
                    bk5[0:1, :], env.spf("ones_col", p=64),
                    env.SC2[0:64, 0:512], start=True, stop=True),
                    deps=[sq] + env.sB[5].wdeps(), tag=f"l3a_nm{h}_{wi}_{t}")
                env.sB[5].wrote(mn)
                env.sSC2.read(mn)
                cn = P.op("vector", lambda e, ROW=ROW, t=t, bk5=bk5:
                          e.tensor_copy(ROW[0:1, 512 * t : 512 * (t + 1)],
                                        bk5[0:1, :]),
                          deps=[mn] + (sROW.wdeps() if t == 0 else []),
                          tag=f"l3a_nc{h}_{wi}_{t}")
                env.sB[5].read(cn)
                sROW.read(cn)
                nb_ops.append(cn)
        env.sSC1.wrote_many(nb_ops)
        env.sMUR.wrote_many(nb_ops)
        BB = env.SC2[0:1, 0:4]
        def bfn(e):
            e.reduce_max(BB[0:1, 0:1], QN, axis=AX.X)
            e.drain()
            e.reduce_max(BB[0:1, 1:2], KN, axis=AX.X)
            e.drain()
            return None
        vb = P.op("vector", bfn, deps=nb_ops + env.sSC2.wdeps(), tag=f"l3a_b{h}")
        env.sSC1.read(vb)
        env.sMUR.read(vb)
        env.sSC2.wrote(vb)
        sb1 = P.op("scalar", lambda e: e.activation(BB[0:1, 0:2], BB[0:1, 0:2],
                   AF.Sqrt), deps=[vb], tag=f"l3a_bs{h}")
        env.sSC2.wrote(sb1)
        def bfn2(e):
            e.tensor_tensor(BB[0:1, 2:3], BB[0:1, 0:1], BB[0:1, 1:2], ALU.mult)
            e.drain()
            return e.tensor_scalar(BB[0:1, 2:3], BB[0:1, 2:3], -0.125, None,
                                   ALU.mult)
        vb2 = P.op("vector", bfn2, deps=[sb1], tag=f"l3a_b2{h}")
        env.sSC2.wrote(vb2)
        bk5 = env.bank(5)
        mbb = P.op("tensor", lambda e, bk5=bk5: e.matmul(
            bk5[0:128, 0:1], env.spf("ones_row"), BB[0:1, 2:3],
            start=True, stop=True), deps=[vb2] + env.sB[5].wdeps(),
            tag=f"l3a_bb{h}")
        env.sB[5].wrote(mbb)
        env.sSC2.read(mbb)
        NB = env.SC2[:, 512:513]
        cbb = P.op("vector", lambda e, NB=NB, bk5=bk5: e.tensor_copy(
            NB, bk5[0:128, 0:1]), deps=[mbb] + env.sSC2.wdeps(), tag=f"l3a_be{h}")
        env.sB[5].read(cbb)
        env.sSC2.wrote(cbb)
        for qt in range(4):
            den_mm = None
            hmms = []
            for kc in range(16):
                bkq = env.bank(kc % 2)
                mq = P.op("tensor", lambda e, kc=kc, qt=qt, bkq=bkq: e.matmul(
                    bkq, KF3[0:64, 128 * kc : 128 * (kc + 1)],
                    QF3[0:64, 512 * qt : 512 * (qt + 1)], start=True, stop=True),
                    deps=sK3.rdeps() + sQ3.rdeps() + env.sB[kc % 2].wdeps(),
                    tag=f"l3a_s{h}_{qt}_{kc}")
                env.sB[kc % 2].wrote(mq)
                sK3.read(mq)
                sQ3.read(mq)
                EBt = env.Dg.ap().bitcast(BF16)[:, 4096 + 1024 * (kc % 2) :
                                                4096 + 1024 * (kc % 2) + 512]
                se = P.op("scalar", lambda e, bkq=bkq, EBt=EBt, NB=NB:
                          e.activation(EBt, bkq, AF.Exp, bias=NB,
                                       scale=env.spf("c_inv8")),
                          deps=[mq, cbb] + sE3.wdeps(), tag=f"l3a_E{h}_{qt}_{kc}")
                env.sB[kc % 2].read(se)
                env.sSC2.read(se)
                sE3.read(se)
                den_mm = P.op("tensor", lambda e, EBt=EBt, st_=(kc == 0),
                              sp_=(kc == 15): e.matmul(
                    env.bank(6)[0:1, :], env.spb("ones_colb"), EBt,
                    start=st_, stop=sp_),
                    deps=[se] + (env.sB[6].wdeps() if kc == 0 else []),
                    tag=f"l3a_d{h}_{qt}_{kc}")
                if kc == 0:
                    env.sB[6].wrote(den_mm)
                sE3.read(den_mm)
                hm = P.op("tensor", lambda e, kc=kc, EBt=EBt, st_=(kc == 0),
                          sp_=(kc == 15): e.matmul(
                    env.bank(7)[0:64, :], VT3[:, 64 * kc : 64 * (kc + 1)], EBt,
                    start=st_, stop=sp_),
                    deps=[se] + sV3.rdeps() +
                    (env.sB[7].wdeps() if kc == 0 else []),
                    tag=f"l3a_h{h}_{qt}_{kc}")
                if kc == 0:
                    env.sB[7].wrote(hm)
                sV3.read(hm)
                sE3.read(hm)
                hmms.append(hm)
            RR = env.SC1[0:1, 0:512]
            def rfn(e, RR=RR):
                e.tensor_copy(RR, env.bank(6)[0:1, :])
                e.drain()
                return e.reciprocal(RR, RR)
            vr = P.op("vector", rfn, deps=[den_mm] + env.sSC1.wdeps(),
                      tag=f"l3a_r{h}_{qt}")
            env.sB[6].read(vr)
            env.sSC1.wrote(vr)
            bk6 = env.bank(6)
            mrb = P.op("tensor", lambda e, bk6=bk6, RR=RR: e.matmul(
                bk6[0:64, :], env.spf("ones_row", w=64), RR, start=True,
                stop=True), deps=[vr] + env.sB[6].wdeps(), tag=f"l3a_rb{h}_{qt}")
            env.sB[6].wrote(mrb)
            env.sSC1.read(mrb)
            TMP = env.SC2[0:64, 0:512]
            cph = P.op("scalar", lambda e, TMP=TMP: e.activation(
                TMP, env.bank(7)[0:64, :], AF.Identity),
                deps=hmms + env.sSC2.wdeps(), tag=f"l3a_hev{h}_{qt}")
            env.sB[7].read(cph)
            env.sSC2.wrote(cph)
            vh = P.op("vector", lambda e, TMP=TMP, bk6=bk6: e.tensor_tensor(
                TMP, TMP, bk6[0:64, :], ALU.mult),
                deps=[cph, mrb], tag=f"l3a_o{h}_{qt}")
            env.sB[6].read(vh)
            env.sSC2.wrote(vh)
            bp = 64 * (h % 2)
            xsl = slice((h // 2) * S + 512 * qt, (h // 2) * S + 512 * qt + 512)
            if bp:
                TMP2 = env.SC2[64:128, 512:1024]
                c64 = P.op("vector", lambda e, TMP=TMP, TMP2=TMP2:
                           e.tensor_copy(TMP2, TMP),
                           deps=[vh], tag=f"l3a_c64{h}_{qt}")
                env.sSC2.wrote(c64)
                TMP, vh = TMP2, c64
            vx = P.op("vector", lambda e, bp=bp, xsl=xsl, TMP=TMP:
                      e.tensor_tensor(env.X[bp : bp + 64, xsl],
                                      env.X[bp : bp + 64, xsl], TMP, ALU.add),
                      deps=[vh] + env.sX.wdeps() + env.sX.readers,
                      tag=f"l3a_x{h}_{qt}")
            env.sSC2.read(vx)
            env.sX.wrote(vx)


_SPF = None
_SPB = None
_SPF_SHAPE = None
_SPB_SHAPE = None
_NC_CACHE = {}
LAST_RESULTS = None
DEBUG_TAPS = False


def kernel(**inputs):
    global _SPF, _SPB, _SPF_SHAPE, _SPB_SHAPE
    _SPF, _SPB = _build_packs(inputs)
    _SPF_SHAPE = (128, _SPF.cols)
    _SPB_SHAPE = (128, _SPB.cols)

    if N_STAGES not in _NC_CACHE:
        _NC_CACHE[N_STAGES] = _build(N_STAGES)
    nc = _NC_CACHE[N_STAGES]

    x = np.asarray(inputs["L0_attn_x"], dtype=np.float32)
    spf_host = _SPF.host()
    spb_host = _SPB.host()
    w1t = np.asarray(inputs["L0_ffn_W1"]).transpose(0, 2, 1)
    l0w1t = _f32(w1t.reshape(8, 8, 128, 32, 128).transpose(0, 3, 2, 1, 4)
                 .reshape(8, 32, 128, 1024))
    w2t = np.asarray(inputs["L0_ffn_W2"]).transpose(0, 2, 1)
    l0w2t = _f32(w2t.reshape(8, 32, 128, 8, 128).transpose(0, 3, 2, 1, 4)
                 .reshape(8, 8, 128, 4096))

    def tile_sq(w):
        return _f32(np.asarray(w).reshape(8, 128, 8, 128).transpose(2, 1, 0, 3)
                    .reshape(8, 128, 1024))
    l1wq = tile_sq(inputs["L1_attn_Wq"])
    l1wk = tile_sq(inputs["L1_attn_Wk"])
    l1wv = tile_sq(inputs["L1_attn_Wv"])
    l1wo = tile_sq(inputs["L1_attn_Wo"])
    w1t = np.asarray(inputs["L1_ffn_W1"]).T
    l1w1t = _f32(w1t.reshape(8, 128, 32, 128).transpose(2, 1, 0, 3)
                 .reshape(32, 128, 1024))
    w2t = np.asarray(inputs["L1_ffn_W2"]).T
    l1w2t = _f32(w2t.reshape(32, 128, 8, 128).transpose(2, 1, 0, 3)
                 .reshape(8, 128, 4096))
    upw = np.asarray(inputs["L2_attn_up_w"]).transpose(0, 2, 1)
    l2up = _f32(upw.reshape(4, 8, 128, 32, 128).transpose(0, 3, 2, 1, 4)
                .reshape(4, 32, 128, 1024))
    dnw = np.asarray(inputs["L2_attn_down_w"]).transpose(0, 2, 1)
    l2dn = _f32(dnw.reshape(4, 16, 128, 8, 128).transpose(0, 3, 2, 1, 4)
                .reshape(4, 8, 128, 2048))
    latw = np.asarray(inputs["L2_attn_lat_w"]).T
    l2lat = _f32(latw.reshape(8, 128, 8, 128).transpose(2, 1, 0, 3)
                 .reshape(8, 128, 1024))

    def blockdiag(w):
        w = np.asarray(w, dtype=np.float32)
        B = np.zeros((4, 16, 128, 128), np.float32)
        for b in range(32):
            B[:, :, 4 * b : 4 * b + 4, 4 * b : 4 * b + 4] = \
                w.reshape(4, 16, 32, 4, 4)[:, :, b].transpose(0, 1, 3, 2)
        return np.ascontiguousarray(B)
    l2bq = blockdiag(inputs["L2_attn_q_w"])
    l2bk = blockdiag(inputs["L2_attn_k_w"])
    l2bv = blockdiag(inputs["L2_attn_v_w"])
    w1 = np.asarray(inputs["L2_ffn_w1"]).transpose(0, 2, 1)
    l2fw1 = _f32(w1.reshape(16, 8, 128, 4, 128).transpose(0, 3, 2, 1, 4)
                 .reshape(16, 4, 128, 1024))
    w2 = np.asarray(inputs["L2_ffn_w2"]).transpose(0, 2, 1)
    l2fw2 = _f32(w2.reshape(16, 4, 128, 8, 128).transpose(0, 3, 2, 1, 4)
                 .reshape(16, 8, 128, 512))
    sf1 = np.asarray(inputs["L2_ffn_sfc1"]).T
    l2sf1 = _f32(sf1.reshape(8, 128, 16, 128).transpose(2, 1, 0, 3)
                 .reshape(16, 128, 1024))
    sf2 = np.asarray(inputs["L2_ffn_sfc2"]).T
    l2sf2 = _f32(sf2.reshape(8, 128, 16, 128).transpose(2, 1, 0, 3)
                 .reshape(16, 128, 1024))
    sf3 = np.asarray(inputs["L2_ffn_sfc3"]).T
    l2sf3 = _f32(sf3.reshape(16, 128, 8, 128).transpose(2, 1, 0, 3)
                 .reshape(8, 128, 2048))
    l3pk = np.ascontiguousarray(
        np.asarray(inputs["L3_ffn_packed_weight"]).reshape(1024, 256).T)
    W3 = np.asarray(inputs["L3_attn_W"])
    perm = []
    for blk in range(3):
        for h in range(16):
            perm.extend(range(192 * h + 64 * blk, 192 * h + 64 * blk + 64))
    W3p = W3[:, perm]
    l3w = _f32(W3p.reshape(8, 128, 48, 64).transpose(2, 1, 0, 3)
               .reshape(48, 128, 512))
    in_maps = []
    for b in range(NCORE):
        in_maps.append({
            "x": np.ascontiguousarray(x[b].T),
            "spf": spf_host,
            "spb": spb_host,
            "l0w1t": l0w1t,
            "l0w2t": l0w2t,
            "l1w1t": l1w1t,
            "l1w2t": l1w2t,
            "l3pk": l3pk,
            "l3w": l3w,
            "l2up": l2up,
            "l2dn": l2dn,
            "l2lat": l2lat,
            "l2bq": l2bq,
            "l2bk": l2bk,
            "l2bv": l2bv,
            "l2fw1": l2fw1,
            "l2fw2": l2fw2,
            "l2sf1": l2sf1,
            "l2sf2": l2sf2,
            "l2sf3": l2sf3,
            "l1wq": l1wq,
            "l1wk": l1wk,
            "l1wv": l1wv,
            "l1wo": l1wo,
        })
    res = run_bass_kernel_spmd(nc, in_maps, core_ids=list(range(NCORE)))
    global LAST_RESULTS
    LAST_RESULTS = res.results
    out = np.stack([np.asarray(r["out"]).T for r in res.results])
    return np.ascontiguousarray(out, dtype=np.float32)
